# version 04
# baseline (speedup 1.0000x reference)
import numpy as np
import contextlib
import concourse.bass as bass
import concourse.mybir as mybir
from concourse.bass_utils import run_bass_kernel_spmd
from concourse.bass_core_rust import IndirectOffsetOnAxis

F32 = mybir.dt.float32
BF16 = mybir.dt.bfloat16
I32 = mybir.dt.int32
AF = mybir.ActivationFunctionType
ALU = mybir.AluOpType
AX = mybir.AxisListType

T = 2048
D = 1024
NT = 16
NH = 8
HDQ, HDV = 128, 256
NIC = 8
EPS = 1e-5
LNEG = -30000.0
NCORES = 8
FF = 4096
NE = 8
NFC = 32

ENGS = ("sy", "sc", "ve", "gp", "te")


class Prog:
    def __init__(self, nc, ctx):
        self.nc = nc
        self.ctx = ctx
        self.q = {e: [] for e in ENGS}
        self.sems = {}
        self.cnt = {}

    def sem(self, *names):
        for n in names:
            self.sems[n] = self.ctx.enter_context(self.nc.semaphore(n))
            self.cnt[n] = 0

    def w(self, eng, sem, val=None):
        v = self.cnt[sem] if val is None else val
        if v <= 0:
            return
        h = self.sems[sem]
        self.q[eng].append(lambda E, h=h, v=v: E.wait_ge(h, v))

    def o(self, eng, fn, inc=None, n=1):
        if inc is not None:
            self.cnt[inc] += n
            h = self.sems[inc]
            self.q[eng].append(lambda E, fn=fn, h=h, n=n: fn(E).then_inc(h, n))
            return self.cnt[inc]
        self.q[eng].append(fn)
        return None

    def dma(self, eng, sem, out, in_, **kw):
        return self.o(eng, lambda E, out=out, in_=in_, kw=kw: E.dma_start(out=out, in_=in_, **kw),
                      inc=sem, n=16)

    def dmat(self, eng, sem, out, in_):
        return self.o(eng, lambda E, out=out, in_=in_: E.dma_start(out=out, in_=in_, transpose=True),
                      inc=sem, n=16)

    def c(self, sem):
        return self.cnt[sem]


def build_nc(stage=99, taps=(), n_heads=NH, stop_qkv=False, no_moe_cast=False):
    nc = bass.Bass("TRN2", target_bir_lowering=False, debug=False,
                   detect_race_conditions=False)

    dram = lambda n, sh, dt, kind: nc.dram_tensor(n, sh, dt, kind=kind).ap()
    x_in = dram("x_in", [T, D], F32, "ExternalInput")
    wqkv_in = dram("wqkv_in", [D, 4096], F32, "ExternalInput")
    wgat_in = dram("wgat_in", [D, 2048], F32, "ExternalInput")
    wprj_in = dram("wprj_in", [2048, D], F32, "ExternalInput")
    decay_r = dram("decay_r", [1, NH], F32, "ExternalInput")
    tab_in = dram("tab_in", [NH, NT, 128, 256], F32, "ExternalInput")
    umask_in = dram("umask_in", [128, 128], F32, "ExternalInput")
    gidx_in = dram("gidx_in", [128, NT], I32, "ExternalInput")
    ipw_in = dram("ipw_in", [D, 3072], F32, "ExternalInput")
    outw_in = dram("outw_in", [D, D], F32, "ExternalInput")
    gatew_in = dram("gatew_in", [D, NE], F32, "ExternalInput")
    gateb_in = dram("gateb_in", [1, NE], F32, "ExternalInput")
    bias_in = dram("bias_in", [1, 3072 + D + NE * D], F32, "ExternalInput")
    w1pre_in = dram("w1pre_in", [NE, NFC * 128, NIC * 128], F32, "ExternalInput")
    b1c_in = dram("b1c_in", [128, NE * NFC], F32, "ExternalInput")
    w2pre_in = dram("w2pre_in", [NE, FF, D], F32, "ExternalInput")
    lng_in = dram("lng_in", [128, 4 * D], F32, "ExternalInput")

    y_out = dram("y_out", [T, D], F32, "ExternalOutput")
    tap_ap = {}
    for tn in taps:
        shapes = {"hfm": [D, T], "x1": [T, D], "xmid": [T, D], "qk": [T, 2048],
                  "ctx": [T, D], "comb": [T, NE], "x1b": [T, D], "gnfm": [2048, T], "accf": [T, D], "h1f": [FF, 512],
                  "qkfm": [2 * HDQ + HDV, T], "xlnfm": [D, T]}
        tap_ap[tn] = dram("tap_" + tn, shapes[tn], F32, "ExternalOutput")

    ag_in = nc.dram_tensor("ag_in", [T, D], BF16)
    ag_out = nc.dram_tensor("ag_out", [NCORES * T, D], BF16)
    bias_c = nc.dram_tensor("bias_c", [1, 3072 + D + NE * D], BF16)
    wqkv_c = nc.dram_tensor("wqkv_c", [D, 4096], BF16)
    wgat_c = nc.dram_tensor("wgat_c", [D, 2048], BF16)
    wprj_c = nc.dram_tensor("wprj_c", [2048, D], BF16)
    ipw_c = nc.dram_tensor("ipw_c", [D, 3072], BF16)
    outw_c = nc.dram_tensor("outw_c", [D, D], BF16)
    w1_c = nc.dram_tensor("w1_c", [NE, NFC * 128, NIC * 128], BF16)
    w2_c = nc.dram_tensor("w2_c", [NE, FF, D], BF16)

    ctx = contextlib.ExitStack()
    SB = lambda n, cols, dt: ctx.enter_context(nc.sbuf_tensor(n, [128, cols], dt))
    AR1 = SB("AR1", 16384, BF16)
    AR2 = SB("AR2", 16384, BF16)
    AR3 = SB("AR3", 32768, BF16)
    AR4 = SB("AR4", 16384, BF16)
    AR6 = SB("AR6", 3072, F32)
    AR7 = SB("AR7", 12288, BF16)
    umask = SB("umask", 128, F32)
    tcol = SB("tcol", NT, F32)
    qdec = SB("qdec", NT, F32)
    kdec = SB("kdec", NT, F32)
    ldb = SB("ldb", NH, F32)
    nldb = SB("nldb", NH, F32)
    gidx = SB("gidx", NT, I32)
    xstat = SB("xstat", 2 * NT, F32)
    st6 = SB("st6", 12, F32)
    mv2 = SB("mv2", 8, F32)
    b1c = SB("b1c", NE * NFC, F32)
    comb = SB("comb", NT * NE, F32)
    gwsb = SB("gwsb", NIC * NE, BF16)
    attw = SB("attw", NH * 256, BF16)
    mqr = SB("mqr", 8, F32)
    ccol = SB("ccol", 4, F32)
    decs = ctx.enter_context(nc.sbuf_tensor("decs", [1, NH], F32))
    ones_f = ctx.enter_context(nc.sbuf_tensor("ones_f", [1, 128], F32))
    ones_b = ctx.enter_context(nc.sbuf_tensor("ones_b", [1, 128], BF16))
    gateb_r = ctx.enter_context(nc.sbuf_tensor("gateb_r", [1, NE], BF16))
    brow = ctx.enter_context(nc.sbuf_tensor("brow", [1, 2048], BF16))

    PS = [ctx.enter_context(nc.psum_tensor("ps%d" % i, [128, 512], F32)) for i in range(8)]

    p = Prog(nc, ctx)
    p.sem("SY", "SC", "GP", "PE", "VE", "AC", "PO", "CC", "ST",
          "TB0", "TB1", "WQ", "IP0", "IP1", "W10", "W11", "W12", "W13", "W20", "W21", "W22", "W23", "MC", "GPI")

    A4_V, A4_QFM, A4_KFM, A4_QKVW, A4_GATEW, A4_STB, A4_QKS = \
        0, 4096, 6144, 8192, 12288, 14336, 15360
    A6_TAB, A6_SCR, A6_OFM = 0, 512, 2560
    A7_OTM, A7_GT, A7_GN, A7_HST = 0, 1024, 2048, 4096
    B7_IPW, B7_LN = 0, 8192
    M7_W1, M7_W2 = 0, 4096

    scr = lambda i: AR6[:, A6_SCR + i * 1024: A6_SCR + (i + 1) * 1024]
    bf6 = AR6[:, :].bitcast(BF16)
    f32_AR3 = AR3[:, :].bitcast(F32)
    tabs = lambda par: AR6[:, A6_TAB + par * 256: A6_TAB + (par + 1) * 256]
    hfm = lambda ic: AR2[:, ic * T:(ic + 1) * T]
    g1t = AR7[:, B7_LN + 0 * D: B7_LN + 1 * D]
    b1t = AR7[:, B7_LN + 1 * D: B7_LN + 2 * D]
    g2t = AR7[:, B7_LN + 2 * D: B7_LN + 3 * D]
    b2t = AR7[:, B7_LN + 3 * D: B7_LN + 4 * D]

    p.dma("sy", "SY", umask[:, :], umask_in)
    p.dma("sy", "SY", gidx[:, :], gidx_in)
    p.dma("sy", "SY", decs[:, :], decay_r)
    p.dma("sy", "SY", b1c[:, :], b1c_in)
    misc_loaded = p.c("SY")

    for tt in range(NT):
        p.dma("gp", "GP", AR1[:, tt * D:(tt + 1) * D], x_in[tt * 128:(tt + 1) * 128, :])
    x_loaded = p.c("GP")
    p.dma("gp", "GP", gateb_r[:, :], gateb_in)
    p.dma("gp", "GP", bias_c.ap(), bias_in)
    p.dma("gp", "GP", gwsb[:, :].rearrange("p (ic e) -> p ic e", e=NE), gatew_in.rearrange("(ic p) e -> p ic e", p=128))
    p.dma("gp", "GP", AR7[:, B7_LN:B7_LN + 4 * D], lng_in)
    small_loaded = p.c("GP")
    p.dma("gp", "MC", wqkv_c.ap(), wqkv_in)
    p.dma("gp", "MC", wgat_c.ap(), wgat_in)
    qkv_cast = p.c("MC")
    p.dma("gp", "MC", ipw_c.ap(), ipw_in)
    p.dma("gp", "MC", outw_c.ap(), outw_in)
    p.dma("gp", "MC", wprj_c.ap(), wprj_in)
    misc_cast = p.c("MC")
    moe_cast = []
    for e in range(NE):
        if not no_moe_cast:
            p.dma("gp", "MC", w1_c.ap()[e, :, :], w1pre_in[e, :, :])
            p.dma("gp", "MC", w2_c.ap()[e, :, :], w2pre_in[e, :, :])
        moe_cast.append(p.c("MC"))

    p.o("ve", lambda E: E.memset(ones_f[:, :], 1.0))
    p.o("ve", lambda E: E.memset(ccol[:, 0:1], EPS))
    p.o("ve", lambda E: E.memset(ccol[:, 1:2], -0.5))
    p.o("ve", lambda E: E.memset(ccol[:, 2:3], float(np.log(128.0 ** -0.5))))
    p.o("ve", lambda E: E.memset(ones_b[:, :], 1.0), inc="VE")
    ones_ready = p.c("VE")
    p.o("gp", lambda E: E.iota(tcol[:, :], pattern=[[128, NT]], base=0, channel_multiplier=1,
                               allow_small_or_imprecise_dtypes=True), inc="PO")
    iota_done = p.c("PO")
    p.w("sc", "SY", misc_loaded)
    p.o("sc", lambda E: E.activation(decs[:, :], decs[:, :], AF.Ln), inc="AC")
    ln_dec = p.c("AC")
    p.w("te", "AC", ln_dec)
    p.w("te", "VE", ones_ready)
    p.o("te", lambda E: E.matmul(PS[6][:, 0:NH], ones_f[:, :], decs[:, :],
                                 start=True, stop=True), inc="PE")
    p.w("ve", "PE")
    p.o("ve", lambda E: E.tensor_copy(ldb[:, :], PS[6][:, 0:NH]), inc="VE")
    p.w("ve", "VE")
    p.o("ve", lambda E: E.tensor_scalar_mul(nldb[:, :], ldb[:, :], -1.0), inc="VE")
    ldb_ready = p.c("VE")
    p.w("sc", "VE", ldb_ready)
    p.w("sc", "PO", iota_done)
    dec_ready = p.c("AC")

    qkvw_early = AR4[:, 8192:8192 + 4096]
    gatew_early = AR4[:, 12288:12288 + 2048]
    p.w("sy", "MC", qkv_cast)
    p.dma("sy", "WQ", qkvw_early[:, :].rearrange("p (ic c) -> p ic c", c=512),
          wqkv_c.ap().rearrange("(ic p) o -> p ic o", p=128)[:, :, 0:512])
    p.dma("sy", "WQ", gatew_early[:, :].rearrange("p (ic c) -> p ic c", c=256),
          wgat_c.ap().rearrange("(ic p) o -> p ic o", p=128)[:, :, 0:256])
    early_wq = p.c("WQ")
    p.dma("sy", "TB0", AR6[:, 0:256], tab_in[0, 0, :, :])
    p.dma("sy", "TB1", AR6[:, 256:512], tab_in[0, 1, :, :])
    early_tb0, early_tb1 = p.c("TB0"), p.c("TB1")
    hst = lambda par: AR7[:, A7_HST + par * D: A7_HST + (par + 1) * D]
    hst_tp = {}
    hfm_tile_tp = {}
    p.w("ve", "GP", x_loaded)
    for tt in range(NT):
        xt = AR1[:, tt * D:(tt + 1) * D]
        p.o("ve", lambda E, xt=xt: E.bn_stats(st6[:, 0:6], xt[:, 0:512]))
        p.o("ve", lambda E, xt=xt: E.bn_stats(st6[:, 6:12], xt[:, 512:1024]), inc="VE")
        p.w("ve", "VE")
        p.o("ve", lambda E, tt=tt: E.bn_aggr(xstat[:, tt * 2: tt * 2 + 2], st6[:, 0:12]),
            inc="VE" if tt == NT - 1 else None)
    xstats_done = p.c("VE")
    p.w("sc", "VE", xstats_done)
    for tt in range(NT):
        p.o("sc", lambda E, tt=tt: E.activation(xstat[:, tt * 2 + 1: tt * 2 + 2],
                                                xstat[:, tt * 2 + 1: tt * 2 + 2], AF.Ln,
                                                bias=ccol[:, 0:1]), inc="AC")
    p.w("sc", "AC")
    for tt in range(NT):
        p.o("sc", lambda E, tt=tt: E.activation(xstat[:, tt * 2 + 1: tt * 2 + 2],
                                                xstat[:, tt * 2 + 1: tt * 2 + 2], AF.Exp,
                                                scale=ccol[:, 1:2]),
            inc="AC" if tt == NT - 1 else None)
    xrstd_done = p.c("AC")
    p.w("ve", "AC", xrstd_done)
    for tt in range(NT):
        xt = AR1[:, tt * D:(tt + 1) * D]
        if tt >= 2:
            s_, v_ = hst_tp[tt % 2]
            p.w("ve", s_, v_)
        p.o("ve", lambda E, xt=xt, tt=tt: E.tensor_scalar(
            hst(tt % 2)[:, :], xt, xstat[:, tt * 2: tt * 2 + 1],
            xstat[:, tt * 2 + 1: tt * 2 + 2], op0=ALU.subtract, op1=ALU.mult), inc="VE")
        hdone = p.c("VE")
        qeng, qsem = "sy", "SY"
        p.w(qeng, "VE", hdone)
        for ic in range(NIC):
            p.dmat(qeng, qsem, AR2[:, ic * T + tt * 128: ic * T + (tt + 1) * 128],
                   hst(tt % 2)[:, ic * 128:(ic + 1) * 128])
        hst_tp[tt % 2] = (qsem, p.c(qsem))
        hfm_tile_tp[tt] = p.c("SY")
    hfm_sy, hfm_sc = p.c("SY"), p.c("SC")

    def tap_tm(name, src_fn, gate_sem, gate_val, width=D):
        if name not in tap_ap:
            return
        for tt in range(NT):
            p.w("ve", gate_sem, gate_val)
            p.w("ve", "ST", p.c("ST"))
            p.o("ve", lambda E, tt=tt: E.tensor_copy(scr(0)[:, 0:width], src_fn(tt)), inc="VE")
            p.w("sy", "VE")
            p.dma("sy", "ST", tap_ap[name][tt * 128:(tt + 1) * 128, 0:width], scr(0)[:, 0:width])

    if "hfm" in tap_ap:
        for ic in range(NIC):
            for half in range(2):
                p.w("ve", "SY", hfm_sy)
                p.w("ve", "SC", hfm_sc)
                p.w("ve", "ST", p.c("ST"))
                p.o("ve", lambda E, ic=ic, half=half: E.tensor_copy(
                    scr(0), hfm(ic)[:, half * 1024:(half + 1) * 1024]), inc="VE")
                p.w("sy", "VE")
                p.dma("sy", "ST", tap_ap["hfm"][ic * 128:(ic + 1) * 128,
                                                half * 1024:(half + 1) * 1024], scr(0))
    if stage <= 1:
        finish(p, nc, ctx)
        return nc

    vh = AR4[:, A4_V:A4_V + NT * HDV]
    qfm = AR4[:, A4_QFM:A4_QFM + T]
    kfm = AR4[:, A4_KFM:A4_KFM + T]
    qkvw = AR4[:, A4_QKVW:A4_QKVW + 4096]
    gatew = AR4[:, A4_GATEW:A4_GATEW + 2048]
    stb = lambda par: AR4[:, A4_STB + par * 512: A4_STB + (par + 1) * 512]
    qks = lambda par: AR4[:, A4_QKS + par * 512: A4_QKS + (par + 1) * 512]
    otm_st = AR7[:, A7_OTM:A7_OTM + 1024]
    gt_st = AR7[:, A7_GT:A7_GT + 1024]
    gn_st = lambda par: AR7[:, A7_GN + par * 1024: A7_GN + (par + 1) * 1024]

    tabv = {(0, 0): early_tb0, (0, 1): early_tb1}

    def tab_load(h, tt):
        sem = "TB0" if tt % 2 == 0 else "TB1"
        p.dma("sy", sem, tabs(tt % 2)[:, :], tab_in[h, tt, :, :])
        tabv[(h, tt)] = p.c(sem)

    head_done_pe = 0
    xpos_done_ve = 0
    xpos_done_po = 0
    out_evac = {}
    gate_ac = {}
    ofm_tp = {}
    gn_done = {}
    gn_tp = {}
    prev_head_out_evac = None
    prev_head_gn_done = None

    for h in range(n_heads):
        if h > 0:
            p.w("sy", "PE", head_done_pe)
            p.dma("sy", "WQ", qkvw[:, :].rearrange("p (ic c) -> p ic c", c=512),
                  wqkv_c.ap().rearrange("(ic p) o -> p ic o", p=128)[:, :, h * 512:(h + 1) * 512])
            p.dma("sy", "WQ", gatew[:, :].rearrange("p (ic c) -> p ic c", c=256),
                  wgat_c.ap().rearrange("(ic p) o -> p ic o", p=128)[:, :, h * 256:(h + 1) * 256])
        wq_ready = p.c("WQ")
        p.o("sc", lambda E, h=h: E.activation(qdec[:, :], tcol[:, :], AF.Exp,
                                              bias=ccol[:, 2:3],
                                              scale=ldb[:, h:h + 1]))
        p.o("sc", lambda E, h=h: E.activation(kdec[:, :], tcol[:, :], AF.Exp,
                                              scale=nldb[:, h:h + 1]), inc="AC")

        evac_ac = {}
        tpq = {}
        for tt in range(NT):
            bank = PS[tt % 2]
            p.w("te", "WQ", wq_ready)
            if h == 0:
                p.w("te", "SY", hfm_tile_tp[tt])
            if h == 0 and tt == 0:
                p.w("te", "AC", dec_ready)
            if tt >= 2:
                p.w("te", "AC", evac_ac[tt - 2])
            elif h > 0:
                p.w("te", "AC", prev_evac_last[tt])
            for ic in range(NIC):
                p.o("te", lambda E, b=bank, ic=ic, tt=tt: E.matmul(
                    b[:, :], hfm(ic)[:, tt * 128:(tt + 1) * 128],
                    qkvw[:, ic * 512:(ic + 1) * 512],
                    start=(ic == 0), stop=(ic == NIC - 1)),
                    inc="PE" if ic == NIC - 1 else None)
            mm_done = p.c("PE")
            p.w("sc", "PE", mm_done)
            if tt == 0:
                p.w("sc", "AC", p.c("AC"))
            if tt >= 2 or h > 0:
                p.w("sc", "VE", xpos_done_ve)
                p.w("sc", "PO", xpos_done_po)
            sp = scr(tt % 2)
            p.o("sc", lambda E, b=bank, sp=sp, h=h, tt=tt: E.mul(
                sp[:, 0:128], b[:, 0:128], qdec[:, tt:tt + 1]))
            p.o("sc", lambda E, b=bank, sp=sp, h=h, tt=tt: E.mul(
                sp[:, 128:256], b[:, 128:256], kdec[:, tt:tt + 1]))
            p.o("sc", lambda E, b=bank, tt=tt: E.copy(vh[:, tt * HDV:(tt + 1) * HDV],
                                                      b[:, 256:512]), inc="AC")
            evac_ac[tt] = p.c("AC")
            eng = "ve" if (tt % 3 != 2) else "gp"
            esem = "VE" if eng == "ve" else "PO"
            p.w(eng, "AC", evac_ac[tt])
            p.w(eng, "TB0" if tt % 2 == 0 else "TB1", tabv[(h, tt)])
            if tt >= 2:
                s_, v_ = tpq[tt % 2]
                p.w(eng, s_, v_)
            elif h > 0:
                p.w(eng, "SY", prev_tpq_sy)
                p.w(eng, "SC", prev_tpq_sc)
            tb = tabs(tt % 2)
            stg = qks(tt % 2)
            for o_q in (0, 128):
                base_t = 0 if o_q == 0 else 128
                se = sp[:, o_q:o_q + 128][:, 0::2]
                so = sp[:, o_q:o_q + 128][:, 1::2]
                Ct = tb[:, base_t: base_t + 64]
                St_ = tb[:, base_t + 64: base_t + 128]
                t0 = sp[:, 512:576]
                t1 = sp[:, 576:640]
                de = stg[:, o_q:o_q + 128][:, 0::2]
                do = stg[:, o_q:o_q + 128][:, 1::2]
                p.o(eng, lambda E, a=t0, b_=so, c=St_: E.tensor_mul(a, b_, c))
                p.o(eng, lambda E, a=t1, b_=se, c=Ct: E.tensor_mul(a, b_, c), inc=esem)
                p.w(eng, esem)
                p.o(eng, lambda E, a=de, b_=t1, c=t0: E.tensor_sub(a, b_, c))
                p.o(eng, lambda E, a=t0, b_=se, c=St_: E.tensor_mul(a, b_, c))
                p.o(eng, lambda E, a=t1, b_=so, c=Ct: E.tensor_mul(a, b_, c), inc=esem)
                p.w(eng, esem)
                p.o(eng, lambda E, a=do, b_=t1, c=t0: E.tensor_add(a, b_, c),
                    inc=esem)
            if eng == "ve":
                xpos_done_ve = p.c("VE")
            else:
                xpos_done_po = p.c("PO")
            p.w("sy", esem, p.c(esem))
            if tt + 2 < NT:
                tab_load(h, tt + 2)
            elif h + 1 < NH:
                tab_load(h + 1, tt + 2 - NT)
            qeng, qsem = "sy", "SY"
            p.w(qeng, esem, p.c(esem))
            p.dmat(qeng, qsem, qfm[:, tt * 128:(tt + 1) * 128], stg[:, 0:128])
            p.dmat(qeng, qsem, kfm[:, tt * 128:(tt + 1) * 128], stg[:, 128:256])
            tpq[tt % 2] = (qsem, p.c(qsem))
        prev_evac_last = {0: evac_ac[14], 1: evac_ac[15]}
        prev_tpq_sy, prev_tpq_sc = p.c("SY"), p.c("SC")
        qkv_tp_sy, qkv_tp_sc = p.c("SY"), p.c("SC")
        if "qkfm" in tap_ap and h == n_heads - 1:
            for row, srcv in ((0, qfm), (1, kfm)):
                for half in range(2):
                    p.w("ve", "SY", qkv_tp_sy)
                    p.w("ve", "SC", qkv_tp_sc)
                    p.w("ve", "ST", p.c("ST"))
                    p.o("ve", lambda E, s=srcv, half=half: E.tensor_copy(
                        scr(0), s[:, half * 1024:(half + 1) * 1024]), inc="VE")
                    p.w("sy", "VE")
                    p.dma("sy", "ST", tap_ap["qkfm"][row * 128:(row + 1) * 128,
                                                     half * 1024:(half + 1) * 1024], scr(0))
            for tt4 in range(4):
                p.w("ve", "AC", evac_ac[15])
                p.w("ve", "ST", p.c("ST"))
                p.o("ve", lambda E, tt4=tt4: E.tensor_copy(
                    scr(0), vh[:, tt4 * 1024:(tt4 + 1) * 1024]), inc="VE")
                p.w("sy", "VE")
                p.dma("sy", "ST", tap_ap["qkfm"][256:384, tt4 * 512:(tt4 + 1) * 512],
                      scr(0)[:, 0:512])
                p.w("ve", "ST", p.c("ST"))
                p.o("ve", lambda E: E.engine_nop(), inc="VE")
                p.w("sy", "VE")
                p.dma("sy", "ST", tap_ap["qkfm"][384:512, tt4 * 512:(tt4 + 1) * 512],
                      scr(0)[:, 512:1024])

        if stop_qkv:
            finish(p, nc, ctx)
            return nc
        p.w("te", "SY", qkv_tp_sy)
        p.w("te", "SC", qkv_tp_sc)
        p.w("te", "AC", evac_ac[15])
        st_evac = {}
        sv_pe = {}

        for qb in range(4):
            nkb = 4 * qb + 4

            def emit_st(kb, qb=qb, nkb=nkb):
                bank = PS[2 + (kb % 2)]
                if kb >= 2:
                    s_, v_ = st_evac[(qb, kb - 2)]
                    p.w("te", s_, v_)
                elif qb > 0:
                    pk = 4 * qb
                    s_, v_ = st_evac[(qb - 1, pk - 2 + kb)]
                    p.w("te", s_, v_)
                p.o("te", lambda E, b=bank, kb=kb, qb=qb: E.matmul(
                    b[:, :], kfm[:, kb * 128:(kb + 1) * 128], qfm[:, qb * 512:(qb + 1) * 512],
                    start=True, stop=True), inc="PE")
                st_pe = p.c("PE")
                boundary = kb >= 4 * qb
                eng2 = "ve" if (boundary or kb % 2 == 0) else "sc"
                sem2 = "VE" if eng2 == "ve" else "AC"
                p.w(eng2, "PE", st_pe)
                if (qb, kb - 2) in sv_pe:
                    p.w(eng2, "PE", sv_pe[(qb, kb - 2)])
                elif qb > 0 and kb < 2:
                    p.w(eng2, "PE", sv_pe[(qb - 1, 4 * qb - 2 + kb)])
                dstb = stb(kb % 2)
                if not boundary:
                    if eng2 == "ve":
                        p.o(eng2, lambda E, d=dstb, b=bank: E.tensor_copy(d[:, :], b[:, :]),
                            inc=sem2)
                    else:
                        p.o(eng2, lambda E, d=dstb, b=bank: E.copy(d[:, :], b[:, :]), inc=sem2)
                else:
                    cc_ = kb - 4 * qb
                    if cc_ > 0:
                        p.o(eng2, lambda E, d=dstb, cc_=cc_: E.memset(d[:, 0:cc_ * 128], 0.0))
                    p.o(eng2, lambda E, d=dstb, b=bank, cc_=cc_: E.tensor_mul(
                        d[:, cc_ * 128:(cc_ + 1) * 128], b[:, cc_ * 128:(cc_ + 1) * 128],
                        umask[:, :]), inc=sem2 if cc_ == 3 else None)
                    if cc_ < 3:
                        p.o(eng2, lambda E, d=dstb, b=bank, cc_=cc_: E.tensor_copy(
                            d[:, (cc_ + 1) * 128:512], b[:, (cc_ + 1) * 128:512]), inc=sem2)
                st_evac[(qb, kb)] = (sem2, p.c(sem2))

            def emit_sv(kb, qb=qb, nkb=nkb):
                s_, v_ = st_evac[(qb, kb)]
                p.w("te", s_, v_)
                if kb == 0 and qb > 0:
                    ev, ea = out_evac[qb - 1]
                    p.w("te", "VE", ev)
                    p.w("te", "AC", ea)
                elif kb == 0 and prev_head_out_evac is not None:
                    ev, ea = prev_head_out_evac
                    p.w("te", "VE", ev)
                    p.w("te", "AC", ea)
                for e in range(2):
                    p.o("te", lambda E, kb=kb, e=e, nkb=nkb: E.matmul(
                        PS[4 + e][:, :], vh[:, kb * HDV + e * 128: kb * HDV + e * 128 + 128],
                        stb(kb % 2)[:, :],
                        start=(kb == 0), stop=(kb == nkb - 1)),
                        inc="PE" if e == 1 else None)
                sv_pe[(qb, kb)] = p.c("PE")

            emit_st(0)
            if nkb > 1:
                emit_st(1)
            for kb in range(nkb):
                emit_sv(kb)
                if kb + 2 < nkb:
                    emit_st(kb + 2)
            ofm_stage = bf6[:, 2 * A6_OFM: 2 * A6_OFM + 1024]
            last_pe = sv_pe[(qb, nkb - 1)]
            if qb >= 1:
                s_, v_ = ofm_tp[(qb - 1) % 2]
                p.w("ve", s_, v_)
                p.w("sc", s_, v_)
            elif 3 % 2 in ofm_tp and h > 0:
                s_, v_ = ofm_tp[3 % 2]
                p.w("ve", s_, v_)
                p.w("sc", s_, v_)
            p.w("ve", "PE", last_pe)
            p.o("ve", lambda E, o=ofm_stage: E.tensor_copy(o[:, 0:512], PS[4][:, :]), inc="VE")
            p.w("sc", "PE", last_pe)
            p.o("sc", lambda E, o=ofm_stage: E.copy(o[:, 512:1024], PS[5][:, :]), inc="AC")
            out_evac[qb] = (p.c("VE"), p.c("AC"))
            qeng, qsem = "sy", "SY"
            p.w(qeng, "VE", out_evac[qb][0])
            p.w(qeng, "AC", out_evac[qb][1])
            if qb >= 1:
                p.w(qeng, "VE", gn_done[qb - 1])
            elif prev_head_gn_done is not None:
                p.w(qeng, "VE", prev_head_gn_done)
            for tl in range(4):
                for e in range(2):
                    p.dmat(qeng, qsem, otm_st[:, tl * 256 + e * 128: tl * 256 + e * 128 + 128],
                           ofm_stage[:, e * 512 + tl * 128: e * 512 + tl * 128 + 128])
            ofm_tp[qb % 2] = (qsem, p.c(qsem))
            otm_ready = (qsem, p.c(qsem))

            for tl in range(4):
                tt = 4 * qb + tl
                bank = PS[tl % 2]
                if qb == 0 and tl < 2:
                    p.w("te", "AC", evac_ac[14 + tl])
                elif tl < 2:
                    p.w("te", "AC", gate_ac[(qb - 1, 2 + tl)])
                else:
                    p.w("te", "AC", gate_ac[(qb, tl - 2)])
                for ic in range(NIC):
                    p.o("te", lambda E, b=bank, ic=ic, tt=tt: E.matmul(
                        b[:, 0:256], hfm(ic)[:, tt * 128:(tt + 1) * 128],
                        gatew[:, ic * 256:(ic + 1) * 256],
                        start=(ic == 0), stop=(ic == NIC - 1)),
                        inc="PE" if ic == NIC - 1 else None)
                gmm = p.c("PE")
                p.w("sc", "PE", gmm)
                if qb >= 1:
                    p.w("sc", "VE", gn_done[qb - 1])
                elif prev_head_gn_done is not None:
                    p.w("sc", "VE", prev_head_gn_done)
                p.o("sc", lambda E, b=bank, tl=tl: E.activation(
                    gt_st[:, tl * 256:(tl + 1) * 256], b[:, 0:256], AF.Silu), inc="AC")
                gate_ac[(qb, tl)] = p.c("AC")

            p.w("ve", otm_ready[0], otm_ready[1])
            for tl in range(4):
                ot = otm_st[:, tl * 256:(tl + 1) * 256]
                p.o("ve", lambda E, ot=ot: E.bn_stats(st6[:, 0:6], ot), inc="VE")
                p.w("ve", "VE")
                p.o("ve", lambda E, tl=tl: E.bn_aggr(mqr[:, tl * 2: tl * 2 + 2], st6[:, 0:6]),
                    inc="VE")
                p.w("sc", "VE")
                p.o("sc", lambda E, tl=tl: E.activation(mqr[:, tl * 2 + 1: tl * 2 + 2],
                                                        mqr[:, tl * 2 + 1: tl * 2 + 2],
                                                        AF.Ln, bias=ccol[:, 0:1]), inc="AC")
                p.w("sc", "AC")
                p.o("sc", lambda E, tl=tl: E.activation(mqr[:, tl * 2 + 1: tl * 2 + 2],
                                                        mqr[:, tl * 2 + 1: tl * 2 + 2],
                                                        AF.Exp, scale=ccol[:, 1:2]), inc="AC")
                p.w("ve", "AC")
                p.w("ve", "AC", gate_ac[(qb, tl)])
                gnp = gn_st(qb % 2)
                if tl == 0 and (qb % 2) in gn_tp and (qb >= 2 or h > 0):
                    s_, v_ = gn_tp[qb % 2]
                    p.w("ve", s_, v_)
                p.o("ve", lambda E, ot=ot, tl=tl, g=gnp: E.tensor_scalar(
                    g[:, tl * 256:(tl + 1) * 256], ot, mqr[:, tl * 2: tl * 2 + 1],
                    mqr[:, tl * 2 + 1: tl * 2 + 2], op0=ALU.subtract, op1=ALU.mult))
                p.o("ve", lambda E, tl=tl, g=gnp: E.tensor_mul(
                    g[:, tl * 256:(tl + 1) * 256], g[:, tl * 256:(tl + 1) * 256],
                    gt_st[:, tl * 256:(tl + 1) * 256]), inc="VE")
            gn_done[qb] = p.c("VE")
            qeng, qsem = "sy", "SY"
            p.w(qeng, "VE", gn_done[qb])
            for tl in range(4):
                tt = 4 * qb + tl
                for e in range(2):
                    fc = 2 * h + e
                    p.dmat(qeng, qsem,
                           AR3[:, fc * T + tt * 128: fc * T + (tt + 1) * 128],
                           gn_st(qb % 2)[:, tl * 256 + e * 128: tl * 256 + e * 128 + 128])
            gn_tp[qb % 2] = (qsem, p.c(qsem))
        head_done_pe = p.c("PE")
        prev_head_out_evac = out_evac[3]
        prev_head_gn_done = gn_done[3]

    gnfm_sy, gnfm_sc = p.c("SY"), p.c("SC")
    if "gnfm" in tap_ap:
        for fc in range(16):
            for half in range(2):
                p.w("ve", "SY", gnfm_sy)
                p.w("ve", "SC", gnfm_sc)
                p.w("ve", "ST", p.c("ST"))
                p.o("ve", lambda E, fc=fc, half=half: E.tensor_copy(
                    scr(0), AR3[:, fc * T + half * 1024: fc * T + (half + 1) * 1024]), inc="VE")
                p.w("sy", "VE")
                p.dma("sy", "ST", tap_ap["gnfm"][fc * 128:(fc + 1) * 128,
                                                 half * 1024:(half + 1) * 1024], scr(0))

    p.w("sy", "PE", head_done_pe)
    p.w("sy", "MC", misc_cast)
    p.w("sy", "VE", p.c("VE"))
    p.w("sy", "PO", p.c("PO"))
    p.w("sy", "AC", p.c("AC"))
    p.w("sy", "SC", p.c("SC"))
    p.w("sy", "SY", p.c("SY"))
    for q4 in range(4):
        p.dma("sy", "WQ", AR4[:, q4 * 4096:(q4 + 1) * 4096].rearrange("p (fc c) -> p fc c", c=1024),
              wprj_c.ap().rearrange("(fc p) o -> p fc o", p=128)[:, q4 * 4:(q4 + 1) * 4, :])
    prj_ready = p.c("WQ")
    prjw = lambda fc: AR4[:, fc * 1024:(fc + 1) * 1024]

    p.w("te", "WQ", prj_ready)
    p.w("te", "SY", gnfm_sy)
    p.w("te", "SC", gnfm_sc)
    prev_evac = {}
    x1_ve = {}
    for tt in range(NT):
        for ob in range(2):
            k2 = tt * 2 + ob
            bank = PS[2 + k2 % 2]
            if k2 >= 2:
                p.w("te", "VE", prev_evac[k2 % 2])
            for fc in range(16):
                p.o("te", lambda E, b=bank, fc=fc, tt=tt, ob=ob: E.matmul(
                    b[:, :], AR3[:, fc * T + tt * 128: fc * T + (tt + 1) * 128],
                    prjw(fc)[:, ob * 512:(ob + 1) * 512],
                    start=(fc == 0), stop=(fc == 15)),
                    inc="PE" if fc == 15 else None)
            mmv = p.c("PE")
            p.w("ve", "PE", mmv)
            p.o("ve", lambda E, b=bank, ob=ob: E.tensor_copy(
                scr(0)[:, ob * 512:(ob + 1) * 512], b[:, :]), inc="VE")
            prev_evac[k2 % 2] = p.c("VE")
        p.o("ve", lambda E: E.bn_stats(st6[:, 0:6], scr(0)[:, 0:512]))
        p.o("ve", lambda E: E.bn_stats(st6[:, 6:12], scr(0)[:, 512:1024]), inc="VE")
        p.w("ve", "VE")
        p.o("ve", lambda E: E.bn_aggr(mv2[:, 0:2], st6[:, 0:12]), inc="VE")
        p.w("sc", "VE")
        p.o("sc", lambda E: E.activation(mv2[:, 2:3], mv2[:, 1:2], AF.Ln, bias=ccol[:, 0:1]), inc="AC")
        p.w("sc", "AC")
        p.o("sc", lambda E: E.activation(mv2[:, 2:3], mv2[:, 2:3], AF.Exp, scale=ccol[:, 1:2]), inc="AC")
        p.w("ve", "AC")
        p.o("ve", lambda E: E.tensor_scalar(scr(1), scr(0), mv2[:, 0:1], mv2[:, 2:3],
                                            op0=ALU.subtract, op1=ALU.mult))
        xt = AR1[:, tt * D:(tt + 1) * D]
        p.o("ve", lambda E, xt=xt: E.tensor_add(xt, xt, scr(1)), inc="VE")
        x1_ve[tt] = p.c("VE")
    x1_done = p.c("VE")

    tap_tm("x1", lambda tt: AR1[:, tt * D:(tt + 1) * D], "VE", x1_done)
    if stage <= 2:
        finish(p, nc, ctx)
        return nc

    p.w("sy", "VE", x1_done)
    p.dma("sy", "SY", ag_in.ap().rearrange("(tile p) d -> p tile d", p=128),
          AR1[:, :].rearrange("p (tile d) -> p tile d", d=D))
    agin_done = p.c("SY")
    p.w("gp", "SY", agin_done)
    p.o("gp", lambda E: E.collective_compute(
        "AllGather", ALU.bypass, replica_groups=[list(range(NCORES))],
        ins=[ag_in.ap().opt()], outs=[ag_out.ap().opt()]), inc="CC")
    p.w("gp", "CC", p.c("CC"))
    p.w("gp", "SY", misc_loaded)
    for tt in range(NT):
        p.o("gp", lambda E, tt=tt: E.indirect_dma_start(
            AR1[:, tt * D:(tt + 1) * D], None,
            ag_out.ap(), IndirectOffsetOnAxis(gidx[:, tt:tt + 1], 0)), inc="GPI", n=16)
    p.dma("gp", "GP", ag_in.ap()[0:1, :], AR1[:, 0:8].bitcast(F32)[0:1, 0:4]) if False else None
    p.dma("gp", "GP", ag_in.ap()[0:1, 0:1024], AR1[0:1, 0:2048].bitcast(F32))
    x1b_done = p.c("GP")
    tap_tm("x1b", lambda tt: AR1[:, tt * D:(tt + 1) * D], "GP", x1b_done)

    p.w("ve", "GP", x1b_done)
    bst = lambda par: AR3[:, 30720 + par * 1024: 30720 + (par + 1) * 1024]
    xln_tp = {}
    for tt in range(NT):
        xt = AR1[:, tt * D:(tt + 1) * D]
        p.o("ve", lambda E, xt=xt: E.bn_stats(st6[:, 0:6], xt[:, 0:512]))
        p.o("ve", lambda E, xt=xt: E.bn_stats(st6[:, 6:12], xt[:, 512:1024]), inc="VE")
        p.w("ve", "VE")
        p.o("ve", lambda E, tt=tt: E.bn_aggr(xstat[:, tt * 2: tt * 2 + 2], st6[:, 0:12]),
            inc="VE" if tt == NT - 1 else None)
    bstats_done = p.c("VE")
    p.w("sc", "VE", bstats_done)
    for tt in range(NT):
        p.o("sc", lambda E, tt=tt: E.activation(xstat[:, tt * 2 + 1: tt * 2 + 2],
                                                xstat[:, tt * 2 + 1: tt * 2 + 2], AF.Ln,
                                                bias=ccol[:, 0:1]), inc="AC")
    p.w("sc", "AC")
    for tt in range(NT):
        p.o("sc", lambda E, tt=tt: E.activation(xstat[:, tt * 2 + 1: tt * 2 + 2],
                                                xstat[:, tt * 2 + 1: tt * 2 + 2], AF.Exp,
                                                scale=ccol[:, 1:2]),
            inc="AC" if tt == NT - 1 else None)
    brstd_done = p.c("AC")
    p.w("ve", "AC", brstd_done)
    for tt in range(NT):
        xt = AR1[:, tt * D:(tt + 1) * D]
        if tt >= 2:
            s_, v_ = xln_tp[tt % 2]
            p.w("ve", s_, v_)
        p.o("ve", lambda E, xt=xt, tt=tt: E.tensor_scalar(
            bst(tt % 2)[:, :], xt, xstat[:, tt * 2: tt * 2 + 1],
            xstat[:, tt * 2 + 1: tt * 2 + 2], op0=ALU.subtract, op1=ALU.mult), inc="VE")
        hdone = p.c("VE")
        qeng, qsem = "sy", "SY"
        p.w(qeng, "VE", hdone)
        for ic in range(NIC):
            p.dmat(qeng, qsem, AR2[:, ic * T + tt * 128: ic * T + (tt + 1) * 128],
                   bst(tt % 2)[:, ic * 128:(ic + 1) * 128])
        xln_tp[tt % 2] = (qsem, p.c(qsem))
    xln_sy, xln_sc = p.c("SY"), p.c("SC")
    xlnfm = lambda ic: AR2[:, ic * T:(ic + 1) * T]
    if "xlnfm" in tap_ap:
        for ic in range(NIC):
            for half in range(2):
                p.w("ve", "SY", xln_sy)
                p.w("ve", "ST", p.c("ST"))
                p.o("ve", lambda E, ic=ic, half=half: E.tensor_copy(
                    scr(0), xlnfm(ic)[:, half * 1024:(half + 1) * 1024]), inc="VE")
                p.w("sy", "VE")
                p.dma("sy", "ST", tap_ap["xlnfm"][ic * 128:(ic + 1) * 128,
                                                  half * 1024:(half + 1) * 1024], scr(0))

    ipw_par = lambda par: AR7[:, B7_IPW + par * 4096: B7_IPW + (par + 1) * 4096]
    ipsem = ("IP0", "IP1")
    ip_loaded = {}

    def load_ip(src_c, ob, pe_gate, bias_off=None):
        par = ob % 2
        p.w("sy", "PE", pe_gate)
        p.w("sy", "MC", misc_cast)
        p.w("sy", "GP", small_loaded)
        p.dma("sy", ipsem[par], ipw_par(par)[:, :].rearrange("p (ic c) -> p ic c", c=512),
              src_c.ap().rearrange("(ic p) o -> p ic o", p=128)[:, :, ob * 512:(ob + 1) * 512])
        if bias_off is not None:
            p.dma("sy", ipsem[par], brow[:, par * 1024: par * 1024 + 512],
                  bias_c.ap()[:, bias_off: bias_off + 512])
        return p.c(ipsem[par])

    ip_loaded[0] = load_ip(ipw_c, 0, 0, bias_off=0)
    ip_loaded[1] = load_ip(ipw_c, 1, 0, bias_off=512)
    ip_consumed = {}
    ip_evac = {}
    for ob in range(6):
        par = ob % 2
        p.w("te", ipsem[par], ip_loaded[ob])
        if ob == 0:
            p.w("te", "SY", xln_sy)
            p.w("te", "SC", xln_sc)
            p.w("te", "GP", small_loaded)
        for tt in range(NT):
            k2 = tt % 2
            bank = PS[4 + k2]
            if (ob * NT + tt) >= 2:
                p.w("te", "VE", ip_evac[k2])
            p.o("te", lambda E, b=bank, par=par: E.matmul(
                b[:, :], ones_b[:, :], brow[:, par * 1024: par * 1024 + 512], start=True, stop=False))
            for ic in range(NIC):
                p.o("te", lambda E, b=bank, ic=ic, tt=tt, par=par: E.matmul(
                    b[:, :], xlnfm(ic)[:, tt * 128:(tt + 1) * 128],
                    ipw_par(par)[:, ic * 512:(ic + 1) * 512],
                    start=False, stop=(ic == NIC - 1)),
                    inc="PE" if ic == NIC - 1 else None)
            mmv = p.c("PE")
            p.w("ve", "PE", mmv)
            if ob < 2:
                dst = AR3[:, tt * 1024 + ob * 512: tt * 1024 + (ob + 1) * 512]
                p.o("ve", lambda E, b=bank, d=dst: E.tensor_scalar_mul(d, b[:, :], 0.125),
                    inc="VE")
            elif ob < 4:
                dst = AR3[:, 16384 + tt * 1024 + (ob - 2) * 512: 16384 + tt * 1024 + (ob - 1) * 512]
                p.o("ve", lambda E, b=bank, d=dst: E.tensor_copy(d, b[:, :]), inc="VE")
            else:
                dst = AR4[:, tt * 1024 + (ob - 4) * 512: tt * 1024 + (ob - 3) * 512]
                p.o("ve", lambda E, b=bank, d=dst: E.tensor_copy(d, b[:, :]), inc="VE")
            ip_evac[k2] = p.c("VE")
        ip_consumed[ob] = p.c("PE")
        if ob + 2 < 6:
            ip_loaded[ob + 2] = load_ip(ipw_c, ob + 2, ip_consumed[ob], bias_off=(ob + 2) * 512)
    qk_done = p.c("VE")

    if "qk" in tap_ap:
        for tt in range(NT):
            p.w("ve", "VE", qk_done)
            p.w("ve", "ST", p.c("ST"))
            p.o("ve", lambda E, tt=tt: E.tensor_copy(scr(0), AR3[:, tt * 1024:(tt + 1) * 1024]),
                inc="VE")
            p.w("sy", "VE")
            p.dma("sy", "ST", tap_ap["qk"][tt * 128:(tt + 1) * 128, 0:1024], scr(0))
            p.w("ve", "ST", p.c("ST"))
            p.o("ve", lambda E, tt=tt: E.tensor_copy(
                scr(0), AR3[:, 16384 + tt * 1024: 16384 + (tt + 1) * 1024]), inc="VE")
            p.w("sy", "VE")
            p.dma("sy", "ST", tap_ap["qk"][tt * 128:(tt + 1) * 128, 1024:2048], scr(0))

    mbuf = bf6[:, 2 * (A6_SCR + 1024): 2 * (A6_SCR + 1024) + 1024]
    mbuf3 = mbuf.rearrange("p (h c) -> p h c", c=64)
    p.w("ve", "VE", qk_done)
    sc_done = {}
    for s in range(NH):
        for hf in range(2):
            qtile = AR3[:, (s * 2 + hf) * 1024: (s * 2 + hf + 1) * 1024]
            for t in range(NH):
                ktile = AR3[:, 16384 + (t * 2 + hf) * 1024: 16384 + (t * 2 + hf + 1) * 1024]
                p.o("ve", lambda E, q=qtile, k=ktile: E.tensor_mul(mbuf[:, :], q, k))
                dst = scr(0)[:, hf * 128: hf * 128 + 128].rearrange(
                    "p (h t8) -> p h t8", t8=8)[:, :, t:t + 1]
                p.o("ve", lambda E, d=dst: E.tensor_reduce(d, mbuf3, axis=AX.X, op=ALU.add))
        for hf in range(2):
            sc3 = scr(0)[:, hf * 128: hf * 128 + 128].rearrange("p (h t8) -> p h t8", t8=8)
            mx = scr(1)[:, 992 + hf * 16: 992 + hf * 16 + 16]
            p.o("ve", lambda E, a=mx, b=sc3: E.tensor_reduce(
                a.unsqueeze(2), b, axis=AX.X, op=ALU.max), inc="VE")
            p.w("ve", "VE")
            p.o("ve", lambda E, a=sc3, m=mx: E.tensor_sub(
                a, a, m.unsqueeze(2).broadcast_to([128, 16, 8])), inc="VE")
            p.w("sc", "VE")
            flat = scr(0)[:, hf * 128: hf * 128 + 128]
            p.o("sc", lambda E, f=flat: E.activation(f, f, AF.Exp), inc="AC")
            p.w("ve", "AC")
            p.o("ve", lambda E, a=mx, b=sc3: E.tensor_reduce(
                a.unsqueeze(2), b, axis=AX.X, op=ALU.add), inc="VE")
            p.w("ve", "VE")
            p.o("ve", lambda E, a=mx: E.reciprocal(a, a), inc="VE")
            p.w("ve", "VE")
            dstw = attw[:, s * 256 + hf * 128: s * 256 + hf * 128 + 128].rearrange(
                "p (h t8) -> p h t8", t8=8)
            p.o("ve", lambda E, d=dstw, b=sc3, m=mx: E.tensor_tensor(
                d, b, m.unsqueeze(2).broadcast_to([128, 16, 8]), op=ALU.mult),
                inc="VE")
        sc_done[s] = p.c("VE")
    scores_all = p.c("VE")
    ctx_done = {}
    p.w("ve", "VE", scores_all)
    for s in range(NH):
        for hf in range(2):
            dstc = AR3[:, (s * 2 + hf) * 1024: (s * 2 + hf + 1) * 1024]
            d3 = dstc.rearrange("p (h c) -> p h c", c=64)
            aw = attw[:, s * 256 + hf * 128: s * 256 + hf * 128 + 128].rearrange(
                "p (h t8) -> p h t8", t8=8)
            for t in range(NH):
                vtile = AR4[:, (t * 2 + hf) * 1024: (t * 2 + hf + 1) * 1024]
                v3 = vtile.rearrange("p (h c) -> p h c", c=64)
                awb = aw[:, :, t:t + 1].broadcast_to([128, 16, 64])
                if t == 0:
                    p.o("ve", lambda E, d=d3, v=v3, a=awb: E.tensor_tensor(d, v, a, op=ALU.mult))
                else:
                    p.o("ve", lambda E, v=v3, a=awb: E.tensor_tensor(mbuf3, v, a, op=ALU.mult))
                    p.o("ve", lambda E, d=d3: E.tensor_add(d, d, mbuf3))
            p.o("ve", lambda E: E.engine_nop(), inc="VE")
        ctx_done[s] = p.c("VE")
    for s in range(NH):
        for hf in range(2):
            tt = s * 2 + hf
            qeng, qsem = "sy", "SY"
            p.w(qeng, "VE", ctx_done[s])
            for ic in range(NIC):
                p.dmat(qeng, qsem, AR2[:, ic * T + tt * 128: ic * T + (tt + 1) * 128],
                       AR3[:, tt * 1024 + ic * 128: tt * 1024 + ic * 128 + 128])
    ctxfm_sy, ctxfm_sc = p.c("SY"), p.c("SC")
    tap_tm("ctx", lambda tt: AR3[:, tt * 1024:(tt + 1) * 1024], "VE", ctx_done[NH - 1])

    ow_loaded = {ob: load_ip(outw_c, ob, ip_consumed[5], bias_off=3072 + ob * 512) for ob in range(2)}
    p.w("te", "SY", ctxfm_sy)
    p.w("te", "SC", ctxfm_sc)
    ao_evac = {}
    zfm_tp = {}
    xmid_ve = {}
    for tt in range(NT):
        for ob in range(2):
            bank = PS[4 + ob]
            p.w("te", ipsem[ob], ow_loaded[ob])
            if tt >= 1:
                p.w("te", "VE", ao_evac[ob])
            p.o("te", lambda E, b=bank, ob=ob: E.matmul(
                b[:, :], ones_b[:, :], brow[:, ob * 1024: ob * 1024 + 512],
                start=True, stop=False))
            for ic in range(NIC):
                p.o("te", lambda E, b=bank, ic=ic, tt=tt, ob=ob: E.matmul(
                    b[:, :], AR2[:, ic * T + tt * 128: ic * T + (tt + 1) * 128],
                    ipw_par(ob)[:, ic * 512:(ic + 1) * 512],
                    start=False, stop=(ic == NIC - 1)),
                    inc="PE" if ic == NIC - 1 else None)
            mmv = p.c("PE")
            p.w("ve", "PE", mmv)
            p.o("ve", lambda E, b=bank, ob=ob: E.tensor_copy(
                scr(0)[:, ob * 512:(ob + 1) * 512], b[:, :]), inc="VE")
            ao_evac[ob] = p.c("VE")
        xt = AR1[:, tt * D:(tt + 1) * D]
        p.o("ve", lambda E, xt=xt, tt=tt: E.tensor_scalar(
            scr(1), xt, xstat[:, tt * 2: tt * 2 + 1], xstat[:, tt * 2 + 1: tt * 2 + 2],
            op0=ALU.subtract, op1=ALU.mult))
        p.o("ve", lambda E: E.tensor_add(scr(0), scr(0), scr(1)))
        p.o("ve", lambda E: E.bn_stats(st6[:, 0:6], scr(0)[:, 0:512]))
        p.o("ve", lambda E: E.bn_stats(st6[:, 6:12], scr(0)[:, 512:1024]), inc="VE")
        p.w("ve", "VE")
        p.o("ve", lambda E: E.bn_aggr(mv2[:, 0:2], st6[:, 0:12]), inc="VE")
        p.w("sc", "VE")
        p.o("sc", lambda E: E.activation(mv2[:, 2:3], mv2[:, 1:2], AF.Ln, bias=ccol[:, 0:1]), inc="AC")
        p.w("sc", "AC")
        p.o("sc", lambda E: E.activation(mv2[:, 2:3], mv2[:, 2:3], AF.Exp, scale=ccol[:, 1:2]), inc="AC")
        p.w("ve", "AC")
        p.o("ve", lambda E: E.tensor_scalar(scr(1), scr(0), mv2[:, 0:1], mv2[:, 2:3],
                                            op0=ALU.subtract, op1=ALU.mult))
        p.o("ve", lambda E: E.tensor_mul(scr(1), scr(1), g1t))
        xmb = AR3[:, tt * D:(tt + 1) * D]
        p.o("ve", lambda E, d=xmb: E.tensor_tensor(d, scr(1), b1t, op=ALU.add), inc="VE")
        xmid_ve[tt] = p.c("VE")
        qeng, qsem = "sy", "SY"
        p.w(qeng, "VE", xmid_ve[tt])
        for ic in range(NIC):
            p.dmat(qeng, qsem, AR4[:, ic * T + tt * 128: ic * T + (tt + 1) * 128],
                   xmb[:, ic * 128:(ic + 1) * 128])
        zfm_tp[tt] = (qsem, p.c(qsem))
    zfm_sy, zfm_sc = p.c("SY"), p.c("SC")
    zfm = lambda ic: AR4[:, ic * T:(ic + 1) * T]
    xmid_done = p.c("VE")
    tap_tm("xmid", lambda tt: AR3[:, tt * D:(tt + 1) * D], "VE", xmid_done)
    if stage <= 3:
        finish(p, nc, ctx)
        return nc

    p.w("te", "SY", zfm_sy)
    p.w("te", "SC", zfm_sc)
    gat_evac = {}
    for tt in range(NT):
        bank = PS[tt % 2]
        if tt >= 2:
            p.w("te", "VE", gat_evac[tt % 2])
        p.o("te", lambda E, b=bank: E.matmul(b[:, 0:NE], ones_b[:, :], gateb_r[:, :],
                                             start=True, stop=False))
        for ic in range(NIC):
            p.o("te", lambda E, b=bank, ic=ic, tt=tt: E.matmul(
                b[:, 0:NE], zfm(ic)[:, tt * 128:(tt + 1) * 128], gwsb[:, ic * NE:(ic + 1) * NE],
                start=False, stop=(ic == NIC - 1)),
                inc="PE" if ic == NIC - 1 else None)
        mmv = p.c("PE")
        p.w("ve", "PE", mmv)
        pr = scr(0)[:, 0:NE]
        m1 = scr(0)[:, 8:16]
        v1 = scr(0)[:, 16:17]
        v2 = scr(0)[:, 17:18]
        def ev(fn):
            p.o("ve", fn, inc="VE")
            p.w("ve", "VE")
        ev(lambda E, b=bank, pr=pr: E.tensor_copy(pr, b[:, 0:NE]))
        ev(lambda E, pr=pr, v1=v1: E.tensor_reduce(v1, pr, axis=AX.X, op=ALU.max))
        p.o("ve", lambda E, pr=pr, v1=v1: E.tensor_single_scalar(pr, pr, v1[:, 0:1],
                                                                 op=ALU.subtract), inc="VE")
        p.w("sc", "VE")
        p.o("sc", lambda E, pr=pr: E.activation(pr, pr, AF.Exp), inc="AC")
        p.w("ve", "AC")
        ev(lambda E, pr=pr, v1=v1: E.tensor_reduce(v1, pr, axis=AX.X, op=ALU.add))
        ev(lambda E, v1=v1: E.reciprocal(v1, v1))
        ev(lambda E, pr=pr, v1=v1: E.tensor_scalar_mul(pr, pr, v1[:, 0:1]))
        ev(lambda E, pr=pr, v1=v1: E.tensor_reduce(v1, pr, axis=AX.X, op=ALU.max))
        ev(lambda E, pr=pr, m1=m1, v1=v1: E.tensor_single_scalar(m1, pr, v1[:, 0:1],
                                                                 op=ALU.is_ge))
        ev(lambda E, pr=pr, m1=m1: E.scalar_tensor_tensor(m1, m1, LNEG, pr,
                                                          op0=ALU.mult, op1=ALU.add))
        ev(lambda E, m1=m1, v2=v2: E.tensor_reduce(v2, m1, axis=AX.X, op=ALU.max))
        ev(lambda E, pr=pr, m1=m1, v2=v2: E.tensor_single_scalar(m1, pr, v2[:, 0:1],
                                                                 op=ALU.is_ge))
        ev(lambda E, pr=pr, m1=m1: E.tensor_mul(pr, pr, m1))
        ev(lambda E, pr=pr, v1=v1: E.tensor_reduce(v1, pr, axis=AX.X, op=ALU.add))
        ev(lambda E, v1=v1: E.reciprocal(v1, v1))
        p.o("ve", lambda E, pr=pr, v1=v1, tt=tt: E.tensor_scalar_mul(
            comb[:, tt * NE:(tt + 1) * NE], pr, v1[:, 0:1]), inc="VE")
        gat_evac[tt % 2] = p.c("VE")
    comb_done = p.c("VE")
    tap_tm("comb", lambda tt: comb[:, tt * NE:(tt + 1) * NE], "VE", comb_done, width=NE)
    if stage == 35:
        finish(p, nc, ctx)
        return nc

    w1buf = lambda par: AR7[:, M7_W1 + (par % 4) * 1024: M7_W1 + (par % 4 + 1) * 1024]
    w2buf = lambda par: AR7[:, M7_W2 + (par % 4) * 1024: M7_W2 + (par % 4 + 1) * 1024]
    acc_par = lambda tb: f32_AR3[:, 8192 + (tb % 2) * 4096: 8192 + (tb % 2 + 1) * 4096]
    h1buf = AR2
    w1sems = ("W10", "W11", "W12", "W13")
    w2sems = ("W20", "W21", "W22", "W23")
    out_stage = lambda par: scr(par)
    st_cnt = 0
    acc_consumed = {}
    y_evac = 0

    def load_w1(e, fc, pe_gate):
        par = fc % 4
        p.w("sy", "PE", pe_gate)
        p.w("sy", "MC", moe_cast[e])
        p.dma("sy", w1sems[par], w1buf(par)[:, :],
              w1_c.ap()[e, :, :].rearrange("(fc p) c -> fc p c", p=128)[fc, :, :])
        return p.c(w1sems[par])

    def load_w2(e, fc, pe_gate):
        par = fc % 4
        p.w("sy", "PE", pe_gate)
        p.w("sy", "MC", moe_cast[e])
        p.dma("sy", w2sems[par], w2buf(par)[:, :],
              w2_c.ap()[e, fc * 128:(fc + 1) * 128, :])
        return p.c(w2sems[par])

    w1_ld = {i: load_w1(0, i, 0) for i in range(4)}
    w2_ld = {i: load_w2(0, i, 0) for i in range(4)}
    if stage == 41:
        for s_ in ("W10", "W11", "W12", "W13", "W20", "W21", "W22", "W23", "MC"):
            p.w("sy", s_, p.c(s_))
        finish(p, nc, ctx)
        return nc
    h1_evac = {}
    first_iter = True
    epi_pending = None
    for tb in range(4):
        acc_f32 = acc_par(tb)
        p.w("ve", "VE", comb_done)
        if tb > 1:
            p.w("ve", "VE", acc_consumed[tb % 2])
        for tc in range(4):
            tt = tb * 4 + tc
            p.o("ve", lambda E, tt=tt, tc=tc, acc_f32=acc_f32: E.tensor_copy(
                acc_f32[:, tc * 1024:(tc + 1) * 1024], AR3[:, tt * D:(tt + 1) * D]), inc="VE")
        acc_init = p.c("VE")
        for e in range(NE):
            if first_iter:
                p.w("te", "SY", zfm_sy)
                p.w("te", "SC", zfm_sc)
                p.w("te", "VE", comb_done)
            else:
                p.w("te", "VE", y_evac01)
            for fc in range(NFC):
                par = fc % 4
                bpar = fc % 2
                p.w("te", w1sems[par], w1_ld[fc])
                if fc >= 2:
                    s_, v_ = h1_evac[bpar]
                    p.w("te", s_, v_)
                bank = PS[bpar]
                for ic in range(NIC):
                    p.o("te", lambda E, b=bank, ic=ic, tb=tb, par=par: E.matmul(
                        b[:, :], w1buf(par)[:, ic * 128:(ic + 1) * 128],
                        zfm(ic)[:, tb * 512:(tb + 1) * 512],
                        start=(ic == 0), stop=(ic == NIC - 1)),
                        inc="PE" if ic == NIC - 1 else None)
                w1_pe = p.c("PE")
                if fc + 4 < NFC:
                    w1_ld[fc + 4] = load_w1(e, fc + 4, w1_pe)
                else:
                    ne_, nt_ = (e + 1, tb) if e + 1 < NE else (0, tb + 1)
                    if nt_ < 4:
                        w1_ld[fc + 4 - NFC] = load_w1(ne_, fc + 4 - NFC, w1_pe)
                eng3, sem3 = ("sc", "AC") if fc % 2 == 0 else ("ve", "VE")
                p.w(eng3, "PE", w1_pe)
                if stage == 39:
                    if eng3 == "sc":
                        p.o(eng3, lambda E, b=bank, fc=fc: E.copy(
                            h1buf[:, fc * 512:(fc + 1) * 512], b[:, :]), inc=sem3)
                    else:
                        p.o(eng3, lambda E, b=bank, fc=fc: E.tensor_copy(
                            h1buf[:, fc * 512:(fc + 1) * 512], b[:, :]), inc=sem3)
                elif eng3 == "sc":
                    p.o(eng3, lambda E, b=bank, e=e, fc=fc: E.activation(
                        h1buf[:, fc * 512:(fc + 1) * 512], b[:, :], AF.Relu,
                        bias=b1c[:, e * NFC + fc: e * NFC + fc + 1]), inc=sem3)
                else:
                    p.o(eng3, lambda E, b=bank, e=e, fc=fc: E.tensor_scalar(
                        h1buf[:, fc * 512:(fc + 1) * 512], b[:, :],
                        b1c[:, e * NFC + fc: e * NFC + fc + 1], 0.0,
                        op0=ALU.add, op1=ALU.max), inc=sem3)
                h1_evac[bpar] = (sem3, p.c(sem3))
            h1_ve, h1_ac = p.c("VE"), p.c("AC")
            if stage in (38, 39) and e == 0:
                first_iter = False
                break
            p.w("sy", "PE", p.c("PE"))
            p.dma("sy", "WQ", brow[:, 0:1024], bias_c.ap()[:, 3072 + D + e * D: 3072 + D + (e + 1) * D])
            b2_ld = p.c("WQ")
            p.w("te", "WQ", b2_ld)
            p.w("te", "VE", h1_ve)
            p.w("te", "AC", h1_ac)
            if not first_iter:
                p.w("te", "VE", y_evac)
            if tb > 0 or e > 0:
                p.w("te", "VE", acc_init)
            for tc in range(4):
                for ob in range(2):
                    bank = PS[tc * 2 + ob]
                    p.o("te", lambda E, b=bank, ob=ob: E.matmul(
                        b[:, :], ones_b[:, :], brow[:, ob * 512:(ob + 1) * 512],
                        start=True, stop=False))
            for fc in range(NFC):
                par = fc % 4
                p.w("te", w2sems[par], w2_ld[fc])
                for tc in range(4):
                    for ob in range(2):
                        bank = PS[tc * 2 + ob]
                        p.o("te", lambda E, b=bank, fc=fc, tc=tc, ob=ob: E.matmul(
                            b[:, :], h1buf[:, fc * 512 + tc * 128: fc * 512 + tc * 128 + 128],
                            w2buf(fc % 4)[:, ob * 512:(ob + 1) * 512],
                            start=False, stop=(fc == NFC - 1)),
                            inc="PE" if (tc == 3 and ob == 1) else None)
                w2_pe = p.c("PE")
                if fc + 4 < NFC:
                    w2_ld[fc + 4] = load_w2(e, fc + 4, w2_pe)
                else:
                    ne_, nt_ = (e + 1, tb) if e + 1 < NE else (0, tb + 1)
                    if nt_ < 4:
                        w2_ld[fc + 4 - NFC] = load_w2(ne_, fc + 4 - NFC, w2_pe)
            y_done = p.c("PE")
            p.w("ve", "PE", y_done)
            p.w("ve", "VE", acc_init)
            for tc in range(4):
                tt = tb * 4 + tc
                for ob in range(2):
                    bank = PS[tc * 2 + ob]
                    dst = acc_f32[:, tc * 1024 + ob * 512: tc * 1024 + (ob + 1) * 512]
                    p.o("ve", lambda E, b=bank, d=dst, tt=tt, e=e: E.scalar_tensor_tensor(
                        d, b[:, :], comb[:, tt * NE + e: tt * NE + e + 1], d,
                        op0=ALU.mult, op1=ALU.add),
                        inc="VE" if (tc == 0 and ob == 1) or (tc == 3 and ob == 1) else None)
                if tc == 0:
                    y_evac01 = p.c("VE")
            y_evac = p.c("VE")
            first_iter = False
            if stage == 37 and e == 0:
                break
        if "accf" in tap_ap:
            for tc in range(4):
                tt = tb * 4 + tc
                p.w("ve", "VE", y_evac)
                p.w("ve", "ST", p.c("ST"))
                p.o("ve", lambda E, tc=tc: E.tensor_copy(scr(0), acc_f32[:, tc * 1024:(tc + 1) * 1024]), inc="VE")
                p.w("sy", "VE")
                p.dma("sy", "ST", tap_ap["accf"][tt * 128:(tt + 1) * 128, :], scr(0))
        if "h1f" in tap_ap and tb == 0:
            for fc in range(NFC):
                p.w("ve", "VE", y_evac)
                p.w("ve", "ST", p.c("ST"))
                p.o("ve", lambda E, fc=fc: E.tensor_copy(scr(0)[:, 0:512], h1buf[:, fc * 512:(fc + 1) * 512]), inc="VE")
                p.w("sy", "VE")
                p.dma("sy", "ST", tap_ap["h1f"][fc * 128:(fc + 1) * 128, :], scr(0)[:, 0:512])
        if stage in (36, 37, 38, 39) and tb == 0:
            break
        for tc in range(4):
            tt = tb * 4 + tc
            at = acc_f32[:, tc * 1024:(tc + 1) * 1024]
            p.w("ve", "VE", y_evac)
            p.o("ve", lambda E, at=at: E.bn_stats(st6[:, 0:6], at[:, 0:512]))
            p.o("ve", lambda E, at=at: E.bn_stats(st6[:, 6:12], at[:, 512:1024]), inc="VE")
            p.w("ve", "VE")
            p.o("ve", lambda E: E.bn_aggr(mv2[:, 0:2], st6[:, 0:12]), inc="VE")
            p.w("sc", "VE")
            p.o("sc", lambda E: E.activation(mv2[:, 2:3], mv2[:, 1:2], AF.Ln, bias=ccol[:, 0:1]), inc="AC")
            p.w("sc", "AC")
            p.o("sc", lambda E: E.activation(mv2[:, 2:3], mv2[:, 2:3], AF.Exp, scale=ccol[:, 1:2]),
                inc="AC")
            p.w("ve", "AC")
            if st_cnt >= 2:
                p.w("ve", "ST", 16 * (st_cnt - 1))
            osd = out_stage(tc % 2)
            p.o("ve", lambda E, at=at, o=osd: E.tensor_scalar(o, at, mv2[:, 0:1], mv2[:, 2:3],
                                                              op0=ALU.subtract, op1=ALU.mult))
            p.o("ve", lambda E, o=osd: E.tensor_mul(o, o, g2t))
            p.o("ve", lambda E, o=osd: E.tensor_tensor(o, o, b2t, op=ALU.add))
            p.o("ve", lambda E, o=osd: E.bn_stats(st6[:, 0:6], o[:, 0:512]))
            p.o("ve", lambda E, o=osd: E.bn_stats(st6[:, 6:12], o[:, 512:1024]), inc="VE")
            p.w("ve", "VE")
            p.o("ve", lambda E: E.bn_aggr(mv2[:, 0:2], st6[:, 0:12]), inc="VE")
            p.w("sc", "VE")
            p.o("sc", lambda E: E.activation(mv2[:, 2:3], mv2[:, 1:2], AF.Ln, bias=ccol[:, 0:1]), inc="AC")
            p.w("sc", "AC")
            p.o("sc", lambda E: E.activation(mv2[:, 2:3], mv2[:, 2:3], AF.Exp, scale=ccol[:, 1:2]),
                inc="AC")
            p.w("ve", "AC")
            p.o("ve", lambda E, o=osd: E.tensor_scalar(o, o, mv2[:, 0:1], mv2[:, 2:3],
                                                       op0=ALU.subtract, op1=ALU.mult))
            xt = AR1[:, tt * D:(tt + 1) * D]
            p.o("ve", lambda E, xt=xt, o=osd: E.scalar_tensor_tensor(
                o, xt, 1.0, o, op0=ALU.mult, op1=ALU.add), inc="VE")
            odone = p.c("VE")
            p.w("sy", "VE", odone)
            p.dma("sy", "ST", y_out[tt * 128:(tt + 1) * 128, :], osd)
            st_cnt += 1
        acc_consumed[tb % 2] = p.c("VE")
        if stage == 36:
            break

    finish(p, nc, ctx)
    return nc


def finish(p, nc, ctx):
    p.w("sy", "ST", p.c("ST"))
    with nc.Block() as block:
        @block.sync
        def _(E):
            for fn in p.q["sy"]:
                fn(E)

        @block.scalar
        def _(E):
            for fn in p.q["sc"]:
                fn(E)

        @block.vector
        def _(E):
            for fn in p.q["ve"]:
                fn(E)

        @block.gpsimd
        def _(E):
            for fn in p.q["gp"]:
                fn(E)

        @block.tensor
        def _(E):
            for fn in p.q["te"]:
                fn(E)
    ctx.close()


def host_prep(inputs):
    f = np.float32
    x = np.ascontiguousarray(inputs["L0_attn_hidden_states"], dtype=f)
    w_qkv = np.asarray(inputs["L0_attn_w_qkv"], dtype=f)
    wq, wk, wv = w_qkv[0:1024], w_qkv[1024:2048], w_qkv[2048:4096]
    groups = []
    for h in range(NH):
        groups += [wq[h * 128:(h + 1) * 128].T, wk[h * 128:(h + 1) * 128].T,
                   wv[h * 256:(h + 1) * 256].T]
    wqkv_g = np.ascontiguousarray(np.concatenate(groups, axis=1))
    wgat_t = np.ascontiguousarray(np.asarray(inputs["L0_attn_w_gated"], dtype=f).T)
    wprj_t = np.ascontiguousarray(np.asarray(inputs["L0_attn_w_proj"], dtype=f).T)
    decay_r = np.asarray(inputs["L0_attn_decay"], dtype=f).reshape(1, NH)
    Dm, SCALE_BASE = 1024, 512.0
    half = Dm // 2
    sv = (np.arange(0, Dm, 2, dtype=np.float64) + 0.4 * Dm) / (1.4 * Dm)
    pos = np.arange(T, dtype=np.float64)
    scale = sv[None, :] ** (pos[:, None] / SCALE_BASE)
    inv_freq = 1.0 / 10000.0 ** (np.arange(half, dtype=np.float64) / half)
    sinus = pos[:, None] * inv_freq[None, :]
    sin, cos = np.sin(sinus), np.cos(sinus)
    Cq = (cos * scale).astype(f)
    Sq = (sin * scale).astype(f)
    Ck = (cos / scale).astype(f)
    Sk = (sin / scale).astype(f)
    tab = np.empty((NH, NT, 128, 256), dtype=f)
    for h in range(NH):
        cols = slice(h * 64, (h + 1) * 64)
        for tt in range(NT):
            rows = slice(tt * 128, (tt + 1) * 128)
            tab[h, tt, :, 0:64] = Cq[rows, cols]
            tab[h, tt, :, 64:128] = Sq[rows, cols]
            tab[h, tt, :, 128:192] = Ck[rows, cols]
            tab[h, tt, :, 192:256] = Sk[rows, cols]
    umask = np.triu(np.ones((128, 128), dtype=f)).copy()
    ipw = np.ascontiguousarray(np.asarray(inputs["L0_ffn_in_proj_w"], dtype=f).T)
    ipb = np.asarray(inputs["L0_ffn_in_proj_b"], dtype=f).reshape(1, 3072)
    outw = np.ascontiguousarray(np.asarray(inputs["L0_ffn_out_w"], dtype=f).T)
    outb = np.asarray(inputs["L0_ffn_out_b"], dtype=f).reshape(1, D)
    gatew = np.ascontiguousarray(np.asarray(inputs["L0_ffn_gate_w"], dtype=f).T)
    gateb = np.asarray(inputs["L0_ffn_gate_b"], dtype=f).reshape(1, NE)
    w1 = np.asarray(inputs["L0_ffn_w1"], dtype=f)
    w1pre = np.ascontiguousarray(
        w1.reshape(NE, NFC, 128, NIC, 128).transpose(0, 1, 4, 3, 2).reshape(NE, NFC * 128, NIC * 128))
    b1 = np.asarray(inputs["L0_ffn_b1"], dtype=f)
    b1c = np.ascontiguousarray(b1.reshape(NE, NFC, 128).transpose(2, 0, 1).reshape(128, NE * NFC))
    w2 = np.asarray(inputs["L0_ffn_w2"], dtype=f)
    w2pre = np.ascontiguousarray(w2.transpose(0, 2, 1))
    b2r = np.asarray(inputs["L0_ffn_b2"], dtype=f).reshape(1, NE * D)
    lng = np.empty((128, 4 * D), dtype=f)
    lng[:, 0:D] = np.asarray(inputs["L0_ffn_ln1_g"], dtype=f)[None, :]
    lng[:, D:2 * D] = np.asarray(inputs["L0_ffn_ln1_b"], dtype=f)[None, :]
    lng[:, 2 * D:3 * D] = np.asarray(inputs["L0_ffn_ln2_g"], dtype=f)[None, :]
    lng[:, 3 * D:4 * D] = np.asarray(inputs["L0_ffn_ln2_b"], dtype=f)[None, :]
    base = {
        "wqkv_in": wqkv_g, "wgat_in": wgat_t, "wprj_in": wprj_t, "decay_r": decay_r,
        "tab_in": tab, "umask_in": umask,
        "ipw_in": ipw, "outw_in": outw,
        "gatew_in": gatew, "gateb_in": gateb,
        "bias_in": np.ascontiguousarray(np.concatenate([ipb, outb, b2r], axis=1)),
        "w1pre_in": w1pre, "b1c_in": b1c, "w2pre_in": w2pre,
        "lng_in": np.ascontiguousarray(lng),
    }
    in_maps = []
    for c in range(NCORES):
        m = dict(base)
        m["x_in"] = np.ascontiguousarray(x[c])
        j = np.arange(T)
        gidx_v = ((j // 256) * T + c * 256 + (j % 256)).astype(np.int32)
        m["gidx_in"] = np.ascontiguousarray(gidx_v.reshape(NT, 128).T)
        in_maps.append(m)
    return in_maps


def kernel(**inputs):
    in_maps = host_prep(inputs)
    nc = build_nc(stage=99)
    res = run_bass_kernel_spmd(nc, in_maps, core_ids=list(range(NCORES)))
    out = np.empty((NCORES, T, D), dtype=np.float32)
    for c in range(NCORES):
        y = np.asarray(res.results[c]["y_out"])
        out[:, c * 256:(c + 1) * 256, :] = y.reshape(NCORES, 256, D)
    return out


# version 05
# speedup vs baseline: 1.0025x; 1.0025x over previous
import numpy as np
import contextlib
import concourse.bass as bass
import concourse.mybir as mybir
from concourse.bass_utils import run_bass_kernel_spmd
from concourse.bass_core_rust import IndirectOffsetOnAxis

F32 = mybir.dt.float32
BF16 = mybir.dt.bfloat16
I32 = mybir.dt.int32
AF = mybir.ActivationFunctionType
ALU = mybir.AluOpType
AX = mybir.AxisListType

T = 2048
D = 1024
NT = 16
NH = 8
HDQ, HDV = 128, 256
NIC = 8
EPS = 1e-5
LNEG = -30000.0
NCORES = 8
FF = 4096
NE = 8
NFC = 32

ENGS = ("sy", "sc", "ve", "gp", "te")


class Prog:
    def __init__(self, nc, ctx):
        self.nc = nc
        self.ctx = ctx
        self.q = {e: [] for e in ENGS}
        self.sems = {}
        self.cnt = {}

    def sem(self, *names):
        for n in names:
            self.sems[n] = self.ctx.enter_context(self.nc.semaphore(n))
            self.cnt[n] = 0

    def w(self, eng, sem, val=None):
        v = self.cnt[sem] if val is None else val
        if v <= 0:
            return
        h = self.sems[sem]
        self.q[eng].append(lambda E, h=h, v=v: E.wait_ge(h, v))

    def o(self, eng, fn, inc=None, n=1):
        if inc is not None:
            self.cnt[inc] += n
            h = self.sems[inc]
            self.q[eng].append(lambda E, fn=fn, h=h, n=n: fn(E).then_inc(h, n))
            return self.cnt[inc]
        self.q[eng].append(fn)
        return None

    def dma(self, eng, sem, out, in_, **kw):
        return self.o(eng, lambda E, out=out, in_=in_, kw=kw: E.dma_start(out=out, in_=in_, **kw),
                      inc=sem, n=16)

    def dmat(self, eng, sem, out, in_):
        return self.o(eng, lambda E, out=out, in_=in_: E.dma_start(out=out, in_=in_, transpose=True),
                      inc=sem, n=16)

    def c(self, sem):
        return self.cnt[sem]


def build_nc(stage=99, taps=(), n_heads=NH, stop_qkv=False, no_moe_cast=False):
    nc = bass.Bass("TRN2", target_bir_lowering=False, debug=False,
                   detect_race_conditions=False)

    dram = lambda n, sh, dt, kind: nc.dram_tensor(n, sh, dt, kind=kind).ap()
    x_in = dram("x_in", [T, D], F32, "ExternalInput")
    wqkv_in = dram("wqkv_in", [D, 4096], F32, "ExternalInput")
    wgat_in = dram("wgat_in", [D, 2048], F32, "ExternalInput")
    wprj_in = dram("wprj_in", [2048, D], F32, "ExternalInput")
    decay_r = dram("decay_r", [1, NH], F32, "ExternalInput")
    tab_in = dram("tab_in", [NH, NT, 128, 256], F32, "ExternalInput")
    umask_in = dram("umask_in", [128, 128], F32, "ExternalInput")
    gidx_in = dram("gidx_in", [128, NT], I32, "ExternalInput")
    ipw_in = dram("ipw_in", [D, 3072], F32, "ExternalInput")
    outw_in = dram("outw_in", [D, D], F32, "ExternalInput")
    gatew_in = dram("gatew_in", [D, NE], F32, "ExternalInput")
    gateb_in = dram("gateb_in", [1, NE], F32, "ExternalInput")
    bias_in = dram("bias_in", [1, 3072 + D + NE * D], F32, "ExternalInput")
    w1pre_in = dram("w1pre_in", [NE, NFC * 128, NIC * 128], F32, "ExternalInput")
    b1c_in = dram("b1c_in", [128, NE * NFC], F32, "ExternalInput")
    w2pre_in = dram("w2pre_in", [NE, FF, D], F32, "ExternalInput")
    lng_in = dram("lng_in", [128, 4 * D], F32, "ExternalInput")

    y_out = dram("y_out", [T, D], F32, "ExternalOutput")
    tap_ap = {}
    for tn in taps:
        shapes = {"hfm": [D, T], "x1": [T, D], "xmid": [T, D], "qk": [T, 2048],
                  "ctx": [T, D], "comb": [T, NE], "x1b": [T, D], "gnfm": [2048, T], "accf": [T, D], "h1f": [FF, 512],
                  "qkfm": [2 * HDQ + HDV, T], "xlnfm": [D, T]}
        tap_ap[tn] = dram("tap_" + tn, shapes[tn], F32, "ExternalOutput")

    ag_in = nc.dram_tensor("ag_in", [T, D], BF16)
    ag_out = nc.dram_tensor("ag_out", [NCORES * T, D], BF16)
    bias_c = nc.dram_tensor("bias_c", [1, 3072 + D + NE * D], BF16)
    wqkv_c = nc.dram_tensor("wqkv_c", [D, 4096], BF16)
    wgat_c = nc.dram_tensor("wgat_c", [D, 2048], BF16)
    wprj_c = nc.dram_tensor("wprj_c", [2048, D], BF16)
    ipw_c = nc.dram_tensor("ipw_c", [D, 3072], BF16)
    outw_c = nc.dram_tensor("outw_c", [D, D], BF16)
    w1_c = nc.dram_tensor("w1_c", [NE, NFC * 128, NIC * 128], BF16)
    w2_c = nc.dram_tensor("w2_c", [NE, FF, D], BF16)

    ctx = contextlib.ExitStack()
    SB = lambda n, cols, dt: ctx.enter_context(nc.sbuf_tensor(n, [128, cols], dt))
    AR1 = SB("AR1", 16384, BF16)
    AR2 = SB("AR2", 16384, BF16)
    AR3 = SB("AR3", 32768, BF16)
    AR4 = SB("AR4", 16384, BF16)
    AR6 = SB("AR6", 3072, F32)
    AR7 = SB("AR7", 12288, BF16)
    umask = SB("umask", 128, F32)
    tcol = SB("tcol", NT, F32)
    qdec = SB("qdec", NT, F32)
    kdec = SB("kdec", NT, F32)
    ldb = SB("ldb", NH, F32)
    nldb = SB("nldb", NH, F32)
    gidx = SB("gidx", NT, I32)
    xstat = SB("xstat", 2 * NT, F32)
    st6 = SB("st6", 12, F32)
    mv2 = SB("mv2", 8, F32)
    b1c = SB("b1c", NE * NFC, F32)
    comb = SB("comb", NT * NE, F32)
    gwsb = SB("gwsb", NIC * NE, BF16)
    attw = SB("attw", NH * 256, BF16)
    mqr = SB("mqr", 8, F32)
    ccol = SB("ccol", 4, F32)
    decs = ctx.enter_context(nc.sbuf_tensor("decs", [1, NH], F32))
    ones_f = ctx.enter_context(nc.sbuf_tensor("ones_f", [1, 128], F32))
    ones_b = ctx.enter_context(nc.sbuf_tensor("ones_b", [1, 128], BF16))
    gateb_r = ctx.enter_context(nc.sbuf_tensor("gateb_r", [1, NE], BF16))
    brow = ctx.enter_context(nc.sbuf_tensor("brow", [1, 2048], BF16))

    PS = [ctx.enter_context(nc.psum_tensor("ps%d" % i, [128, 512], F32)) for i in range(8)]

    p = Prog(nc, ctx)
    p.sem("SY", "SC", "GP", "PE", "VE", "AC", "PO", "CC", "ST",
          "TB0", "TB1", "WQ", "IP0", "IP1", "W10", "W11", "W12", "W13", "W20", "W21", "W22", "W23", "MC", "GPI")

    A4_V, A4_QFM, A4_KFM, A4_QKVW, A4_GATEW, A4_STB, A4_QKS = \
        0, 4096, 6144, 8192, 12288, 14336, 15360
    A6_TAB, A6_SCR, A6_OFM = 0, 512, 2560
    A7_OTM, A7_GT, A7_GN, A7_HST = 0, 1024, 2048, 4096
    B7_IPW, B7_LN = 0, 8192
    M7_W1, M7_W2 = 0, 4096

    scr = lambda i: AR6[:, A6_SCR + i * 1024: A6_SCR + (i + 1) * 1024]
    bf6 = AR6[:, :].bitcast(BF16)
    f32_AR3 = AR3[:, :].bitcast(F32)
    tabs = lambda par: AR6[:, A6_TAB + par * 256: A6_TAB + (par + 1) * 256]
    hfm = lambda ic: AR2[:, ic * T:(ic + 1) * T]
    g1t = AR7[:, B7_LN + 0 * D: B7_LN + 1 * D]
    b1t = AR7[:, B7_LN + 1 * D: B7_LN + 2 * D]
    g2t = AR7[:, B7_LN + 2 * D: B7_LN + 3 * D]
    b2t = AR7[:, B7_LN + 3 * D: B7_LN + 4 * D]

    p.dma("sy", "SY", umask[:, :], umask_in)
    p.dma("sy", "SY", gidx[:, :], gidx_in)
    p.dma("sy", "SY", decs[:, :], decay_r)
    p.dma("sy", "SY", b1c[:, :], b1c_in)
    misc_loaded = p.c("SY")

    for tt in range(NT):
        p.dma("gp", "GP", AR1[:, tt * D:(tt + 1) * D], x_in[tt * 128:(tt + 1) * 128, :])
    x_loaded = p.c("GP")
    p.dma("gp", "GP", gateb_r[:, :], gateb_in)
    p.dma("gp", "GP", bias_c.ap(), bias_in)
    p.dma("gp", "GP", gwsb[:, :].rearrange("p (ic e) -> p ic e", e=NE), gatew_in.rearrange("(ic p) e -> p ic e", p=128))
    p.dma("gp", "GP", AR7[:, B7_LN:B7_LN + 4 * D], lng_in)
    small_loaded = p.c("GP")
    p.dma("gp", "MC", wqkv_c.ap(), wqkv_in)
    p.dma("gp", "MC", wgat_c.ap(), wgat_in)
    qkv_cast = p.c("MC")
    p.dma("gp", "MC", ipw_c.ap(), ipw_in)
    p.dma("gp", "MC", outw_c.ap(), outw_in)
    p.dma("gp", "MC", wprj_c.ap(), wprj_in)
    misc_cast = p.c("MC")

    p.o("ve", lambda E: E.memset(ones_f[:, :], 1.0))
    p.o("ve", lambda E: E.memset(ccol[:, 0:1], EPS))
    p.o("ve", lambda E: E.memset(ccol[:, 1:2], -0.5))
    p.o("ve", lambda E: E.memset(ccol[:, 2:3], float(np.log(128.0 ** -0.5))))
    p.o("ve", lambda E: E.memset(ones_b[:, :], 1.0), inc="VE")
    ones_ready = p.c("VE")
    p.o("gp", lambda E: E.iota(tcol[:, :], pattern=[[128, NT]], base=0, channel_multiplier=1,
                               allow_small_or_imprecise_dtypes=True), inc="PO")
    iota_done = p.c("PO")
    p.w("sc", "SY", misc_loaded)
    p.o("sc", lambda E: E.activation(decs[:, :], decs[:, :], AF.Ln), inc="AC")
    ln_dec = p.c("AC")
    p.w("te", "AC", ln_dec)
    p.w("te", "VE", ones_ready)
    p.o("te", lambda E: E.matmul(PS[6][:, 0:NH], ones_f[:, :], decs[:, :],
                                 start=True, stop=True), inc="PE")
    p.w("ve", "PE")
    p.o("ve", lambda E: E.tensor_copy(ldb[:, :], PS[6][:, 0:NH]), inc="VE")
    p.w("ve", "VE")
    p.o("ve", lambda E: E.tensor_scalar_mul(nldb[:, :], ldb[:, :], -1.0), inc="VE")
    ldb_ready = p.c("VE")
    p.w("sc", "VE", ldb_ready)
    p.w("sc", "PO", iota_done)
    dec_ready = p.c("AC")

    qkvw_early = AR4[:, 8192:8192 + 4096]
    gatew_early = AR4[:, 12288:12288 + 2048]
    p.w("sy", "MC", qkv_cast)
    p.dma("sy", "WQ", qkvw_early[:, :].rearrange("p (ic c) -> p ic c", c=512),
          wqkv_c.ap().rearrange("(ic p) o -> p ic o", p=128)[:, :, 0:512])
    p.dma("sy", "WQ", gatew_early[:, :].rearrange("p (ic c) -> p ic c", c=256),
          wgat_c.ap().rearrange("(ic p) o -> p ic o", p=128)[:, :, 0:256])
    early_wq = p.c("WQ")
    p.dma("sy", "TB0", AR6[:, 0:256], tab_in[0, 0, :, :])
    p.dma("sy", "TB1", AR6[:, 256:512], tab_in[0, 1, :, :])
    early_tb0, early_tb1 = p.c("TB0"), p.c("TB1")
    hst = lambda par: AR7[:, A7_HST + par * D: A7_HST + (par + 1) * D]
    hst_tp = {}
    hfm_tile_tp = {}
    p.w("ve", "GP", x_loaded)
    for tt in range(NT):
        xt = AR1[:, tt * D:(tt + 1) * D]
        p.o("ve", lambda E, xt=xt: E.bn_stats(st6[:, 0:6], xt[:, 0:512]))
        p.o("ve", lambda E, xt=xt: E.bn_stats(st6[:, 6:12], xt[:, 512:1024]), inc="VE")
        p.w("ve", "VE")
        p.o("ve", lambda E, tt=tt: E.bn_aggr(xstat[:, tt * 2: tt * 2 + 2], st6[:, 0:12]),
            inc="VE" if tt == NT - 1 else None)
    xstats_done = p.c("VE")
    p.w("sc", "VE", xstats_done)
    for tt in range(NT):
        p.o("sc", lambda E, tt=tt: E.activation(xstat[:, tt * 2 + 1: tt * 2 + 2],
                                                xstat[:, tt * 2 + 1: tt * 2 + 2], AF.Ln,
                                                bias=ccol[:, 0:1]), inc="AC")
    p.w("sc", "AC")
    for tt in range(NT):
        p.o("sc", lambda E, tt=tt: E.activation(xstat[:, tt * 2 + 1: tt * 2 + 2],
                                                xstat[:, tt * 2 + 1: tt * 2 + 2], AF.Exp,
                                                scale=ccol[:, 1:2]),
            inc="AC" if tt == NT - 1 else None)
    xrstd_done = p.c("AC")
    p.w("ve", "AC", xrstd_done)
    for tt in range(NT):
        xt = AR1[:, tt * D:(tt + 1) * D]
        if tt >= 2:
            s_, v_ = hst_tp[tt % 2]
            p.w("ve", s_, v_)
        p.o("ve", lambda E, xt=xt, tt=tt: E.tensor_scalar(
            hst(tt % 2)[:, :], xt, xstat[:, tt * 2: tt * 2 + 1],
            xstat[:, tt * 2 + 1: tt * 2 + 2], op0=ALU.subtract, op1=ALU.mult), inc="VE")
        hdone = p.c("VE")
        qeng, qsem = "sy", "SY"
        p.w(qeng, "VE", hdone)
        for ic in range(NIC):
            p.dmat(qeng, qsem, AR2[:, ic * T + tt * 128: ic * T + (tt + 1) * 128],
                   hst(tt % 2)[:, ic * 128:(ic + 1) * 128])
        hst_tp[tt % 2] = (qsem, p.c(qsem))
        hfm_tile_tp[tt] = p.c("SY")
    hfm_sy, hfm_sc = p.c("SY"), p.c("SC")
    moe_cast = []
    p.w("gp", "SY", hfm_sy)
    for e in range(NE):
        if not no_moe_cast:
            p.dma("gp", "MC", w1_c.ap()[e, :, :], w1pre_in[e, :, :])
            p.dma("gp", "MC", w2_c.ap()[e, :, :], w2pre_in[e, :, :])
        moe_cast.append(p.c("MC"))

    def tap_tm(name, src_fn, gate_sem, gate_val, width=D):
        if name not in tap_ap:
            return
        for tt in range(NT):
            p.w("ve", gate_sem, gate_val)
            p.w("ve", "ST", p.c("ST"))
            p.o("ve", lambda E, tt=tt: E.tensor_copy(scr(0)[:, 0:width], src_fn(tt)), inc="VE")
            p.w("sy", "VE")
            p.dma("sy", "ST", tap_ap[name][tt * 128:(tt + 1) * 128, 0:width], scr(0)[:, 0:width])

    if "hfm" in tap_ap:
        for ic in range(NIC):
            for half in range(2):
                p.w("ve", "SY", hfm_sy)
                p.w("ve", "SC", hfm_sc)
                p.w("ve", "ST", p.c("ST"))
                p.o("ve", lambda E, ic=ic, half=half: E.tensor_copy(
                    scr(0), hfm(ic)[:, half * 1024:(half + 1) * 1024]), inc="VE")
                p.w("sy", "VE")
                p.dma("sy", "ST", tap_ap["hfm"][ic * 128:(ic + 1) * 128,
                                                half * 1024:(half + 1) * 1024], scr(0))
    if stage <= 1:
        finish(p, nc, ctx)
        return nc

    vh = AR4[:, A4_V:A4_V + NT * HDV]
    qfm = AR4[:, A4_QFM:A4_QFM + T]
    kfm = AR4[:, A4_KFM:A4_KFM + T]
    qkvw = AR4[:, A4_QKVW:A4_QKVW + 4096]
    gatew = AR4[:, A4_GATEW:A4_GATEW + 2048]
    stb = lambda par: AR4[:, A4_STB + par * 512: A4_STB + (par + 1) * 512]
    qks = lambda par: AR4[:, A4_QKS + par * 512: A4_QKS + (par + 1) * 512]
    otm_st = AR7[:, A7_OTM:A7_OTM + 1024]
    gt_st = AR7[:, A7_GT:A7_GT + 1024]
    gn_st = lambda par: AR7[:, A7_GN + par * 1024: A7_GN + (par + 1) * 1024]

    tabv = {(0, 0): early_tb0, (0, 1): early_tb1}

    def tab_load(h, tt):
        sem = "TB0" if tt % 2 == 0 else "TB1"
        p.dma("sy", sem, tabs(tt % 2)[:, :], tab_in[h, tt, :, :])
        tabv[(h, tt)] = p.c(sem)

    head_done_pe = 0
    xpos_done_ve = 0
    xpos_done_po = 0
    out_evac = {}
    gate_ac = {}
    ofm_tp = {}
    gn_done = {}
    gn_tp = {}
    prev_head_out_evac = None
    prev_head_gn_done = None

    for h in range(n_heads):
        if h > 0:
            p.w("sy", "PE", head_done_pe)
            p.dma("sy", "WQ", gatew[:, :].rearrange("p (ic c) -> p ic c", c=256),
                  wgat_c.ap().rearrange("(ic p) o -> p ic o", p=128)[:, :, h * 256:(h + 1) * 256])
        wq_ready = p.c("WQ")
        p.o("sc", lambda E, h=h: E.activation(qdec[:, :], tcol[:, :], AF.Exp,
                                              bias=ccol[:, 2:3],
                                              scale=ldb[:, h:h + 1]))
        p.o("sc", lambda E, h=h: E.activation(kdec[:, :], tcol[:, :], AF.Exp,
                                              scale=nldb[:, h:h + 1]), inc="AC")

        evac_ac = {}
        tpq = {}
        for tt in range(NT):
            bank = PS[tt % 2]
            p.w("te", "WQ", wq_ready)
            if h == 0:
                p.w("te", "SY", hfm_tile_tp[tt])
            if h == 0 and tt == 0:
                p.w("te", "AC", dec_ready)
            if tt >= 2:
                p.w("te", "AC", evac_ac[tt - 2])
            elif h > 0:
                p.w("te", "AC", prev_evac_last[tt])
            for ic in range(NIC):
                p.o("te", lambda E, b=bank, ic=ic, tt=tt: E.matmul(
                    b[:, :], hfm(ic)[:, tt * 128:(tt + 1) * 128],
                    qkvw[:, ic * 512:(ic + 1) * 512],
                    start=(ic == 0), stop=(ic == NIC - 1)),
                    inc="PE" if ic == NIC - 1 else None)
            mm_done = p.c("PE")
            p.w("sc", "PE", mm_done)
            if tt == 0:
                p.w("sc", "AC", p.c("AC"))
            if tt >= 2 or h > 0:
                p.w("sc", "VE", xpos_done_ve)
                p.w("sc", "PO", xpos_done_po)
            sp = scr(tt % 2)
            p.o("sc", lambda E, b=bank, sp=sp, h=h, tt=tt: E.mul(
                sp[:, 0:128], b[:, 0:128], qdec[:, tt:tt + 1]))
            p.o("sc", lambda E, b=bank, sp=sp, h=h, tt=tt: E.mul(
                sp[:, 128:256], b[:, 128:256], kdec[:, tt:tt + 1]))
            p.o("sc", lambda E, b=bank, tt=tt: E.copy(vh[:, tt * HDV:(tt + 1) * HDV],
                                                      b[:, 256:512]), inc="AC")
            evac_ac[tt] = p.c("AC")
            eng = "ve" if (tt % 3 != 2) else "gp"
            esem = "VE" if eng == "ve" else "PO"
            p.w(eng, "AC", evac_ac[tt])
            p.w(eng, "TB0" if tt % 2 == 0 else "TB1", tabv[(h, tt)])
            if tt >= 2:
                s_, v_ = tpq[tt % 2]
                p.w(eng, s_, v_)
            elif h > 0:
                p.w(eng, "SY", prev_tpq_sy)
                p.w(eng, "SC", prev_tpq_sc)
            tb = tabs(tt % 2)
            stg = qks(tt % 2)
            for o_q in (0, 128):
                base_t = 0 if o_q == 0 else 128
                se = sp[:, o_q:o_q + 128][:, 0::2]
                so = sp[:, o_q:o_q + 128][:, 1::2]
                Ct = tb[:, base_t: base_t + 64]
                St_ = tb[:, base_t + 64: base_t + 128]
                t0 = sp[:, 512:576]
                t1 = sp[:, 576:640]
                de = stg[:, o_q:o_q + 128][:, 0::2]
                do = stg[:, o_q:o_q + 128][:, 1::2]
                p.o(eng, lambda E, a=t0, b_=so, c=St_: E.tensor_mul(a, b_, c))
                p.o(eng, lambda E, a=t1, b_=se, c=Ct: E.tensor_mul(a, b_, c), inc=esem)
                p.w(eng, esem)
                p.o(eng, lambda E, a=de, b_=t1, c=t0: E.tensor_sub(a, b_, c))
                p.o(eng, lambda E, a=t0, b_=se, c=St_: E.tensor_mul(a, b_, c))
                p.o(eng, lambda E, a=t1, b_=so, c=Ct: E.tensor_mul(a, b_, c), inc=esem)
                p.w(eng, esem)
                p.o(eng, lambda E, a=do, b_=t1, c=t0: E.tensor_add(a, b_, c),
                    inc=esem)
            if eng == "ve":
                xpos_done_ve = p.c("VE")
            else:
                xpos_done_po = p.c("PO")
            p.w("sy", esem, p.c(esem))
            if tt + 2 < NT:
                tab_load(h, tt + 2)
            elif h + 1 < NH:
                tab_load(h + 1, tt + 2 - NT)
            qeng, qsem = "sy", "SY"
            p.w(qeng, esem, p.c(esem))
            p.dmat(qeng, qsem, qfm[:, tt * 128:(tt + 1) * 128], stg[:, 0:128])
            p.dmat(qeng, qsem, kfm[:, tt * 128:(tt + 1) * 128], stg[:, 128:256])
            tpq[tt % 2] = (qsem, p.c(qsem))
        prev_evac_last = {0: evac_ac[14], 1: evac_ac[15]}
        prev_tpq_sy, prev_tpq_sc = p.c("SY"), p.c("SC")
        qkv_tp_sy, qkv_tp_sc = p.c("SY"), p.c("SC")
        if h + 1 < n_heads:
            p.w("sy", "PE", p.c("PE"))
            p.dma("sy", "WQ", qkvw[:, :].rearrange("p (ic c) -> p ic c", c=512),
                  wqkv_c.ap().rearrange("(ic p) o -> p ic o", p=128)[:, :, (h + 1) * 512:(h + 2) * 512])
        if "qkfm" in tap_ap and h == n_heads - 1:
            for row, srcv in ((0, qfm), (1, kfm)):
                for half in range(2):
                    p.w("ve", "SY", qkv_tp_sy)
                    p.w("ve", "SC", qkv_tp_sc)
                    p.w("ve", "ST", p.c("ST"))
                    p.o("ve", lambda E, s=srcv, half=half: E.tensor_copy(
                        scr(0), s[:, half * 1024:(half + 1) * 1024]), inc="VE")
                    p.w("sy", "VE")
                    p.dma("sy", "ST", tap_ap["qkfm"][row * 128:(row + 1) * 128,
                                                     half * 1024:(half + 1) * 1024], scr(0))
            for tt4 in range(4):
                p.w("ve", "AC", evac_ac[15])
                p.w("ve", "ST", p.c("ST"))
                p.o("ve", lambda E, tt4=tt4: E.tensor_copy(
                    scr(0), vh[:, tt4 * 1024:(tt4 + 1) * 1024]), inc="VE")
                p.w("sy", "VE")
                p.dma("sy", "ST", tap_ap["qkfm"][256:384, tt4 * 512:(tt4 + 1) * 512],
                      scr(0)[:, 0:512])
                p.w("ve", "ST", p.c("ST"))
                p.o("ve", lambda E: E.engine_nop(), inc="VE")
                p.w("sy", "VE")
                p.dma("sy", "ST", tap_ap["qkfm"][384:512, tt4 * 512:(tt4 + 1) * 512],
                      scr(0)[:, 512:1024])

        if stop_qkv:
            finish(p, nc, ctx)
            return nc
        p.w("te", "SY", qkv_tp_sy)
        p.w("te", "SC", qkv_tp_sc)
        p.w("te", "AC", evac_ac[15])
        st_evac = {}
        sv_pe = {}

        for qb in range(4):
            nkb = 4 * qb + 4

            def emit_st(kb, qb=qb, nkb=nkb):
                bank = PS[2 + (kb % 2)]
                if kb >= 2:
                    s_, v_ = st_evac[(qb, kb - 2)]
                    p.w("te", s_, v_)
                elif qb > 0:
                    pk = 4 * qb
                    s_, v_ = st_evac[(qb - 1, pk - 2 + kb)]
                    p.w("te", s_, v_)
                p.o("te", lambda E, b=bank, kb=kb, qb=qb: E.matmul(
                    b[:, :], kfm[:, kb * 128:(kb + 1) * 128], qfm[:, qb * 512:(qb + 1) * 512],
                    start=True, stop=True), inc="PE")
                st_pe = p.c("PE")
                boundary = kb >= 4 * qb
                eng2 = "ve" if (boundary or kb % 2 == 0) else "sc"
                sem2 = "VE" if eng2 == "ve" else "AC"
                p.w(eng2, "PE", st_pe)
                if (qb, kb - 2) in sv_pe:
                    p.w(eng2, "PE", sv_pe[(qb, kb - 2)])
                elif qb > 0 and kb < 2:
                    p.w(eng2, "PE", sv_pe[(qb - 1, 4 * qb - 2 + kb)])
                dstb = stb(kb % 2)
                if not boundary:
                    if eng2 == "ve":
                        p.o(eng2, lambda E, d=dstb, b=bank: E.tensor_copy(d[:, :], b[:, :]),
                            inc=sem2)
                    else:
                        p.o(eng2, lambda E, d=dstb, b=bank: E.copy(d[:, :], b[:, :]), inc=sem2)
                else:
                    cc_ = kb - 4 * qb
                    if cc_ > 0:
                        p.o(eng2, lambda E, d=dstb, cc_=cc_: E.memset(d[:, 0:cc_ * 128], 0.0))
                    p.o(eng2, lambda E, d=dstb, b=bank, cc_=cc_: E.tensor_mul(
                        d[:, cc_ * 128:(cc_ + 1) * 128], b[:, cc_ * 128:(cc_ + 1) * 128],
                        umask[:, :]), inc=sem2 if cc_ == 3 else None)
                    if cc_ < 3:
                        p.o(eng2, lambda E, d=dstb, b=bank, cc_=cc_: E.tensor_copy(
                            d[:, (cc_ + 1) * 128:512], b[:, (cc_ + 1) * 128:512]), inc=sem2)
                st_evac[(qb, kb)] = (sem2, p.c(sem2))

            def emit_sv(kb, qb=qb, nkb=nkb):
                s_, v_ = st_evac[(qb, kb)]
                p.w("te", s_, v_)
                if kb == 0 and qb > 0:
                    ev, ea = out_evac[qb - 1]
                    p.w("te", "VE", ev)
                    p.w("te", "AC", ea)
                elif kb == 0 and prev_head_out_evac is not None:
                    ev, ea = prev_head_out_evac
                    p.w("te", "VE", ev)
                    p.w("te", "AC", ea)
                for e in range(2):
                    p.o("te", lambda E, kb=kb, e=e, nkb=nkb: E.matmul(
                        PS[4 + e][:, :], vh[:, kb * HDV + e * 128: kb * HDV + e * 128 + 128],
                        stb(kb % 2)[:, :],
                        start=(kb == 0), stop=(kb == nkb - 1)),
                        inc="PE" if e == 1 else None)
                sv_pe[(qb, kb)] = p.c("PE")

            emit_st(0)
            if nkb > 1:
                emit_st(1)
            for kb in range(nkb):
                emit_sv(kb)
                if kb + 2 < nkb:
                    emit_st(kb + 2)
            ofm_stage = bf6[:, 2 * A6_OFM: 2 * A6_OFM + 1024]
            last_pe = sv_pe[(qb, nkb - 1)]
            if qb >= 1:
                s_, v_ = ofm_tp[(qb - 1) % 2]
                p.w("ve", s_, v_)
                p.w("sc", s_, v_)
            elif 3 % 2 in ofm_tp and h > 0:
                s_, v_ = ofm_tp[3 % 2]
                p.w("ve", s_, v_)
                p.w("sc", s_, v_)
            p.w("ve", "PE", last_pe)
            p.o("ve", lambda E, o=ofm_stage: E.tensor_copy(o[:, 0:512], PS[4][:, :]), inc="VE")
            p.w("sc", "PE", last_pe)
            p.o("sc", lambda E, o=ofm_stage: E.copy(o[:, 512:1024], PS[5][:, :]), inc="AC")
            out_evac[qb] = (p.c("VE"), p.c("AC"))
            qeng, qsem = "sy", "SY"
            p.w(qeng, "VE", out_evac[qb][0])
            p.w(qeng, "AC", out_evac[qb][1])
            if qb >= 1:
                p.w(qeng, "VE", gn_done[qb - 1])
            elif prev_head_gn_done is not None:
                p.w(qeng, "VE", prev_head_gn_done)
            for tl in range(4):
                for e in range(2):
                    p.dmat(qeng, qsem, otm_st[:, tl * 256 + e * 128: tl * 256 + e * 128 + 128],
                           ofm_stage[:, e * 512 + tl * 128: e * 512 + tl * 128 + 128])
            ofm_tp[qb % 2] = (qsem, p.c(qsem))
            otm_ready = (qsem, p.c(qsem))

            for tl in range(4):
                tt = 4 * qb + tl
                bank = PS[tl % 2]
                if qb == 0 and tl < 2:
                    p.w("te", "AC", evac_ac[14 + tl])
                elif tl < 2:
                    p.w("te", "AC", gate_ac[(qb - 1, 2 + tl)])
                else:
                    p.w("te", "AC", gate_ac[(qb, tl - 2)])
                for ic in range(NIC):
                    p.o("te", lambda E, b=bank, ic=ic, tt=tt: E.matmul(
                        b[:, 0:256], hfm(ic)[:, tt * 128:(tt + 1) * 128],
                        gatew[:, ic * 256:(ic + 1) * 256],
                        start=(ic == 0), stop=(ic == NIC - 1)),
                        inc="PE" if ic == NIC - 1 else None)
                gmm = p.c("PE")
                p.w("sc", "PE", gmm)
                if qb >= 1:
                    p.w("sc", "VE", gn_done[qb - 1])
                elif prev_head_gn_done is not None:
                    p.w("sc", "VE", prev_head_gn_done)
                p.o("sc", lambda E, b=bank, tl=tl: E.activation(
                    gt_st[:, tl * 256:(tl + 1) * 256], b[:, 0:256], AF.Silu), inc="AC")
                gate_ac[(qb, tl)] = p.c("AC")

            p.w("ve", otm_ready[0], otm_ready[1])
            for tl in range(4):
                ot = otm_st[:, tl * 256:(tl + 1) * 256]
                p.o("ve", lambda E, ot=ot: E.bn_stats(st6[:, 0:6], ot), inc="VE")
                p.w("ve", "VE")
                p.o("ve", lambda E, tl=tl: E.bn_aggr(mqr[:, tl * 2: tl * 2 + 2], st6[:, 0:6]),
                    inc="VE")
                p.w("sc", "VE")
                p.o("sc", lambda E, tl=tl: E.activation(mqr[:, tl * 2 + 1: tl * 2 + 2],
                                                        mqr[:, tl * 2 + 1: tl * 2 + 2],
                                                        AF.Ln, bias=ccol[:, 0:1]), inc="AC")
                p.w("sc", "AC")
                p.o("sc", lambda E, tl=tl: E.activation(mqr[:, tl * 2 + 1: tl * 2 + 2],
                                                        mqr[:, tl * 2 + 1: tl * 2 + 2],
                                                        AF.Exp, scale=ccol[:, 1:2]), inc="AC")
                p.w("ve", "AC")
                p.w("ve", "AC", gate_ac[(qb, tl)])
                gnp = gn_st(qb % 2)
                if tl == 0 and (qb % 2) in gn_tp and (qb >= 2 or h > 0):
                    s_, v_ = gn_tp[qb % 2]
                    p.w("ve", s_, v_)
                p.o("ve", lambda E, ot=ot, tl=tl, g=gnp: E.tensor_scalar(
                    g[:, tl * 256:(tl + 1) * 256], ot, mqr[:, tl * 2: tl * 2 + 1],
                    mqr[:, tl * 2 + 1: tl * 2 + 2], op0=ALU.subtract, op1=ALU.mult))
                p.o("ve", lambda E, tl=tl, g=gnp: E.tensor_mul(
                    g[:, tl * 256:(tl + 1) * 256], g[:, tl * 256:(tl + 1) * 256],
                    gt_st[:, tl * 256:(tl + 1) * 256]), inc="VE")
            gn_done[qb] = p.c("VE")
            qeng, qsem = "sy", "SY"
            p.w(qeng, "VE", gn_done[qb])
            for tl in range(4):
                tt = 4 * qb + tl
                for e in range(2):
                    fc = 2 * h + e
                    p.dmat(qeng, qsem,
                           AR3[:, fc * T + tt * 128: fc * T + (tt + 1) * 128],
                           gn_st(qb % 2)[:, tl * 256 + e * 128: tl * 256 + e * 128 + 128])
            gn_tp[qb % 2] = (qsem, p.c(qsem))
        head_done_pe = p.c("PE")
        prev_head_out_evac = out_evac[3]
        prev_head_gn_done = gn_done[3]

    gnfm_sy, gnfm_sc = p.c("SY"), p.c("SC")
    if "gnfm" in tap_ap:
        for fc in range(16):
            for half in range(2):
                p.w("ve", "SY", gnfm_sy)
                p.w("ve", "SC", gnfm_sc)
                p.w("ve", "ST", p.c("ST"))
                p.o("ve", lambda E, fc=fc, half=half: E.tensor_copy(
                    scr(0), AR3[:, fc * T + half * 1024: fc * T + (half + 1) * 1024]), inc="VE")
                p.w("sy", "VE")
                p.dma("sy", "ST", tap_ap["gnfm"][fc * 128:(fc + 1) * 128,
                                                 half * 1024:(half + 1) * 1024], scr(0))

    p.w("sy", "PE", head_done_pe)
    p.w("sy", "MC", misc_cast)
    p.w("sy", "VE", p.c("VE"))
    p.w("sy", "PO", p.c("PO"))
    p.w("sy", "AC", p.c("AC"))
    p.w("sy", "SC", p.c("SC"))
    p.w("sy", "SY", p.c("SY"))
    for q4 in range(4):
        p.dma("sy", "WQ", AR4[:, q4 * 4096:(q4 + 1) * 4096].rearrange("p (fc c) -> p fc c", c=1024),
              wprj_c.ap().rearrange("(fc p) o -> p fc o", p=128)[:, q4 * 4:(q4 + 1) * 4, :])
    prj_ready = p.c("WQ")
    prjw = lambda fc: AR4[:, fc * 1024:(fc + 1) * 1024]

    p.w("te", "WQ", prj_ready)
    p.w("te", "SY", gnfm_sy)
    p.w("te", "SC", gnfm_sc)
    prev_evac = {}
    x1_ve = {}
    for tt in range(NT):
        for ob in range(2):
            k2 = tt * 2 + ob
            bank = PS[2 + k2 % 2]
            if k2 >= 2:
                p.w("te", "VE", prev_evac[k2 % 2])
            for fc in range(16):
                p.o("te", lambda E, b=bank, fc=fc, tt=tt, ob=ob: E.matmul(
                    b[:, :], AR3[:, fc * T + tt * 128: fc * T + (tt + 1) * 128],
                    prjw(fc)[:, ob * 512:(ob + 1) * 512],
                    start=(fc == 0), stop=(fc == 15)),
                    inc="PE" if fc == 15 else None)
            mmv = p.c("PE")
            p.w("ve", "PE", mmv)
            p.o("ve", lambda E, b=bank, ob=ob: E.tensor_copy(
                scr(0)[:, ob * 512:(ob + 1) * 512], b[:, :]), inc="VE")
            prev_evac[k2 % 2] = p.c("VE")
        p.o("ve", lambda E: E.bn_stats(st6[:, 0:6], scr(0)[:, 0:512]))
        p.o("ve", lambda E: E.bn_stats(st6[:, 6:12], scr(0)[:, 512:1024]), inc="VE")
        p.w("ve", "VE")
        p.o("ve", lambda E: E.bn_aggr(mv2[:, 0:2], st6[:, 0:12]), inc="VE")
        p.w("sc", "VE")
        p.o("sc", lambda E: E.activation(mv2[:, 2:3], mv2[:, 1:2], AF.Ln, bias=ccol[:, 0:1]), inc="AC")
        p.w("sc", "AC")
        p.o("sc", lambda E: E.activation(mv2[:, 2:3], mv2[:, 2:3], AF.Exp, scale=ccol[:, 1:2]), inc="AC")
        p.w("ve", "AC")
        p.o("ve", lambda E: E.tensor_scalar(scr(1), scr(0), mv2[:, 0:1], mv2[:, 2:3],
                                            op0=ALU.subtract, op1=ALU.mult))
        xt = AR1[:, tt * D:(tt + 1) * D]
        p.o("ve", lambda E, xt=xt: E.tensor_add(xt, xt, scr(1)), inc="VE")
        x1_ve[tt] = p.c("VE")
    x1_done = p.c("VE")

    tap_tm("x1", lambda tt: AR1[:, tt * D:(tt + 1) * D], "VE", x1_done)
    if stage <= 2:
        finish(p, nc, ctx)
        return nc

    p.w("sy", "VE", x1_done)
    p.dma("sy", "SY", ag_in.ap().rearrange("(tile p) d -> p tile d", p=128),
          AR1[:, :].rearrange("p (tile d) -> p tile d", d=D))
    agin_done = p.c("SY")
    p.w("gp", "SY", agin_done)
    p.o("gp", lambda E: E.collective_compute(
        "AllGather", ALU.bypass, replica_groups=[list(range(NCORES))],
        ins=[ag_in.ap().opt()], outs=[ag_out.ap().opt()]), inc="CC")
    p.w("gp", "CC", p.c("CC"))
    p.w("gp", "SY", misc_loaded)
    for tt in range(NT):
        p.o("gp", lambda E, tt=tt: E.indirect_dma_start(
            AR1[:, tt * D:(tt + 1) * D], None,
            ag_out.ap(), IndirectOffsetOnAxis(gidx[:, tt:tt + 1], 0)), inc="GPI", n=16)
    p.dma("gp", "GP", ag_in.ap()[0:1, :], AR1[:, 0:8].bitcast(F32)[0:1, 0:4]) if False else None
    p.dma("gp", "GP", ag_in.ap()[0:1, 0:1024], AR1[0:1, 0:2048].bitcast(F32))
    x1b_done = p.c("GP")
    tap_tm("x1b", lambda tt: AR1[:, tt * D:(tt + 1) * D], "GP", x1b_done)

    p.w("ve", "GP", x1b_done)
    bst = lambda par: AR3[:, 30720 + par * 1024: 30720 + (par + 1) * 1024]
    xln_tp = {}
    for tt in range(NT):
        xt = AR1[:, tt * D:(tt + 1) * D]
        p.o("ve", lambda E, xt=xt: E.bn_stats(st6[:, 0:6], xt[:, 0:512]))
        p.o("ve", lambda E, xt=xt: E.bn_stats(st6[:, 6:12], xt[:, 512:1024]), inc="VE")
        p.w("ve", "VE")
        p.o("ve", lambda E, tt=tt: E.bn_aggr(xstat[:, tt * 2: tt * 2 + 2], st6[:, 0:12]),
            inc="VE" if tt == NT - 1 else None)
    bstats_done = p.c("VE")
    p.w("sc", "VE", bstats_done)
    for tt in range(NT):
        p.o("sc", lambda E, tt=tt: E.activation(xstat[:, tt * 2 + 1: tt * 2 + 2],
                                                xstat[:, tt * 2 + 1: tt * 2 + 2], AF.Ln,
                                                bias=ccol[:, 0:1]), inc="AC")
    p.w("sc", "AC")
    for tt in range(NT):
        p.o("sc", lambda E, tt=tt: E.activation(xstat[:, tt * 2 + 1: tt * 2 + 2],
                                                xstat[:, tt * 2 + 1: tt * 2 + 2], AF.Exp,
                                                scale=ccol[:, 1:2]),
            inc="AC" if tt == NT - 1 else None)
    brstd_done = p.c("AC")
    p.w("ve", "AC", brstd_done)
    for tt in range(NT):
        xt = AR1[:, tt * D:(tt + 1) * D]
        if tt >= 2:
            s_, v_ = xln_tp[tt % 2]
            p.w("ve", s_, v_)
        p.o("ve", lambda E, xt=xt, tt=tt: E.tensor_scalar(
            bst(tt % 2)[:, :], xt, xstat[:, tt * 2: tt * 2 + 1],
            xstat[:, tt * 2 + 1: tt * 2 + 2], op0=ALU.subtract, op1=ALU.mult), inc="VE")
        hdone = p.c("VE")
        qeng, qsem = "sy", "SY"
        p.w(qeng, "VE", hdone)
        for ic in range(NIC):
            p.dmat(qeng, qsem, AR2[:, ic * T + tt * 128: ic * T + (tt + 1) * 128],
                   bst(tt % 2)[:, ic * 128:(ic + 1) * 128])
        xln_tp[tt % 2] = (qsem, p.c(qsem))
    xln_sy, xln_sc = p.c("SY"), p.c("SC")
    xlnfm = lambda ic: AR2[:, ic * T:(ic + 1) * T]
    if "xlnfm" in tap_ap:
        for ic in range(NIC):
            for half in range(2):
                p.w("ve", "SY", xln_sy)
                p.w("ve", "ST", p.c("ST"))
                p.o("ve", lambda E, ic=ic, half=half: E.tensor_copy(
                    scr(0), xlnfm(ic)[:, half * 1024:(half + 1) * 1024]), inc="VE")
                p.w("sy", "VE")
                p.dma("sy", "ST", tap_ap["xlnfm"][ic * 128:(ic + 1) * 128,
                                                  half * 1024:(half + 1) * 1024], scr(0))

    ipw_par = lambda par: AR7[:, B7_IPW + par * 4096: B7_IPW + (par + 1) * 4096]
    ipsem = ("IP0", "IP1")
    ip_loaded = {}

    def load_ip(src_c, ob, pe_gate, bias_off=None):
        par = ob % 2
        p.w("sy", "PE", pe_gate)
        p.w("sy", "MC", misc_cast)
        p.w("sy", "GP", small_loaded)
        p.dma("sy", ipsem[par], ipw_par(par)[:, :].rearrange("p (ic c) -> p ic c", c=512),
              src_c.ap().rearrange("(ic p) o -> p ic o", p=128)[:, :, ob * 512:(ob + 1) * 512])
        if bias_off is not None:
            p.dma("sy", ipsem[par], brow[:, par * 1024: par * 1024 + 512],
                  bias_c.ap()[:, bias_off: bias_off + 512])
        return p.c(ipsem[par])

    ip_loaded[0] = load_ip(ipw_c, 0, 0, bias_off=0)
    ip_loaded[1] = load_ip(ipw_c, 1, 0, bias_off=512)
    ip_consumed = {}
    ip_evac = {}
    for ob in range(6):
        par = ob % 2
        p.w("te", ipsem[par], ip_loaded[ob])
        if ob == 0:
            p.w("te", "SY", xln_sy)
            p.w("te", "SC", xln_sc)
            p.w("te", "GP", small_loaded)
        for tt in range(NT):
            k2 = tt % 2
            bank = PS[4 + k2]
            if (ob * NT + tt) >= 2:
                p.w("te", "VE", ip_evac[k2])
            p.o("te", lambda E, b=bank, par=par: E.matmul(
                b[:, :], ones_b[:, :], brow[:, par * 1024: par * 1024 + 512], start=True, stop=False))
            for ic in range(NIC):
                p.o("te", lambda E, b=bank, ic=ic, tt=tt, par=par: E.matmul(
                    b[:, :], xlnfm(ic)[:, tt * 128:(tt + 1) * 128],
                    ipw_par(par)[:, ic * 512:(ic + 1) * 512],
                    start=False, stop=(ic == NIC - 1)),
                    inc="PE" if ic == NIC - 1 else None)
            mmv = p.c("PE")
            p.w("ve", "PE", mmv)
            if ob < 2:
                dst = AR3[:, tt * 1024 + ob * 512: tt * 1024 + (ob + 1) * 512]
                p.o("ve", lambda E, b=bank, d=dst: E.tensor_scalar_mul(d, b[:, :], 0.125),
                    inc="VE")
            elif ob < 4:
                dst = AR3[:, 16384 + tt * 1024 + (ob - 2) * 512: 16384 + tt * 1024 + (ob - 1) * 512]
                p.o("ve", lambda E, b=bank, d=dst: E.tensor_copy(d, b[:, :]), inc="VE")
            else:
                dst = AR4[:, tt * 1024 + (ob - 4) * 512: tt * 1024 + (ob - 3) * 512]
                p.o("ve", lambda E, b=bank, d=dst: E.tensor_copy(d, b[:, :]), inc="VE")
            ip_evac[k2] = p.c("VE")
        ip_consumed[ob] = p.c("PE")
        if ob + 2 < 6:
            ip_loaded[ob + 2] = load_ip(ipw_c, ob + 2, ip_consumed[ob], bias_off=(ob + 2) * 512)
    qk_done = p.c("VE")

    if "qk" in tap_ap:
        for tt in range(NT):
            p.w("ve", "VE", qk_done)
            p.w("ve", "ST", p.c("ST"))
            p.o("ve", lambda E, tt=tt: E.tensor_copy(scr(0), AR3[:, tt * 1024:(tt + 1) * 1024]),
                inc="VE")
            p.w("sy", "VE")
            p.dma("sy", "ST", tap_ap["qk"][tt * 128:(tt + 1) * 128, 0:1024], scr(0))
            p.w("ve", "ST", p.c("ST"))
            p.o("ve", lambda E, tt=tt: E.tensor_copy(
                scr(0), AR3[:, 16384 + tt * 1024: 16384 + (tt + 1) * 1024]), inc="VE")
            p.w("sy", "VE")
            p.dma("sy", "ST", tap_ap["qk"][tt * 128:(tt + 1) * 128, 1024:2048], scr(0))

    mbuf = bf6[:, 2 * (A6_SCR + 1024): 2 * (A6_SCR + 1024) + 1024]
    mbuf3 = mbuf.rearrange("p (h c) -> p h c", c=64)
    p.w("ve", "VE", qk_done)
    sc_done = {}
    for s in range(NH):
        for hf in range(2):
            qtile = AR3[:, (s * 2 + hf) * 1024: (s * 2 + hf + 1) * 1024]
            for t in range(NH):
                ktile = AR3[:, 16384 + (t * 2 + hf) * 1024: 16384 + (t * 2 + hf + 1) * 1024]
                p.o("ve", lambda E, q=qtile, k=ktile: E.tensor_mul(mbuf[:, :], q, k))
                dst = scr(0)[:, hf * 128: hf * 128 + 128].rearrange(
                    "p (h t8) -> p h t8", t8=8)[:, :, t:t + 1]
                p.o("ve", lambda E, d=dst: E.tensor_reduce(d, mbuf3, axis=AX.X, op=ALU.add))
        for hf in range(2):
            sc3 = scr(0)[:, hf * 128: hf * 128 + 128].rearrange("p (h t8) -> p h t8", t8=8)
            mx = scr(1)[:, 992 + hf * 16: 992 + hf * 16 + 16]
            p.o("ve", lambda E, a=mx, b=sc3: E.tensor_reduce(
                a.unsqueeze(2), b, axis=AX.X, op=ALU.max), inc="VE")
            p.w("ve", "VE")
            p.o("ve", lambda E, a=sc3, m=mx: E.tensor_sub(
                a, a, m.unsqueeze(2).broadcast_to([128, 16, 8])), inc="VE")
            p.w("sc", "VE")
            flat = scr(0)[:, hf * 128: hf * 128 + 128]
            p.o("sc", lambda E, f=flat: E.activation(f, f, AF.Exp), inc="AC")
            p.w("ve", "AC")
            p.o("ve", lambda E, a=mx, b=sc3: E.tensor_reduce(
                a.unsqueeze(2), b, axis=AX.X, op=ALU.add), inc="VE")
            p.w("ve", "VE")
            p.o("ve", lambda E, a=mx: E.reciprocal(a, a), inc="VE")
            p.w("ve", "VE")
            dstw = attw[:, s * 256 + hf * 128: s * 256 + hf * 128 + 128].rearrange(
                "p (h t8) -> p h t8", t8=8)
            p.o("ve", lambda E, d=dstw, b=sc3, m=mx: E.tensor_tensor(
                d, b, m.unsqueeze(2).broadcast_to([128, 16, 8]), op=ALU.mult),
                inc="VE")
        sc_done[s] = p.c("VE")
    scores_all = p.c("VE")
    ctx_done = {}
    p.w("ve", "VE", scores_all)
    for s in range(NH):
        for hf in range(2):
            dstc = AR3[:, (s * 2 + hf) * 1024: (s * 2 + hf + 1) * 1024]
            d3 = dstc.rearrange("p (h c) -> p h c", c=64)
            aw = attw[:, s * 256 + hf * 128: s * 256 + hf * 128 + 128].rearrange(
                "p (h t8) -> p h t8", t8=8)
            for t in range(NH):
                vtile = AR4[:, (t * 2 + hf) * 1024: (t * 2 + hf + 1) * 1024]
                v3 = vtile.rearrange("p (h c) -> p h c", c=64)
                awb = aw[:, :, t:t + 1].broadcast_to([128, 16, 64])
                if t == 0:
                    p.o("ve", lambda E, d=d3, v=v3, a=awb: E.tensor_tensor(d, v, a, op=ALU.mult))
                else:
                    p.o("ve", lambda E, v=v3, a=awb: E.tensor_tensor(mbuf3, v, a, op=ALU.mult))
                    p.o("ve", lambda E, d=d3: E.tensor_add(d, d, mbuf3))
            p.o("ve", lambda E: E.engine_nop(), inc="VE")
        ctx_done[s] = p.c("VE")
    for s in range(NH):
        for hf in range(2):
            tt = s * 2 + hf
            qeng, qsem = "sy", "SY"
            p.w(qeng, "VE", ctx_done[s])
            for ic in range(NIC):
                p.dmat(qeng, qsem, AR2[:, ic * T + tt * 128: ic * T + (tt + 1) * 128],
                       AR3[:, tt * 1024 + ic * 128: tt * 1024 + ic * 128 + 128])
    ctxfm_sy, ctxfm_sc = p.c("SY"), p.c("SC")
    tap_tm("ctx", lambda tt: AR3[:, tt * 1024:(tt + 1) * 1024], "VE", ctx_done[NH - 1])

    ow_loaded = {ob: load_ip(outw_c, ob, ip_consumed[5], bias_off=3072 + ob * 512) for ob in range(2)}
    p.w("te", "SY", ctxfm_sy)
    p.w("te", "SC", ctxfm_sc)
    ao_evac = {}
    zfm_tp = {}
    xmid_ve = {}
    for tt in range(NT):
        for ob in range(2):
            bank = PS[4 + ob]
            p.w("te", ipsem[ob], ow_loaded[ob])
            if tt >= 1:
                p.w("te", "VE", ao_evac[ob])
            p.o("te", lambda E, b=bank, ob=ob: E.matmul(
                b[:, :], ones_b[:, :], brow[:, ob * 1024: ob * 1024 + 512],
                start=True, stop=False))
            for ic in range(NIC):
                p.o("te", lambda E, b=bank, ic=ic, tt=tt, ob=ob: E.matmul(
                    b[:, :], AR2[:, ic * T + tt * 128: ic * T + (tt + 1) * 128],
                    ipw_par(ob)[:, ic * 512:(ic + 1) * 512],
                    start=False, stop=(ic == NIC - 1)),
                    inc="PE" if ic == NIC - 1 else None)
            mmv = p.c("PE")
            p.w("ve", "PE", mmv)
            p.o("ve", lambda E, b=bank, ob=ob: E.tensor_copy(
                scr(0)[:, ob * 512:(ob + 1) * 512], b[:, :]), inc="VE")
            ao_evac[ob] = p.c("VE")
        xt = AR1[:, tt * D:(tt + 1) * D]
        p.o("ve", lambda E, xt=xt, tt=tt: E.tensor_scalar(
            scr(1), xt, xstat[:, tt * 2: tt * 2 + 1], xstat[:, tt * 2 + 1: tt * 2 + 2],
            op0=ALU.subtract, op1=ALU.mult))
        p.o("ve", lambda E: E.tensor_add(scr(0), scr(0), scr(1)))
        p.o("ve", lambda E: E.bn_stats(st6[:, 0:6], scr(0)[:, 0:512]))
        p.o("ve", lambda E: E.bn_stats(st6[:, 6:12], scr(0)[:, 512:1024]), inc="VE")
        p.w("ve", "VE")
        p.o("ve", lambda E: E.bn_aggr(mv2[:, 0:2], st6[:, 0:12]), inc="VE")
        p.w("sc", "VE")
        p.o("sc", lambda E: E.activation(mv2[:, 2:3], mv2[:, 1:2], AF.Ln, bias=ccol[:, 0:1]), inc="AC")
        p.w("sc", "AC")
        p.o("sc", lambda E: E.activation(mv2[:, 2:3], mv2[:, 2:3], AF.Exp, scale=ccol[:, 1:2]), inc="AC")
        p.w("ve", "AC")
        p.o("ve", lambda E: E.tensor_scalar(scr(1), scr(0), mv2[:, 0:1], mv2[:, 2:3],
                                            op0=ALU.subtract, op1=ALU.mult))
        p.o("ve", lambda E: E.tensor_mul(scr(1), scr(1), g1t))
        xmb = AR3[:, tt * D:(tt + 1) * D]
        p.o("ve", lambda E, d=xmb: E.tensor_tensor(d, scr(1), b1t, op=ALU.add), inc="VE")
        xmid_ve[tt] = p.c("VE")
        qeng, qsem = "sy", "SY"
        p.w(qeng, "VE", xmid_ve[tt])
        for ic in range(NIC):
            p.dmat(qeng, qsem, AR4[:, ic * T + tt * 128: ic * T + (tt + 1) * 128],
                   xmb[:, ic * 128:(ic + 1) * 128])
        zfm_tp[tt] = (qsem, p.c(qsem))
    zfm_sy, zfm_sc = p.c("SY"), p.c("SC")
    zfm = lambda ic: AR4[:, ic * T:(ic + 1) * T]
    xmid_done = p.c("VE")
    tap_tm("xmid", lambda tt: AR3[:, tt * D:(tt + 1) * D], "VE", xmid_done)
    if stage <= 3:
        finish(p, nc, ctx)
        return nc

    p.w("te", "SY", zfm_sy)
    p.w("te", "SC", zfm_sc)
    gat_evac = {}
    for tt in range(NT):
        bank = PS[tt % 2]
        if tt >= 2:
            p.w("te", "VE", gat_evac[tt % 2])
        p.o("te", lambda E, b=bank: E.matmul(b[:, 0:NE], ones_b[:, :], gateb_r[:, :],
                                             start=True, stop=False))
        for ic in range(NIC):
            p.o("te", lambda E, b=bank, ic=ic, tt=tt: E.matmul(
                b[:, 0:NE], zfm(ic)[:, tt * 128:(tt + 1) * 128], gwsb[:, ic * NE:(ic + 1) * NE],
                start=False, stop=(ic == NIC - 1)),
                inc="PE" if ic == NIC - 1 else None)
        mmv = p.c("PE")
        p.w("ve", "PE", mmv)
        pr = scr(0)[:, 0:NE]
        m1 = scr(0)[:, 8:16]
        v1 = scr(0)[:, 16:17]
        v2 = scr(0)[:, 17:18]
        def ev(fn):
            p.o("ve", fn, inc="VE")
            p.w("ve", "VE")
        ev(lambda E, b=bank, pr=pr: E.tensor_copy(pr, b[:, 0:NE]))
        ev(lambda E, pr=pr, v1=v1: E.tensor_reduce(v1, pr, axis=AX.X, op=ALU.max))
        p.o("ve", lambda E, pr=pr, v1=v1: E.tensor_single_scalar(pr, pr, v1[:, 0:1],
                                                                 op=ALU.subtract), inc="VE")
        p.w("sc", "VE")
        p.o("sc", lambda E, pr=pr: E.activation(pr, pr, AF.Exp), inc="AC")
        p.w("ve", "AC")
        ev(lambda E, pr=pr, v1=v1: E.tensor_reduce(v1, pr, axis=AX.X, op=ALU.add))
        ev(lambda E, v1=v1: E.reciprocal(v1, v1))
        ev(lambda E, pr=pr, v1=v1: E.tensor_scalar_mul(pr, pr, v1[:, 0:1]))
        ev(lambda E, pr=pr, v1=v1: E.tensor_reduce(v1, pr, axis=AX.X, op=ALU.max))
        ev(lambda E, pr=pr, m1=m1, v1=v1: E.tensor_single_scalar(m1, pr, v1[:, 0:1],
                                                                 op=ALU.is_ge))
        ev(lambda E, pr=pr, m1=m1: E.scalar_tensor_tensor(m1, m1, LNEG, pr,
                                                          op0=ALU.mult, op1=ALU.add))
        ev(lambda E, m1=m1, v2=v2: E.tensor_reduce(v2, m1, axis=AX.X, op=ALU.max))
        ev(lambda E, pr=pr, m1=m1, v2=v2: E.tensor_single_scalar(m1, pr, v2[:, 0:1],
                                                                 op=ALU.is_ge))
        ev(lambda E, pr=pr, m1=m1: E.tensor_mul(pr, pr, m1))
        ev(lambda E, pr=pr, v1=v1: E.tensor_reduce(v1, pr, axis=AX.X, op=ALU.add))
        ev(lambda E, v1=v1: E.reciprocal(v1, v1))
        p.o("ve", lambda E, pr=pr, v1=v1, tt=tt: E.tensor_scalar_mul(
            comb[:, tt * NE:(tt + 1) * NE], pr, v1[:, 0:1]), inc="VE")
        gat_evac[tt % 2] = p.c("VE")
    comb_done = p.c("VE")
    tap_tm("comb", lambda tt: comb[:, tt * NE:(tt + 1) * NE], "VE", comb_done, width=NE)
    if stage == 35:
        finish(p, nc, ctx)
        return nc

    w1buf = lambda par: AR7[:, M7_W1 + (par % 4) * 1024: M7_W1 + (par % 4 + 1) * 1024]
    w2buf = lambda par: AR7[:, M7_W2 + (par % 4) * 1024: M7_W2 + (par % 4 + 1) * 1024]
    acc_par = lambda tb: f32_AR3[:, 8192 + (tb % 2) * 4096: 8192 + (tb % 2 + 1) * 4096]
    h1buf = AR2
    w1sems = ("W10", "W11", "W12", "W13")
    w2sems = ("W20", "W21", "W22", "W23")
    out_stage = lambda par: scr(par)
    st_cnt = 0
    acc_consumed = {}
    y_evac = 0

    def load_w1(e, fc, pe_gate):
        par = fc % 4
        p.w("sy", "PE", pe_gate)
        p.w("sy", "MC", moe_cast[e])
        p.dma("sy", w1sems[par], w1buf(par)[:, :],
              w1_c.ap()[e, :, :].rearrange("(fc p) c -> fc p c", p=128)[fc, :, :])
        return p.c(w1sems[par])

    def load_w2(e, fc, pe_gate):
        par = fc % 4
        p.w("sy", "PE", pe_gate)
        p.w("sy", "MC", moe_cast[e])
        p.dma("sy", w2sems[par], w2buf(par)[:, :],
              w2_c.ap()[e, fc * 128:(fc + 1) * 128, :])
        return p.c(w2sems[par])

    w1_ld = {i: load_w1(0, i, 0) for i in range(4)}
    w2_ld = {i: load_w2(0, i, 0) for i in range(4)}
    if stage == 41:
        for s_ in ("W10", "W11", "W12", "W13", "W20", "W21", "W22", "W23", "MC"):
            p.w("sy", s_, p.c(s_))
        finish(p, nc, ctx)
        return nc
    h1_evac = {}
    first_iter = True
    epi_pending = None
    for tb in range(4):
        acc_f32 = acc_par(tb)
        p.w("ve", "VE", comb_done)
        if tb > 1:
            p.w("ve", "VE", acc_consumed[tb % 2])
        for tc in range(4):
            tt = tb * 4 + tc
            p.o("ve", lambda E, tt=tt, tc=tc, acc_f32=acc_f32: E.tensor_copy(
                acc_f32[:, tc * 1024:(tc + 1) * 1024], AR3[:, tt * D:(tt + 1) * D]), inc="VE")
        acc_init = p.c("VE")
        for e in range(NE):
            if first_iter:
                p.w("te", "SY", zfm_sy)
                p.w("te", "SC", zfm_sc)
                p.w("te", "VE", comb_done)
            else:
                p.w("te", "VE", y_evac01)
            for fc in range(NFC):
                par = fc % 4
                bpar = fc % 2
                p.w("te", w1sems[par], w1_ld[fc])
                if fc >= 2:
                    s_, v_ = h1_evac[bpar]
                    p.w("te", s_, v_)
                bank = PS[bpar]
                for ic in range(NIC):
                    p.o("te", lambda E, b=bank, ic=ic, tb=tb, par=par: E.matmul(
                        b[:, :], w1buf(par)[:, ic * 128:(ic + 1) * 128],
                        zfm(ic)[:, tb * 512:(tb + 1) * 512],
                        start=(ic == 0), stop=(ic == NIC - 1)),
                        inc="PE" if ic == NIC - 1 else None)
                w1_pe = p.c("PE")
                if fc + 4 < NFC:
                    w1_ld[fc + 4] = load_w1(e, fc + 4, w1_pe)
                else:
                    ne_, nt_ = (e + 1, tb) if e + 1 < NE else (0, tb + 1)
                    if nt_ < 4:
                        w1_ld[fc + 4 - NFC] = load_w1(ne_, fc + 4 - NFC, w1_pe)
                eng3, sem3 = ("sc", "AC") if fc % 2 == 0 else ("ve", "VE")
                p.w(eng3, "PE", w1_pe)
                if stage == 39:
                    if eng3 == "sc":
                        p.o(eng3, lambda E, b=bank, fc=fc: E.copy(
                            h1buf[:, fc * 512:(fc + 1) * 512], b[:, :]), inc=sem3)
                    else:
                        p.o(eng3, lambda E, b=bank, fc=fc: E.tensor_copy(
                            h1buf[:, fc * 512:(fc + 1) * 512], b[:, :]), inc=sem3)
                elif eng3 == "sc":
                    p.o(eng3, lambda E, b=bank, e=e, fc=fc: E.activation(
                        h1buf[:, fc * 512:(fc + 1) * 512], b[:, :], AF.Relu,
                        bias=b1c[:, e * NFC + fc: e * NFC + fc + 1]), inc=sem3)
                else:
                    p.o(eng3, lambda E, b=bank, e=e, fc=fc: E.tensor_scalar(
                        h1buf[:, fc * 512:(fc + 1) * 512], b[:, :],
                        b1c[:, e * NFC + fc: e * NFC + fc + 1], 0.0,
                        op0=ALU.add, op1=ALU.max), inc=sem3)
                h1_evac[bpar] = (sem3, p.c(sem3))
            h1_ve, h1_ac = p.c("VE"), p.c("AC")
            if stage in (38, 39) and e == 0:
                first_iter = False
                break
            p.w("sy", "PE", p.c("PE"))
            p.dma("sy", "WQ", brow[:, 0:1024], bias_c.ap()[:, 3072 + D + e * D: 3072 + D + (e + 1) * D])
            b2_ld = p.c("WQ")
            p.w("te", "WQ", b2_ld)
            p.w("te", "VE", h1_ve)
            p.w("te", "AC", h1_ac)
            if not first_iter:
                p.w("te", "VE", y_evac)
            if tb > 0 or e > 0:
                p.w("te", "VE", acc_init)
            for tc in range(4):
                for ob in range(2):
                    bank = PS[tc * 2 + ob]
                    p.o("te", lambda E, b=bank, ob=ob: E.matmul(
                        b[:, :], ones_b[:, :], brow[:, ob * 512:(ob + 1) * 512],
                        start=True, stop=False))
            for fc in range(NFC):
                par = fc % 4
                p.w("te", w2sems[par], w2_ld[fc])
                for tc in range(4):
                    for ob in range(2):
                        bank = PS[tc * 2 + ob]
                        p.o("te", lambda E, b=bank, fc=fc, tc=tc, ob=ob: E.matmul(
                            b[:, :], h1buf[:, fc * 512 + tc * 128: fc * 512 + tc * 128 + 128],
                            w2buf(fc % 4)[:, ob * 512:(ob + 1) * 512],
                            start=False, stop=(fc == NFC - 1)),
                            inc="PE" if (tc == 3 and ob == 1) else None)
                w2_pe = p.c("PE")
                if fc + 4 < NFC:
                    w2_ld[fc + 4] = load_w2(e, fc + 4, w2_pe)
                else:
                    ne_, nt_ = (e + 1, tb) if e + 1 < NE else (0, tb + 1)
                    if nt_ < 4:
                        w2_ld[fc + 4 - NFC] = load_w2(ne_, fc + 4 - NFC, w2_pe)
            y_done = p.c("PE")
            p.w("ve", "PE", y_done)
            p.w("ve", "VE", acc_init)
            for tc in range(4):
                tt = tb * 4 + tc
                for ob in range(2):
                    bank = PS[tc * 2 + ob]
                    dst = acc_f32[:, tc * 1024 + ob * 512: tc * 1024 + (ob + 1) * 512]
                    p.o("ve", lambda E, b=bank, d=dst, tt=tt, e=e: E.scalar_tensor_tensor(
                        d, b[:, :], comb[:, tt * NE + e: tt * NE + e + 1], d,
                        op0=ALU.mult, op1=ALU.add),
                        inc="VE" if (tc == 0 and ob == 1) or (tc == 3 and ob == 1) else None)
                if tc == 0:
                    y_evac01 = p.c("VE")
            y_evac = p.c("VE")
            first_iter = False
            if stage == 37 and e == 0:
                break
        if "accf" in tap_ap:
            for tc in range(4):
                tt = tb * 4 + tc
                p.w("ve", "VE", y_evac)
                p.w("ve", "ST", p.c("ST"))
                p.o("ve", lambda E, tc=tc: E.tensor_copy(scr(0), acc_f32[:, tc * 1024:(tc + 1) * 1024]), inc="VE")
                p.w("sy", "VE")
                p.dma("sy", "ST", tap_ap["accf"][tt * 128:(tt + 1) * 128, :], scr(0))
        if "h1f" in tap_ap and tb == 0:
            for fc in range(NFC):
                p.w("ve", "VE", y_evac)
                p.w("ve", "ST", p.c("ST"))
                p.o("ve", lambda E, fc=fc: E.tensor_copy(scr(0)[:, 0:512], h1buf[:, fc * 512:(fc + 1) * 512]), inc="VE")
                p.w("sy", "VE")
                p.dma("sy", "ST", tap_ap["h1f"][fc * 128:(fc + 1) * 128, :], scr(0)[:, 0:512])
        if stage in (36, 37, 38, 39) and tb == 0:
            break
        for tc in range(4):
            tt = tb * 4 + tc
            at = acc_f32[:, tc * 1024:(tc + 1) * 1024]
            p.w("ve", "VE", y_evac)
            p.o("ve", lambda E, at=at: E.bn_stats(st6[:, 0:6], at[:, 0:512]))
            p.o("ve", lambda E, at=at: E.bn_stats(st6[:, 6:12], at[:, 512:1024]), inc="VE")
            p.w("ve", "VE")
            p.o("ve", lambda E: E.bn_aggr(mv2[:, 0:2], st6[:, 0:12]), inc="VE")
            p.w("sc", "VE")
            p.o("sc", lambda E: E.activation(mv2[:, 2:3], mv2[:, 1:2], AF.Ln, bias=ccol[:, 0:1]), inc="AC")
            p.w("sc", "AC")
            p.o("sc", lambda E: E.activation(mv2[:, 2:3], mv2[:, 2:3], AF.Exp, scale=ccol[:, 1:2]),
                inc="AC")
            p.w("ve", "AC")
            if st_cnt >= 2:
                p.w("ve", "ST", 16 * (st_cnt - 1))
            osd = out_stage(tc % 2)
            p.o("ve", lambda E, at=at, o=osd: E.tensor_scalar(o, at, mv2[:, 0:1], mv2[:, 2:3],
                                                              op0=ALU.subtract, op1=ALU.mult))
            p.o("ve", lambda E, o=osd: E.tensor_mul(o, o, g2t))
            p.o("ve", lambda E, o=osd: E.tensor_tensor(o, o, b2t, op=ALU.add))
            p.o("ve", lambda E, o=osd: E.bn_stats(st6[:, 0:6], o[:, 0:512]))
            p.o("ve", lambda E, o=osd: E.bn_stats(st6[:, 6:12], o[:, 512:1024]), inc="VE")
            p.w("ve", "VE")
            p.o("ve", lambda E: E.bn_aggr(mv2[:, 0:2], st6[:, 0:12]), inc="VE")
            p.w("sc", "VE")
            p.o("sc", lambda E: E.activation(mv2[:, 2:3], mv2[:, 1:2], AF.Ln, bias=ccol[:, 0:1]), inc="AC")
            p.w("sc", "AC")
            p.o("sc", lambda E: E.activation(mv2[:, 2:3], mv2[:, 2:3], AF.Exp, scale=ccol[:, 1:2]),
                inc="AC")
            p.w("ve", "AC")
            p.o("ve", lambda E, o=osd: E.tensor_scalar(o, o, mv2[:, 0:1], mv2[:, 2:3],
                                                       op0=ALU.subtract, op1=ALU.mult))
            xt = AR1[:, tt * D:(tt + 1) * D]
            p.o("ve", lambda E, xt=xt, o=osd: E.scalar_tensor_tensor(
                o, xt, 1.0, o, op0=ALU.mult, op1=ALU.add), inc="VE")
            odone = p.c("VE")
            p.w("sy", "VE", odone)
            p.dma("sy", "ST", y_out[tt * 128:(tt + 1) * 128, :], osd)
            st_cnt += 1
        acc_consumed[tb % 2] = p.c("VE")
        if stage == 36:
            break

    finish(p, nc, ctx)
    return nc


def finish(p, nc, ctx):
    p.w("sy", "ST", p.c("ST"))
    with nc.Block() as block:
        @block.sync
        def _(E):
            for fn in p.q["sy"]:
                fn(E)

        @block.scalar
        def _(E):
            for fn in p.q["sc"]:
                fn(E)

        @block.vector
        def _(E):
            for fn in p.q["ve"]:
                fn(E)

        @block.gpsimd
        def _(E):
            for fn in p.q["gp"]:
                fn(E)

        @block.tensor
        def _(E):
            for fn in p.q["te"]:
                fn(E)
    ctx.close()


def host_prep(inputs):
    f = np.float32
    x = np.ascontiguousarray(inputs["L0_attn_hidden_states"], dtype=f)
    w_qkv = np.asarray(inputs["L0_attn_w_qkv"], dtype=f)
    wq, wk, wv = w_qkv[0:1024], w_qkv[1024:2048], w_qkv[2048:4096]
    groups = []
    for h in range(NH):
        groups += [wq[h * 128:(h + 1) * 128].T, wk[h * 128:(h + 1) * 128].T,
                   wv[h * 256:(h + 1) * 256].T]
    wqkv_g = np.ascontiguousarray(np.concatenate(groups, axis=1))
    wgat_t = np.ascontiguousarray(np.asarray(inputs["L0_attn_w_gated"], dtype=f).T)
    wprj_t = np.ascontiguousarray(np.asarray(inputs["L0_attn_w_proj"], dtype=f).T)
    decay_r = np.asarray(inputs["L0_attn_decay"], dtype=f).reshape(1, NH)
    Dm, SCALE_BASE = 1024, 512.0
    half = Dm // 2
    sv = (np.arange(0, Dm, 2, dtype=np.float64) + 0.4 * Dm) / (1.4 * Dm)
    pos = np.arange(T, dtype=np.float64)
    scale = sv[None, :] ** (pos[:, None] / SCALE_BASE)
    inv_freq = 1.0 / 10000.0 ** (np.arange(half, dtype=np.float64) / half)
    sinus = pos[:, None] * inv_freq[None, :]
    sin, cos = np.sin(sinus), np.cos(sinus)
    Cq = (cos * scale).astype(f)
    Sq = (sin * scale).astype(f)
    Ck = (cos / scale).astype(f)
    Sk = (sin / scale).astype(f)
    tab = np.empty((NH, NT, 128, 256), dtype=f)
    for h in range(NH):
        cols = slice(h * 64, (h + 1) * 64)
        for tt in range(NT):
            rows = slice(tt * 128, (tt + 1) * 128)
            tab[h, tt, :, 0:64] = Cq[rows, cols]
            tab[h, tt, :, 64:128] = Sq[rows, cols]
            tab[h, tt, :, 128:192] = Ck[rows, cols]
            tab[h, tt, :, 192:256] = Sk[rows, cols]
    umask = np.triu(np.ones((128, 128), dtype=f)).copy()
    ipw = np.ascontiguousarray(np.asarray(inputs["L0_ffn_in_proj_w"], dtype=f).T)
    ipb = np.asarray(inputs["L0_ffn_in_proj_b"], dtype=f).reshape(1, 3072)
    outw = np.ascontiguousarray(np.asarray(inputs["L0_ffn_out_w"], dtype=f).T)
    outb = np.asarray(inputs["L0_ffn_out_b"], dtype=f).reshape(1, D)
    gatew = np.ascontiguousarray(np.asarray(inputs["L0_ffn_gate_w"], dtype=f).T)
    gateb = np.asarray(inputs["L0_ffn_gate_b"], dtype=f).reshape(1, NE)
    w1 = np.asarray(inputs["L0_ffn_w1"], dtype=f)
    w1pre = np.ascontiguousarray(
        w1.reshape(NE, NFC, 128, NIC, 128).transpose(0, 1, 4, 3, 2).reshape(NE, NFC * 128, NIC * 128))
    b1 = np.asarray(inputs["L0_ffn_b1"], dtype=f)
    b1c = np.ascontiguousarray(b1.reshape(NE, NFC, 128).transpose(2, 0, 1).reshape(128, NE * NFC))
    w2 = np.asarray(inputs["L0_ffn_w2"], dtype=f)
    w2pre = np.ascontiguousarray(w2.transpose(0, 2, 1))
    b2r = np.asarray(inputs["L0_ffn_b2"], dtype=f).reshape(1, NE * D)
    lng = np.empty((128, 4 * D), dtype=f)
    lng[:, 0:D] = np.asarray(inputs["L0_ffn_ln1_g"], dtype=f)[None, :]
    lng[:, D:2 * D] = np.asarray(inputs["L0_ffn_ln1_b"], dtype=f)[None, :]
    lng[:, 2 * D:3 * D] = np.asarray(inputs["L0_ffn_ln2_g"], dtype=f)[None, :]
    lng[:, 3 * D:4 * D] = np.asarray(inputs["L0_ffn_ln2_b"], dtype=f)[None, :]
    base = {
        "wqkv_in": wqkv_g, "wgat_in": wgat_t, "wprj_in": wprj_t, "decay_r": decay_r,
        "tab_in": tab, "umask_in": umask,
        "ipw_in": ipw, "outw_in": outw,
        "gatew_in": gatew, "gateb_in": gateb,
        "bias_in": np.ascontiguousarray(np.concatenate([ipb, outb, b2r], axis=1)),
        "w1pre_in": w1pre, "b1c_in": b1c, "w2pre_in": w2pre,
        "lng_in": np.ascontiguousarray(lng),
    }
    in_maps = []
    for c in range(NCORES):
        m = dict(base)
        m["x_in"] = np.ascontiguousarray(x[c])
        j = np.arange(T)
        gidx_v = ((j // 256) * T + c * 256 + (j % 256)).astype(np.int32)
        m["gidx_in"] = np.ascontiguousarray(gidx_v.reshape(NT, 128).T)
        in_maps.append(m)
    return in_maps


def kernel(**inputs):
    in_maps = host_prep(inputs)
    nc = build_nc(stage=99)
    res = run_bass_kernel_spmd(nc, in_maps, core_ids=list(range(NCORES)))
    out = np.empty((NCORES, T, D), dtype=np.float32)
    for c in range(NCORES):
        y = np.asarray(res.results[c]["y_out"])
        out[:, c * 256:(c + 1) * 256, :] = y.reshape(NCORES, 256, D)
    return out


# version 06
# speedup vs baseline: 1.0044x; 1.0019x over previous
import numpy as np
import contextlib
import concourse.bass as bass
import concourse.mybir as mybir
from concourse.bass_utils import run_bass_kernel_spmd
from concourse.bass_core_rust import IndirectOffsetOnAxis

F32 = mybir.dt.float32
BF16 = mybir.dt.bfloat16
I32 = mybir.dt.int32
AF = mybir.ActivationFunctionType
ALU = mybir.AluOpType
AX = mybir.AxisListType

T = 2048
D = 1024
NT = 16
NH = 8
HDQ, HDV = 128, 256
NIC = 8
EPS = 1e-5
LNEG = -30000.0
NCORES = 8
FF = 4096
NE = 8
NFC = 32

ENGS = ("sy", "sc", "ve", "gp", "te")


class Prog:
    def __init__(self, nc, ctx):
        self.nc = nc
        self.ctx = ctx
        self.q = {e: [] for e in ENGS}
        self.sems = {}
        self.cnt = {}

    def sem(self, *names):
        for n in names:
            self.sems[n] = self.ctx.enter_context(self.nc.semaphore(n))
            self.cnt[n] = 0

    def w(self, eng, sem, val=None):
        v = self.cnt[sem] if val is None else val
        if v <= 0:
            return
        h = self.sems[sem]
        self.q[eng].append(lambda E, h=h, v=v: E.wait_ge(h, v))

    def o(self, eng, fn, inc=None, n=1):
        if inc is not None:
            self.cnt[inc] += n
            h = self.sems[inc]
            self.q[eng].append(lambda E, fn=fn, h=h, n=n: fn(E).then_inc(h, n))
            return self.cnt[inc]
        self.q[eng].append(fn)
        return None

    def dma(self, eng, sem, out, in_, **kw):
        return self.o(eng, lambda E, out=out, in_=in_, kw=kw: E.dma_start(out=out, in_=in_, **kw),
                      inc=sem, n=16)

    def dmat(self, eng, sem, out, in_):
        return self.o(eng, lambda E, out=out, in_=in_: E.dma_start(out=out, in_=in_, transpose=True),
                      inc=sem, n=16)

    def c(self, sem):
        return self.cnt[sem]


def build_nc(stage=99, taps=(), n_heads=NH, stop_qkv=False, no_moe_cast=False):
    nc = bass.Bass("TRN2", target_bir_lowering=False, debug=False,
                   detect_race_conditions=False)

    dram = lambda n, sh, dt, kind: nc.dram_tensor(n, sh, dt, kind=kind).ap()
    x_in = dram("x_in", [T, D], F32, "ExternalInput")
    wqkv_in = dram("wqkv_in", [D, 4096], F32, "ExternalInput")
    wgat_in = dram("wgat_in", [D, 2048], F32, "ExternalInput")
    wprj_in = dram("wprj_in", [2048, D], F32, "ExternalInput")
    decay_r = dram("decay_r", [1, NH], F32, "ExternalInput")
    tab_in = dram("tab_in", [NH, NT, 128, 256], F32, "ExternalInput")
    umask_in = dram("umask_in", [128, 128], F32, "ExternalInput")
    gidx_in = dram("gidx_in", [128, NT], I32, "ExternalInput")
    ipw_in = dram("ipw_in", [D, 3072], F32, "ExternalInput")
    outw_in = dram("outw_in", [D, D], F32, "ExternalInput")
    gatew_in = dram("gatew_in", [D, NE], F32, "ExternalInput")
    gateb_in = dram("gateb_in", [1, NE], F32, "ExternalInput")
    bias_in = dram("bias_in", [1, 3072 + D + NE * D], F32, "ExternalInput")
    w1pre_in = dram("w1pre_in", [NE, NFC * 128, NIC * 128], F32, "ExternalInput")
    b1c_in = dram("b1c_in", [128, NE * NFC], F32, "ExternalInput")
    w2pre_in = dram("w2pre_in", [NE, FF, D], F32, "ExternalInput")
    lng_in = dram("lng_in", [128, 4 * D], F32, "ExternalInput")

    y_out = dram("y_out", [T, D], F32, "ExternalOutput")
    tap_ap = {}
    for tn in taps:
        shapes = {"hfm": [D, T], "x1": [T, D], "xmid": [T, D], "qk": [T, 2048],
                  "ctx": [T, D], "comb": [T, NE], "x1b": [T, D], "gnfm": [2048, T], "accf": [T, D], "h1f": [FF, 512],
                  "qkfm": [2 * HDQ + HDV, T], "xlnfm": [D, T]}
        tap_ap[tn] = dram("tap_" + tn, shapes[tn], F32, "ExternalOutput")

    ag_in = nc.dram_tensor("ag_in", [T, D], BF16)
    ag_out = nc.dram_tensor("ag_out", [NCORES * T, D], BF16)
    bias_c = nc.dram_tensor("bias_c", [1, 3072 + D + NE * D], BF16)
    wqkv_c = nc.dram_tensor("wqkv_c", [D, 4096], BF16)
    wgat_c = nc.dram_tensor("wgat_c", [D, 2048], BF16)
    wprj_c = nc.dram_tensor("wprj_c", [2048, D], BF16)
    ipw_c = nc.dram_tensor("ipw_c", [D, 3072], BF16)
    outw_c = nc.dram_tensor("outw_c", [D, D], BF16)
    w1_c = nc.dram_tensor("w1_c", [NE, NFC * 128, NIC * 128], BF16)
    w2_c = nc.dram_tensor("w2_c", [NE, FF, D], BF16)

    ctx = contextlib.ExitStack()
    SB = lambda n, cols, dt: ctx.enter_context(nc.sbuf_tensor(n, [128, cols], dt))
    AR1 = SB("AR1", 16384, BF16)
    AR2 = SB("AR2", 16384, BF16)
    AR3 = SB("AR3", 32768, BF16)
    AR4 = SB("AR4", 16384, BF16)
    AR6 = SB("AR6", 3072, F32)
    AR7 = SB("AR7", 12288, BF16)
    umask = SB("umask", 128, F32)
    tcol = SB("tcol", NT, F32)
    qdec = SB("qdec", NT, F32)
    kdec = SB("kdec", NT, F32)
    ldb = SB("ldb", NH, F32)
    nldb = SB("nldb", NH, F32)
    gidx = SB("gidx", NT, I32)
    xstat = SB("xstat", 2 * NT, F32)
    st6 = SB("st6", 12, F32)
    mv2 = SB("mv2", 8, F32)
    b1c = SB("b1c", NE * NFC, F32)
    comb = SB("comb", NT * NE, F32)
    gwsb = SB("gwsb", NIC * NE, BF16)
    attw = SB("attw", NH * 256, BF16)
    mqr = SB("mqr", 8, F32)
    ccol = SB("ccol", 4, F32)
    decs = ctx.enter_context(nc.sbuf_tensor("decs", [1, NH], F32))
    ones_f = ctx.enter_context(nc.sbuf_tensor("ones_f", [1, 128], F32))
    ones_b = ctx.enter_context(nc.sbuf_tensor("ones_b", [1, 128], BF16))
    gateb_r = ctx.enter_context(nc.sbuf_tensor("gateb_r", [1, NE], BF16))
    brow = ctx.enter_context(nc.sbuf_tensor("brow", [1, 2048], BF16))

    PS = [ctx.enter_context(nc.psum_tensor("ps%d" % i, [128, 512], F32)) for i in range(8)]

    p = Prog(nc, ctx)
    p.sem("SY", "SC", "GP", "PE", "VE", "AC", "PO", "CC", "ST",
          "TB0", "TB1", "WQ", "IP0", "IP1", "W10", "W11", "W12", "W13", "W20", "W21", "W22", "W23", "MC", "GPI")

    A4_V, A4_QFM, A4_KFM, A4_QKVW, A4_GATEW, A4_STB, A4_QKS = \
        0, 4096, 6144, 8192, 12288, 14336, 15360
    A6_TAB, A6_SCR, A6_OFM = 0, 512, 2560
    A7_OTM, A7_GT, A7_GN, A7_HST = 0, 1024, 2048, 4096
    B7_IPW, B7_LN = 0, 8192
    M7_W1, M7_W2 = 0, 4096

    scr = lambda i: AR6[:, A6_SCR + i * 1024: A6_SCR + (i + 1) * 1024]
    bf6 = AR6[:, :].bitcast(BF16)
    f32_AR3 = AR3[:, :].bitcast(F32)
    tabs = lambda par: AR6[:, A6_TAB + par * 256: A6_TAB + (par + 1) * 256]
    hfm = lambda ic: AR2[:, ic * T:(ic + 1) * T]
    g1t = AR7[:, B7_LN + 0 * D: B7_LN + 1 * D]
    b1t = AR7[:, B7_LN + 1 * D: B7_LN + 2 * D]
    g2t = AR7[:, B7_LN + 2 * D: B7_LN + 3 * D]
    b2t = AR7[:, B7_LN + 3 * D: B7_LN + 4 * D]

    p.dma("sy", "SY", umask[:, :], umask_in)
    p.dma("sy", "SY", gidx[:, :], gidx_in)
    p.dma("sy", "SY", decs[:, :], decay_r)
    p.dma("sy", "SY", b1c[:, :], b1c_in)
    misc_loaded = p.c("SY")

    for tt in range(NT):
        p.dma("gp", "GP", AR1[:, tt * D:(tt + 1) * D], x_in[tt * 128:(tt + 1) * 128, :])
    x_loaded = p.c("GP")
    p.dma("gp", "GP", gateb_r[:, :], gateb_in)
    p.dma("gp", "GP", bias_c.ap(), bias_in)
    p.dma("gp", "GP", gwsb[:, :].rearrange("p (ic e) -> p ic e", e=NE), gatew_in.rearrange("(ic p) e -> p ic e", p=128))
    p.dma("gp", "GP", AR7[:, B7_LN:B7_LN + 4 * D], lng_in)
    small_loaded = p.c("GP")
    p.dma("gp", "MC", wqkv_c.ap(), wqkv_in)
    p.dma("gp", "MC", wgat_c.ap(), wgat_in)
    qkv_cast = p.c("MC")

    p.o("ve", lambda E: E.memset(ones_f[:, :], 1.0))
    p.o("ve", lambda E: E.memset(ccol[:, 0:1], EPS))
    p.o("ve", lambda E: E.memset(ccol[:, 1:2], -0.5))
    p.o("ve", lambda E: E.memset(ccol[:, 2:3], float(np.log(128.0 ** -0.5))))
    p.o("ve", lambda E: E.memset(ones_b[:, :], 1.0), inc="VE")
    ones_ready = p.c("VE")
    p.o("gp", lambda E: E.iota(tcol[:, :], pattern=[[128, NT]], base=0, channel_multiplier=1,
                               allow_small_or_imprecise_dtypes=True), inc="PO")
    iota_done = p.c("PO")
    p.w("sc", "SY", misc_loaded)
    p.o("sc", lambda E: E.activation(decs[:, :], decs[:, :], AF.Ln), inc="AC")
    ln_dec = p.c("AC")
    p.w("te", "AC", ln_dec)
    p.w("te", "VE", ones_ready)
    p.o("te", lambda E: E.matmul(PS[6][:, 0:NH], ones_f[:, :], decs[:, :],
                                 start=True, stop=True), inc="PE")
    p.w("ve", "PE")
    p.o("ve", lambda E: E.tensor_copy(ldb[:, :], PS[6][:, 0:NH]), inc="VE")
    p.w("ve", "VE")
    p.o("ve", lambda E: E.tensor_scalar_mul(nldb[:, :], ldb[:, :], -1.0), inc="VE")
    ldb_ready = p.c("VE")
    p.w("sc", "VE", ldb_ready)
    p.w("sc", "PO", iota_done)
    dec_ready = p.c("AC")

    qkvw_early = AR4[:, 8192:8192 + 4096]
    gatew_early = AR4[:, 12288:12288 + 2048]
    p.w("sy", "MC", qkv_cast)
    p.dma("sy", "WQ", qkvw_early[:, :].rearrange("p (ic c) -> p ic c", c=512),
          wqkv_c.ap().rearrange("(ic p) o -> p ic o", p=128)[:, :, 0:512])
    p.dma("sy", "WQ", gatew_early[:, :].rearrange("p (ic c) -> p ic c", c=256),
          wgat_c.ap().rearrange("(ic p) o -> p ic o", p=128)[:, :, 0:256])
    early_wq = p.c("WQ")
    p.dma("sy", "TB0", AR6[:, 0:256], tab_in[0, 0, :, :])
    p.dma("sy", "TB1", AR6[:, 256:512], tab_in[0, 1, :, :])
    early_tb0, early_tb1 = p.c("TB0"), p.c("TB1")
    hst = lambda par: AR7[:, A7_HST + par * D: A7_HST + (par + 1) * D]
    hst_tp = {}
    hfm_tile_tp = {}
    p.w("ve", "GP", x_loaded)
    for tt in range(NT):
        xt = AR1[:, tt * D:(tt + 1) * D]
        p.o("ve", lambda E, xt=xt: E.bn_stats(st6[:, 0:6], xt[:, 0:512]))
        p.o("ve", lambda E, xt=xt: E.bn_stats(st6[:, 6:12], xt[:, 512:1024]), inc="VE")
        p.w("ve", "VE")
        p.o("ve", lambda E, tt=tt: E.bn_aggr(xstat[:, tt * 2: tt * 2 + 2], st6[:, 0:12]),
            inc="VE" if tt == NT - 1 else None)
    xstats_done = p.c("VE")
    p.w("sc", "VE", xstats_done)
    for tt in range(NT):
        p.o("sc", lambda E, tt=tt: E.activation(xstat[:, tt * 2 + 1: tt * 2 + 2],
                                                xstat[:, tt * 2 + 1: tt * 2 + 2], AF.Ln,
                                                bias=ccol[:, 0:1]), inc="AC")
    p.w("sc", "AC")
    for tt in range(NT):
        p.o("sc", lambda E, tt=tt: E.activation(xstat[:, tt * 2 + 1: tt * 2 + 2],
                                                xstat[:, tt * 2 + 1: tt * 2 + 2], AF.Exp,
                                                scale=ccol[:, 1:2]),
            inc="AC" if tt == NT - 1 else None)
    xrstd_done = p.c("AC")
    p.w("ve", "AC", xrstd_done)
    for tt in range(NT):
        xt = AR1[:, tt * D:(tt + 1) * D]
        if tt >= 2:
            s_, v_ = hst_tp[tt % 2]
            p.w("ve", s_, v_)
        p.o("ve", lambda E, xt=xt, tt=tt: E.tensor_scalar(
            hst(tt % 2)[:, :], xt, xstat[:, tt * 2: tt * 2 + 1],
            xstat[:, tt * 2 + 1: tt * 2 + 2], op0=ALU.subtract, op1=ALU.mult), inc="VE")
        hdone = p.c("VE")
        qeng, qsem = "sy", "SY"
        p.w(qeng, "VE", hdone)
        for ic in range(NIC):
            p.dmat(qeng, qsem, AR2[:, ic * T + tt * 128: ic * T + (tt + 1) * 128],
                   hst(tt % 2)[:, ic * 128:(ic + 1) * 128])
        hst_tp[tt % 2] = (qsem, p.c(qsem))
        hfm_tile_tp[tt] = p.c("SY")
    hfm_sy, hfm_sc = p.c("SY"), p.c("SC")
    p.w("gp", "SY", hfm_sy)
    p.dma("gp", "MC", ipw_c.ap(), ipw_in)
    p.dma("gp", "MC", outw_c.ap(), outw_in)
    p.dma("gp", "MC", wprj_c.ap(), wprj_in)
    misc_cast = p.c("MC")

    def tap_tm(name, src_fn, gate_sem, gate_val, width=D):
        if name not in tap_ap:
            return
        for tt in range(NT):
            p.w("ve", gate_sem, gate_val)
            p.w("ve", "ST", p.c("ST"))
            p.o("ve", lambda E, tt=tt: E.tensor_copy(scr(0)[:, 0:width], src_fn(tt)), inc="VE")
            p.w("sy", "VE")
            p.dma("sy", "ST", tap_ap[name][tt * 128:(tt + 1) * 128, 0:width], scr(0)[:, 0:width])

    if "hfm" in tap_ap:
        for ic in range(NIC):
            for half in range(2):
                p.w("ve", "SY", hfm_sy)
                p.w("ve", "SC", hfm_sc)
                p.w("ve", "ST", p.c("ST"))
                p.o("ve", lambda E, ic=ic, half=half: E.tensor_copy(
                    scr(0), hfm(ic)[:, half * 1024:(half + 1) * 1024]), inc="VE")
                p.w("sy", "VE")
                p.dma("sy", "ST", tap_ap["hfm"][ic * 128:(ic + 1) * 128,
                                                half * 1024:(half + 1) * 1024], scr(0))
    if stage <= 1:
        finish(p, nc, ctx)
        return nc

    vh = AR4[:, A4_V:A4_V + NT * HDV]
    qfm = AR4[:, A4_QFM:A4_QFM + T]
    kfm = AR4[:, A4_KFM:A4_KFM + T]
    qkvw = AR4[:, A4_QKVW:A4_QKVW + 4096]
    gatew = AR4[:, A4_GATEW:A4_GATEW + 2048]
    stb = lambda par: AR4[:, A4_STB + par * 512: A4_STB + (par + 1) * 512]
    qks = lambda par: AR4[:, A4_QKS + par * 512: A4_QKS + (par + 1) * 512]
    otm_st = AR7[:, A7_OTM:A7_OTM + 1024]
    gt_st = AR7[:, A7_GT:A7_GT + 1024]
    gn_st = lambda par: AR7[:, A7_GN + par * 1024: A7_GN + (par + 1) * 1024]

    tabv = {(0, 0): early_tb0, (0, 1): early_tb1}

    def tab_load(h, tt):
        sem = "TB0" if tt % 2 == 0 else "TB1"
        p.dma("sy", sem, tabs(tt % 2)[:, :], tab_in[h, tt, :, :])
        tabv[(h, tt)] = p.c(sem)

    head_done_pe = 0
    xpos_done_ve = 0
    xpos_done_po = 0
    out_evac = {}
    gate_ac = {}
    ofm_tp = {}
    gn_done = {}
    gn_tp = {}
    prev_head_out_evac = None
    prev_head_gn_done = None

    for h in range(n_heads):
        if h > 0:
            p.w("sy", "PE", head_done_pe)
            p.dma("sy", "WQ", gatew[:, :].rearrange("p (ic c) -> p ic c", c=256),
                  wgat_c.ap().rearrange("(ic p) o -> p ic o", p=128)[:, :, h * 256:(h + 1) * 256])
        wq_ready = p.c("WQ")
        p.o("sc", lambda E, h=h: E.activation(qdec[:, :], tcol[:, :], AF.Exp,
                                              bias=ccol[:, 2:3],
                                              scale=ldb[:, h:h + 1]))
        p.o("sc", lambda E, h=h: E.activation(kdec[:, :], tcol[:, :], AF.Exp,
                                              scale=nldb[:, h:h + 1]), inc="AC")

        evac_ac = {}
        tpq = {}
        for tt in range(NT):
            bank = PS[tt % 2]
            p.w("te", "WQ", wq_ready)
            if h == 0:
                p.w("te", "SY", hfm_tile_tp[tt])
            if h == 0 and tt == 0:
                p.w("te", "AC", dec_ready)
            if tt >= 2:
                p.w("te", "AC", evac_ac[tt - 2])
            elif h > 0:
                p.w("te", "AC", prev_evac_last[tt])
            for ic in range(NIC):
                p.o("te", lambda E, b=bank, ic=ic, tt=tt: E.matmul(
                    b[:, :], hfm(ic)[:, tt * 128:(tt + 1) * 128],
                    qkvw[:, ic * 512:(ic + 1) * 512],
                    start=(ic == 0), stop=(ic == NIC - 1)),
                    inc="PE" if ic == NIC - 1 else None)
            mm_done = p.c("PE")
            p.w("sc", "PE", mm_done)
            if tt == 0:
                p.w("sc", "AC", p.c("AC"))
            if tt >= 2 or h > 0:
                p.w("sc", "VE", xpos_done_ve)
                p.w("sc", "PO", xpos_done_po)
            sp = scr(tt % 2)
            p.o("sc", lambda E, b=bank, sp=sp, h=h, tt=tt: E.mul(
                sp[:, 0:128], b[:, 0:128], qdec[:, tt:tt + 1]))
            p.o("sc", lambda E, b=bank, sp=sp, h=h, tt=tt: E.mul(
                sp[:, 128:256], b[:, 128:256], kdec[:, tt:tt + 1]))
            p.o("sc", lambda E, b=bank, tt=tt: E.copy(vh[:, tt * HDV:(tt + 1) * HDV],
                                                      b[:, 256:512]), inc="AC")
            evac_ac[tt] = p.c("AC")
            eng = "ve" if (tt % 3 != 2) else "gp"
            esem = "VE" if eng == "ve" else "PO"
            p.w(eng, "AC", evac_ac[tt])
            p.w(eng, "TB0" if tt % 2 == 0 else "TB1", tabv[(h, tt)])
            if tt >= 2:
                s_, v_ = tpq[tt % 2]
                p.w(eng, s_, v_)
            elif h > 0:
                p.w(eng, "SY", prev_tpq_sy)
                p.w(eng, "SC", prev_tpq_sc)
            tb = tabs(tt % 2)
            stg = qks(tt % 2)
            for o_q in (0, 128):
                base_t = 0 if o_q == 0 else 128
                se = sp[:, o_q:o_q + 128][:, 0::2]
                so = sp[:, o_q:o_q + 128][:, 1::2]
                Ct = tb[:, base_t: base_t + 64]
                St_ = tb[:, base_t + 64: base_t + 128]
                t0 = sp[:, 512:576]
                t1 = sp[:, 576:640]
                de = stg[:, o_q:o_q + 128][:, 0::2]
                do = stg[:, o_q:o_q + 128][:, 1::2]
                p.o(eng, lambda E, a=t0, b_=so, c=St_: E.tensor_mul(a, b_, c))
                p.o(eng, lambda E, a=t1, b_=se, c=Ct: E.tensor_mul(a, b_, c), inc=esem)
                p.w(eng, esem)
                p.o(eng, lambda E, a=de, b_=t1, c=t0: E.tensor_sub(a, b_, c))
                p.o(eng, lambda E, a=t0, b_=se, c=St_: E.tensor_mul(a, b_, c))
                p.o(eng, lambda E, a=t1, b_=so, c=Ct: E.tensor_mul(a, b_, c), inc=esem)
                p.w(eng, esem)
                p.o(eng, lambda E, a=do, b_=t1, c=t0: E.tensor_add(a, b_, c),
                    inc=esem)
            if eng == "ve":
                xpos_done_ve = p.c("VE")
            else:
                xpos_done_po = p.c("PO")
            p.w("sy", esem, p.c(esem))
            if tt + 2 < NT:
                tab_load(h, tt + 2)
            elif h + 1 < NH:
                tab_load(h + 1, tt + 2 - NT)
            qeng, qsem = "sy", "SY"
            p.w(qeng, esem, p.c(esem))
            p.dmat(qeng, qsem, qfm[:, tt * 128:(tt + 1) * 128], stg[:, 0:128])
            p.dmat(qeng, qsem, kfm[:, tt * 128:(tt + 1) * 128], stg[:, 128:256])
            tpq[tt % 2] = (qsem, p.c(qsem))
        prev_evac_last = {0: evac_ac[14], 1: evac_ac[15]}
        prev_tpq_sy, prev_tpq_sc = p.c("SY"), p.c("SC")
        qkv_tp_sy, qkv_tp_sc = p.c("SY"), p.c("SC")
        if h + 1 < n_heads:
            p.w("sy", "PE", p.c("PE"))
            p.dma("sy", "WQ", qkvw[:, :].rearrange("p (ic c) -> p ic c", c=512),
                  wqkv_c.ap().rearrange("(ic p) o -> p ic o", p=128)[:, :, (h + 1) * 512:(h + 2) * 512])
        if "qkfm" in tap_ap and h == n_heads - 1:
            for row, srcv in ((0, qfm), (1, kfm)):
                for half in range(2):
                    p.w("ve", "SY", qkv_tp_sy)
                    p.w("ve", "SC", qkv_tp_sc)
                    p.w("ve", "ST", p.c("ST"))
                    p.o("ve", lambda E, s=srcv, half=half: E.tensor_copy(
                        scr(0), s[:, half * 1024:(half + 1) * 1024]), inc="VE")
                    p.w("sy", "VE")
                    p.dma("sy", "ST", tap_ap["qkfm"][row * 128:(row + 1) * 128,
                                                     half * 1024:(half + 1) * 1024], scr(0))
            for tt4 in range(4):
                p.w("ve", "AC", evac_ac[15])
                p.w("ve", "ST", p.c("ST"))
                p.o("ve", lambda E, tt4=tt4: E.tensor_copy(
                    scr(0), vh[:, tt4 * 1024:(tt4 + 1) * 1024]), inc="VE")
                p.w("sy", "VE")
                p.dma("sy", "ST", tap_ap["qkfm"][256:384, tt4 * 512:(tt4 + 1) * 512],
                      scr(0)[:, 0:512])
                p.w("ve", "ST", p.c("ST"))
                p.o("ve", lambda E: E.engine_nop(), inc="VE")
                p.w("sy", "VE")
                p.dma("sy", "ST", tap_ap["qkfm"][384:512, tt4 * 512:(tt4 + 1) * 512],
                      scr(0)[:, 512:1024])

        if stop_qkv:
            finish(p, nc, ctx)
            return nc
        p.w("te", "SY", qkv_tp_sy)
        p.w("te", "SC", qkv_tp_sc)
        p.w("te", "AC", evac_ac[15])
        st_evac = {}
        sv_pe = {}

        for qb in range(4):
            nkb = 4 * qb + 4

            def emit_st(kb, qb=qb, nkb=nkb):
                bank = PS[2 + (kb % 2)]
                if kb >= 2:
                    s_, v_ = st_evac[(qb, kb - 2)]
                    p.w("te", s_, v_)
                elif qb > 0:
                    pk = 4 * qb
                    s_, v_ = st_evac[(qb - 1, pk - 2 + kb)]
                    p.w("te", s_, v_)
                p.o("te", lambda E, b=bank, kb=kb, qb=qb: E.matmul(
                    b[:, :], kfm[:, kb * 128:(kb + 1) * 128], qfm[:, qb * 512:(qb + 1) * 512],
                    start=True, stop=True), inc="PE")
                st_pe = p.c("PE")
                boundary = kb >= 4 * qb
                eng2 = "ve" if (boundary or kb % 2 == 0) else "sc"
                sem2 = "VE" if eng2 == "ve" else "AC"
                p.w(eng2, "PE", st_pe)
                if (qb, kb - 2) in sv_pe:
                    p.w(eng2, "PE", sv_pe[(qb, kb - 2)])
                elif qb > 0 and kb < 2:
                    p.w(eng2, "PE", sv_pe[(qb - 1, 4 * qb - 2 + kb)])
                dstb = stb(kb % 2)
                if not boundary:
                    if eng2 == "ve":
                        p.o(eng2, lambda E, d=dstb, b=bank: E.tensor_copy(d[:, :], b[:, :]),
                            inc=sem2)
                    else:
                        p.o(eng2, lambda E, d=dstb, b=bank: E.copy(d[:, :], b[:, :]), inc=sem2)
                else:
                    cc_ = kb - 4 * qb
                    if cc_ > 0:
                        p.o(eng2, lambda E, d=dstb, cc_=cc_: E.memset(d[:, 0:cc_ * 128], 0.0))
                    p.o(eng2, lambda E, d=dstb, b=bank, cc_=cc_: E.tensor_mul(
                        d[:, cc_ * 128:(cc_ + 1) * 128], b[:, cc_ * 128:(cc_ + 1) * 128],
                        umask[:, :]), inc=sem2 if cc_ == 3 else None)
                    if cc_ < 3:
                        p.o(eng2, lambda E, d=dstb, b=bank, cc_=cc_: E.tensor_copy(
                            d[:, (cc_ + 1) * 128:512], b[:, (cc_ + 1) * 128:512]), inc=sem2)
                st_evac[(qb, kb)] = (sem2, p.c(sem2))

            def emit_sv(kb, qb=qb, nkb=nkb):
                s_, v_ = st_evac[(qb, kb)]
                p.w("te", s_, v_)
                if kb == 0 and qb > 0:
                    ev, ea = out_evac[qb - 1]
                    p.w("te", "VE", ev)
                    p.w("te", "AC", ea)
                elif kb == 0 and prev_head_out_evac is not None:
                    ev, ea = prev_head_out_evac
                    p.w("te", "VE", ev)
                    p.w("te", "AC", ea)
                for e in range(2):
                    p.o("te", lambda E, kb=kb, e=e, nkb=nkb: E.matmul(
                        PS[4 + e][:, :], vh[:, kb * HDV + e * 128: kb * HDV + e * 128 + 128],
                        stb(kb % 2)[:, :],
                        start=(kb == 0), stop=(kb == nkb - 1)),
                        inc="PE" if e == 1 else None)
                sv_pe[(qb, kb)] = p.c("PE")

            emit_st(0)
            if nkb > 1:
                emit_st(1)
            for kb in range(nkb):
                emit_sv(kb)
                if kb + 2 < nkb:
                    emit_st(kb + 2)
            ofm_stage = bf6[:, 2 * A6_OFM: 2 * A6_OFM + 1024]
            last_pe = sv_pe[(qb, nkb - 1)]
            if qb >= 1:
                s_, v_ = ofm_tp[(qb - 1) % 2]
                p.w("ve", s_, v_)
                p.w("sc", s_, v_)
            elif 3 % 2 in ofm_tp and h > 0:
                s_, v_ = ofm_tp[3 % 2]
                p.w("ve", s_, v_)
                p.w("sc", s_, v_)
            p.w("ve", "PE", last_pe)
            p.o("ve", lambda E, o=ofm_stage: E.tensor_copy(o[:, 0:512], PS[4][:, :]), inc="VE")
            p.w("sc", "PE", last_pe)
            p.o("sc", lambda E, o=ofm_stage: E.copy(o[:, 512:1024], PS[5][:, :]), inc="AC")
            out_evac[qb] = (p.c("VE"), p.c("AC"))
            qeng, qsem = "sy", "SY"
            p.w(qeng, "VE", out_evac[qb][0])
            p.w(qeng, "AC", out_evac[qb][1])
            if qb >= 1:
                p.w(qeng, "VE", gn_done[qb - 1])
            elif prev_head_gn_done is not None:
                p.w(qeng, "VE", prev_head_gn_done)
            for tl in range(4):
                for e in range(2):
                    p.dmat(qeng, qsem, otm_st[:, tl * 256 + e * 128: tl * 256 + e * 128 + 128],
                           ofm_stage[:, e * 512 + tl * 128: e * 512 + tl * 128 + 128])
            ofm_tp[qb % 2] = (qsem, p.c(qsem))
            otm_ready = (qsem, p.c(qsem))

            for tl in range(4):
                tt = 4 * qb + tl
                bank = PS[tl % 2]
                if qb == 0 and tl < 2:
                    p.w("te", "AC", evac_ac[14 + tl])
                elif tl < 2:
                    p.w("te", "AC", gate_ac[(qb - 1, 2 + tl)])
                else:
                    p.w("te", "AC", gate_ac[(qb, tl - 2)])
                for ic in range(NIC):
                    p.o("te", lambda E, b=bank, ic=ic, tt=tt: E.matmul(
                        b[:, 0:256], hfm(ic)[:, tt * 128:(tt + 1) * 128],
                        gatew[:, ic * 256:(ic + 1) * 256],
                        start=(ic == 0), stop=(ic == NIC - 1)),
                        inc="PE" if ic == NIC - 1 else None)
                gmm = p.c("PE")
                p.w("sc", "PE", gmm)
                if qb >= 1:
                    p.w("sc", "VE", gn_done[qb - 1])
                elif prev_head_gn_done is not None:
                    p.w("sc", "VE", prev_head_gn_done)
                p.o("sc", lambda E, b=bank, tl=tl: E.activation(
                    gt_st[:, tl * 256:(tl + 1) * 256], b[:, 0:256], AF.Silu), inc="AC")
                gate_ac[(qb, tl)] = p.c("AC")

            p.w("ve", otm_ready[0], otm_ready[1])
            for tl in range(4):
                ot = otm_st[:, tl * 256:(tl + 1) * 256]
                p.o("ve", lambda E, ot=ot: E.bn_stats(st6[:, 0:6], ot), inc="VE")
                p.w("ve", "VE")
                p.o("ve", lambda E, tl=tl: E.bn_aggr(mqr[:, tl * 2: tl * 2 + 2], st6[:, 0:6]),
                    inc="VE")
                p.w("sc", "VE")
                p.o("sc", lambda E, tl=tl: E.activation(mqr[:, tl * 2 + 1: tl * 2 + 2],
                                                        mqr[:, tl * 2 + 1: tl * 2 + 2],
                                                        AF.Ln, bias=ccol[:, 0:1]), inc="AC")
                p.w("sc", "AC")
                p.o("sc", lambda E, tl=tl: E.activation(mqr[:, tl * 2 + 1: tl * 2 + 2],
                                                        mqr[:, tl * 2 + 1: tl * 2 + 2],
                                                        AF.Exp, scale=ccol[:, 1:2]), inc="AC")
                p.w("ve", "AC")
                p.w("ve", "AC", gate_ac[(qb, tl)])
                gnp = gn_st(qb % 2)
                if tl == 0 and (qb % 2) in gn_tp and (qb >= 2 or h > 0):
                    s_, v_ = gn_tp[qb % 2]
                    p.w("ve", s_, v_)
                p.o("ve", lambda E, ot=ot, tl=tl, g=gnp: E.tensor_scalar(
                    g[:, tl * 256:(tl + 1) * 256], ot, mqr[:, tl * 2: tl * 2 + 1],
                    mqr[:, tl * 2 + 1: tl * 2 + 2], op0=ALU.subtract, op1=ALU.mult))
                p.o("ve", lambda E, tl=tl, g=gnp: E.tensor_mul(
                    g[:, tl * 256:(tl + 1) * 256], g[:, tl * 256:(tl + 1) * 256],
                    gt_st[:, tl * 256:(tl + 1) * 256]), inc="VE")
            gn_done[qb] = p.c("VE")
            qeng, qsem = "sy", "SY"
            p.w(qeng, "VE", gn_done[qb])
            for tl in range(4):
                tt = 4 * qb + tl
                for e in range(2):
                    fc = 2 * h + e
                    p.dmat(qeng, qsem,
                           AR3[:, fc * T + tt * 128: fc * T + (tt + 1) * 128],
                           gn_st(qb % 2)[:, tl * 256 + e * 128: tl * 256 + e * 128 + 128])
            gn_tp[qb % 2] = (qsem, p.c(qsem))
        head_done_pe = p.c("PE")
        prev_head_out_evac = out_evac[3]
        prev_head_gn_done = gn_done[3]

    gnfm_sy, gnfm_sc = p.c("SY"), p.c("SC")
    if "gnfm" in tap_ap:
        for fc in range(16):
            for half in range(2):
                p.w("ve", "SY", gnfm_sy)
                p.w("ve", "SC", gnfm_sc)
                p.w("ve", "ST", p.c("ST"))
                p.o("ve", lambda E, fc=fc, half=half: E.tensor_copy(
                    scr(0), AR3[:, fc * T + half * 1024: fc * T + (half + 1) * 1024]), inc="VE")
                p.w("sy", "VE")
                p.dma("sy", "ST", tap_ap["gnfm"][fc * 128:(fc + 1) * 128,
                                                 half * 1024:(half + 1) * 1024], scr(0))

    p.w("sy", "PE", head_done_pe)
    p.w("sy", "MC", misc_cast)
    p.w("sy", "VE", p.c("VE"))
    p.w("sy", "PO", p.c("PO"))
    p.w("sy", "AC", p.c("AC"))
    p.w("sy", "SC", p.c("SC"))
    p.w("sy", "SY", p.c("SY"))
    for q4 in range(4):
        p.dma("sy", "WQ", AR4[:, q4 * 4096:(q4 + 1) * 4096].rearrange("p (fc c) -> p fc c", c=1024),
              wprj_c.ap().rearrange("(fc p) o -> p fc o", p=128)[:, q4 * 4:(q4 + 1) * 4, :])
    prj_ready = p.c("WQ")
    prjw = lambda fc: AR4[:, fc * 1024:(fc + 1) * 1024]

    p.w("te", "WQ", prj_ready)
    p.w("te", "SY", gnfm_sy)
    p.w("te", "SC", gnfm_sc)
    prev_evac = {}
    x1_ve = {}
    for tt in range(NT):
        for ob in range(2):
            k2 = tt * 2 + ob
            bank = PS[2 + k2 % 2]
            if k2 >= 2:
                p.w("te", "VE", prev_evac[k2 % 2])
            for fc in range(16):
                p.o("te", lambda E, b=bank, fc=fc, tt=tt, ob=ob: E.matmul(
                    b[:, :], AR3[:, fc * T + tt * 128: fc * T + (tt + 1) * 128],
                    prjw(fc)[:, ob * 512:(ob + 1) * 512],
                    start=(fc == 0), stop=(fc == 15)),
                    inc="PE" if fc == 15 else None)
            mmv = p.c("PE")
            p.w("ve", "PE", mmv)
            p.o("ve", lambda E, b=bank, ob=ob: E.tensor_copy(
                scr(0)[:, ob * 512:(ob + 1) * 512], b[:, :]), inc="VE")
            prev_evac[k2 % 2] = p.c("VE")
        p.o("ve", lambda E: E.bn_stats(st6[:, 0:6], scr(0)[:, 0:512]))
        p.o("ve", lambda E: E.bn_stats(st6[:, 6:12], scr(0)[:, 512:1024]), inc="VE")
        p.w("ve", "VE")
        p.o("ve", lambda E: E.bn_aggr(mv2[:, 0:2], st6[:, 0:12]), inc="VE")
        p.w("sc", "VE")
        p.o("sc", lambda E: E.activation(mv2[:, 2:3], mv2[:, 1:2], AF.Ln, bias=ccol[:, 0:1]), inc="AC")
        p.w("sc", "AC")
        p.o("sc", lambda E: E.activation(mv2[:, 2:3], mv2[:, 2:3], AF.Exp, scale=ccol[:, 1:2]), inc="AC")
        p.w("ve", "AC")
        p.o("ve", lambda E: E.tensor_scalar(scr(1), scr(0), mv2[:, 0:1], mv2[:, 2:3],
                                            op0=ALU.subtract, op1=ALU.mult))
        xt = AR1[:, tt * D:(tt + 1) * D]
        p.o("ve", lambda E, xt=xt: E.tensor_add(xt, xt, scr(1)), inc="VE")
        x1_ve[tt] = p.c("VE")
    x1_done = p.c("VE")

    tap_tm("x1", lambda tt: AR1[:, tt * D:(tt + 1) * D], "VE", x1_done)
    if stage <= 2:
        finish(p, nc, ctx)
        return nc

    p.w("sy", "VE", x1_done)
    p.dma("sy", "SY", ag_in.ap().rearrange("(tile p) d -> p tile d", p=128),
          AR1[:, :].rearrange("p (tile d) -> p tile d", d=D))
    agin_done = p.c("SY")
    p.w("gp", "SY", agin_done)
    p.o("gp", lambda E: E.collective_compute(
        "AllGather", ALU.bypass, replica_groups=[list(range(NCORES))],
        ins=[ag_in.ap().opt()], outs=[ag_out.ap().opt()]), inc="CC")
    p.w("gp", "CC", p.c("CC"))
    p.w("gp", "SY", misc_loaded)
    for tt in range(NT):
        p.o("gp", lambda E, tt=tt: E.indirect_dma_start(
            AR1[:, tt * D:(tt + 1) * D], None,
            ag_out.ap(), IndirectOffsetOnAxis(gidx[:, tt:tt + 1], 0)), inc="GPI", n=16)
    p.dma("gp", "GP", ag_in.ap()[0:1, :], AR1[:, 0:8].bitcast(F32)[0:1, 0:4]) if False else None
    p.dma("gp", "GP", ag_in.ap()[0:1, 0:1024], AR1[0:1, 0:2048].bitcast(F32))
    x1b_done = p.c("GP")
    tap_tm("x1b", lambda tt: AR1[:, tt * D:(tt + 1) * D], "GP", x1b_done)

    p.w("ve", "GP", x1b_done)
    bst = lambda par: AR3[:, 30720 + par * 1024: 30720 + (par + 1) * 1024]
    xln_tp = {}
    for tt in range(NT):
        xt = AR1[:, tt * D:(tt + 1) * D]
        p.o("ve", lambda E, xt=xt: E.bn_stats(st6[:, 0:6], xt[:, 0:512]))
        p.o("ve", lambda E, xt=xt: E.bn_stats(st6[:, 6:12], xt[:, 512:1024]), inc="VE")
        p.w("ve", "VE")
        p.o("ve", lambda E, tt=tt: E.bn_aggr(xstat[:, tt * 2: tt * 2 + 2], st6[:, 0:12]),
            inc="VE" if tt == NT - 1 else None)
    bstats_done = p.c("VE")
    p.w("sc", "VE", bstats_done)
    for tt in range(NT):
        p.o("sc", lambda E, tt=tt: E.activation(xstat[:, tt * 2 + 1: tt * 2 + 2],
                                                xstat[:, tt * 2 + 1: tt * 2 + 2], AF.Ln,
                                                bias=ccol[:, 0:1]), inc="AC")
    p.w("sc", "AC")
    for tt in range(NT):
        p.o("sc", lambda E, tt=tt: E.activation(xstat[:, tt * 2 + 1: tt * 2 + 2],
                                                xstat[:, tt * 2 + 1: tt * 2 + 2], AF.Exp,
                                                scale=ccol[:, 1:2]),
            inc="AC" if tt == NT - 1 else None)
    brstd_done = p.c("AC")
    p.w("ve", "AC", brstd_done)
    for tt in range(NT):
        xt = AR1[:, tt * D:(tt + 1) * D]
        if tt >= 2:
            s_, v_ = xln_tp[tt % 2]
            p.w("ve", s_, v_)
        p.o("ve", lambda E, xt=xt, tt=tt: E.tensor_scalar(
            bst(tt % 2)[:, :], xt, xstat[:, tt * 2: tt * 2 + 1],
            xstat[:, tt * 2 + 1: tt * 2 + 2], op0=ALU.subtract, op1=ALU.mult), inc="VE")
        hdone = p.c("VE")
        qeng, qsem = "sy", "SY"
        p.w(qeng, "VE", hdone)
        for ic in range(NIC):
            p.dmat(qeng, qsem, AR2[:, ic * T + tt * 128: ic * T + (tt + 1) * 128],
                   bst(tt % 2)[:, ic * 128:(ic + 1) * 128])
        xln_tp[tt % 2] = (qsem, p.c(qsem))
    xln_sy, xln_sc = p.c("SY"), p.c("SC")
    xlnfm = lambda ic: AR2[:, ic * T:(ic + 1) * T]
    if "xlnfm" in tap_ap:
        for ic in range(NIC):
            for half in range(2):
                p.w("ve", "SY", xln_sy)
                p.w("ve", "ST", p.c("ST"))
                p.o("ve", lambda E, ic=ic, half=half: E.tensor_copy(
                    scr(0), xlnfm(ic)[:, half * 1024:(half + 1) * 1024]), inc="VE")
                p.w("sy", "VE")
                p.dma("sy", "ST", tap_ap["xlnfm"][ic * 128:(ic + 1) * 128,
                                                  half * 1024:(half + 1) * 1024], scr(0))

    ipw_par = lambda par: AR7[:, B7_IPW + par * 4096: B7_IPW + (par + 1) * 4096]
    ipsem = ("IP0", "IP1")
    ip_loaded = {}

    def load_ip(src_c, ob, pe_gate, bias_off=None):
        par = ob % 2
        p.w("sy", "PE", pe_gate)
        p.w("sy", "MC", misc_cast)
        p.w("sy", "GP", small_loaded)
        p.dma("sy", ipsem[par], ipw_par(par)[:, :].rearrange("p (ic c) -> p ic c", c=512),
              src_c.ap().rearrange("(ic p) o -> p ic o", p=128)[:, :, ob * 512:(ob + 1) * 512])
        if bias_off is not None:
            p.dma("sy", ipsem[par], brow[:, par * 1024: par * 1024 + 512],
                  bias_c.ap()[:, bias_off: bias_off + 512])
        return p.c(ipsem[par])

    ip_loaded[0] = load_ip(ipw_c, 0, 0, bias_off=0)
    ip_loaded[1] = load_ip(ipw_c, 1, 0, bias_off=512)
    ip_consumed = {}
    ip_evac = {}
    for ob in range(6):
        par = ob % 2
        p.w("te", ipsem[par], ip_loaded[ob])
        if ob == 0:
            p.w("te", "SY", xln_sy)
            p.w("te", "SC", xln_sc)
            p.w("te", "GP", small_loaded)
        for tt in range(NT):
            k2 = tt % 2
            bank = PS[4 + k2]
            if (ob * NT + tt) >= 2:
                p.w("te", "VE", ip_evac[k2])
            p.o("te", lambda E, b=bank, par=par: E.matmul(
                b[:, :], ones_b[:, :], brow[:, par * 1024: par * 1024 + 512], start=True, stop=False))
            for ic in range(NIC):
                p.o("te", lambda E, b=bank, ic=ic, tt=tt, par=par: E.matmul(
                    b[:, :], xlnfm(ic)[:, tt * 128:(tt + 1) * 128],
                    ipw_par(par)[:, ic * 512:(ic + 1) * 512],
                    start=False, stop=(ic == NIC - 1)),
                    inc="PE" if ic == NIC - 1 else None)
            mmv = p.c("PE")
            p.w("ve", "PE", mmv)
            if ob < 2:
                dst = AR3[:, tt * 1024 + ob * 512: tt * 1024 + (ob + 1) * 512]
                p.o("ve", lambda E, b=bank, d=dst: E.tensor_scalar_mul(d, b[:, :], 0.125),
                    inc="VE")
            elif ob < 4:
                dst = AR3[:, 16384 + tt * 1024 + (ob - 2) * 512: 16384 + tt * 1024 + (ob - 1) * 512]
                p.o("ve", lambda E, b=bank, d=dst: E.tensor_copy(d, b[:, :]), inc="VE")
            else:
                dst = AR4[:, tt * 1024 + (ob - 4) * 512: tt * 1024 + (ob - 3) * 512]
                p.o("ve", lambda E, b=bank, d=dst: E.tensor_copy(d, b[:, :]), inc="VE")
            ip_evac[k2] = p.c("VE")
        ip_consumed[ob] = p.c("PE")
        if ob + 2 < 6:
            ip_loaded[ob + 2] = load_ip(ipw_c, ob + 2, ip_consumed[ob], bias_off=(ob + 2) * 512)
    qk_done = p.c("VE")

    if "qk" in tap_ap:
        for tt in range(NT):
            p.w("ve", "VE", qk_done)
            p.w("ve", "ST", p.c("ST"))
            p.o("ve", lambda E, tt=tt: E.tensor_copy(scr(0), AR3[:, tt * 1024:(tt + 1) * 1024]),
                inc="VE")
            p.w("sy", "VE")
            p.dma("sy", "ST", tap_ap["qk"][tt * 128:(tt + 1) * 128, 0:1024], scr(0))
            p.w("ve", "ST", p.c("ST"))
            p.o("ve", lambda E, tt=tt: E.tensor_copy(
                scr(0), AR3[:, 16384 + tt * 1024: 16384 + (tt + 1) * 1024]), inc="VE")
            p.w("sy", "VE")
            p.dma("sy", "ST", tap_ap["qk"][tt * 128:(tt + 1) * 128, 1024:2048], scr(0))

    mbuf = bf6[:, 2 * (A6_SCR + 1024): 2 * (A6_SCR + 1024) + 1024]
    mbuf3 = mbuf.rearrange("p (h c) -> p h c", c=64)
    p.w("ve", "VE", qk_done)
    sc_done = {}
    for s in range(NH):
        for hf in range(2):
            qtile = AR3[:, (s * 2 + hf) * 1024: (s * 2 + hf + 1) * 1024]
            for t in range(NH):
                ktile = AR3[:, 16384 + (t * 2 + hf) * 1024: 16384 + (t * 2 + hf + 1) * 1024]
                p.o("ve", lambda E, q=qtile, k=ktile: E.tensor_mul(mbuf[:, :], q, k))
                dst = scr(0)[:, hf * 128: hf * 128 + 128].rearrange(
                    "p (h t8) -> p h t8", t8=8)[:, :, t:t + 1]
                p.o("ve", lambda E, d=dst: E.tensor_reduce(d, mbuf3, axis=AX.X, op=ALU.add))
        for hf in range(2):
            sc3 = scr(0)[:, hf * 128: hf * 128 + 128].rearrange("p (h t8) -> p h t8", t8=8)
            mx = scr(1)[:, 992 + hf * 16: 992 + hf * 16 + 16]
            p.o("ve", lambda E, a=mx, b=sc3: E.tensor_reduce(
                a.unsqueeze(2), b, axis=AX.X, op=ALU.max), inc="VE")
            p.w("ve", "VE")
            p.o("ve", lambda E, a=sc3, m=mx: E.tensor_sub(
                a, a, m.unsqueeze(2).broadcast_to([128, 16, 8])), inc="VE")
            p.w("sc", "VE")
            flat = scr(0)[:, hf * 128: hf * 128 + 128]
            p.o("sc", lambda E, f=flat: E.activation(f, f, AF.Exp), inc="AC")
            p.w("ve", "AC")
            p.o("ve", lambda E, a=mx, b=sc3: E.tensor_reduce(
                a.unsqueeze(2), b, axis=AX.X, op=ALU.add), inc="VE")
            p.w("ve", "VE")
            p.o("ve", lambda E, a=mx: E.reciprocal(a, a), inc="VE")
            p.w("ve", "VE")
            dstw = attw[:, s * 256 + hf * 128: s * 256 + hf * 128 + 128].rearrange(
                "p (h t8) -> p h t8", t8=8)
            p.o("ve", lambda E, d=dstw, b=sc3, m=mx: E.tensor_tensor(
                d, b, m.unsqueeze(2).broadcast_to([128, 16, 8]), op=ALU.mult),
                inc="VE")
        sc_done[s] = p.c("VE")
    scores_all = p.c("VE")
    ctx_done = {}
    p.w("ve", "VE", scores_all)
    for s in range(NH):
        for hf in range(2):
            dstc = AR3[:, (s * 2 + hf) * 1024: (s * 2 + hf + 1) * 1024]
            d3 = dstc.rearrange("p (h c) -> p h c", c=64)
            aw = attw[:, s * 256 + hf * 128: s * 256 + hf * 128 + 128].rearrange(
                "p (h t8) -> p h t8", t8=8)
            for t in range(NH):
                vtile = AR4[:, (t * 2 + hf) * 1024: (t * 2 + hf + 1) * 1024]
                v3 = vtile.rearrange("p (h c) -> p h c", c=64)
                awb = aw[:, :, t:t + 1].broadcast_to([128, 16, 64])
                if t == 0:
                    p.o("ve", lambda E, d=d3, v=v3, a=awb: E.tensor_tensor(d, v, a, op=ALU.mult))
                else:
                    p.o("ve", lambda E, v=v3, a=awb: E.tensor_tensor(mbuf3, v, a, op=ALU.mult))
                    p.o("ve", lambda E, d=d3: E.tensor_add(d, d, mbuf3))
            p.o("ve", lambda E: E.engine_nop(), inc="VE")
        ctx_done[s] = p.c("VE")
    for s in range(NH):
        for hf in range(2):
            tt = s * 2 + hf
            qeng, qsem = "sy", "SY"
            p.w(qeng, "VE", ctx_done[s])
            for ic in range(NIC):
                p.dmat(qeng, qsem, AR2[:, ic * T + tt * 128: ic * T + (tt + 1) * 128],
                       AR3[:, tt * 1024 + ic * 128: tt * 1024 + ic * 128 + 128])
    ctxfm_sy, ctxfm_sc = p.c("SY"), p.c("SC")
    moe_cast = []
    p.w("gp", "SY", ctxfm_sy)
    for e in range(NE):
        if not no_moe_cast:
            p.dma("gp", "MC", w1_c.ap()[e, :, :], w1pre_in[e, :, :])
            p.dma("gp", "MC", w2_c.ap()[e, :, :], w2pre_in[e, :, :])
        moe_cast.append(p.c("MC"))
    tap_tm("ctx", lambda tt: AR3[:, tt * 1024:(tt + 1) * 1024], "VE", ctx_done[NH - 1])

    ow_loaded = {ob: load_ip(outw_c, ob, ip_consumed[5], bias_off=3072 + ob * 512) for ob in range(2)}
    p.w("te", "SY", ctxfm_sy)
    p.w("te", "SC", ctxfm_sc)
    ao_evac = {}
    zfm_tp = {}
    xmid_ve = {}
    for tt in range(NT):
        for ob in range(2):
            bank = PS[4 + ob]
            p.w("te", ipsem[ob], ow_loaded[ob])
            if tt >= 1:
                p.w("te", "VE", ao_evac[ob])
            p.o("te", lambda E, b=bank, ob=ob: E.matmul(
                b[:, :], ones_b[:, :], brow[:, ob * 1024: ob * 1024 + 512],
                start=True, stop=False))
            for ic in range(NIC):
                p.o("te", lambda E, b=bank, ic=ic, tt=tt, ob=ob: E.matmul(
                    b[:, :], AR2[:, ic * T + tt * 128: ic * T + (tt + 1) * 128],
                    ipw_par(ob)[:, ic * 512:(ic + 1) * 512],
                    start=False, stop=(ic == NIC - 1)),
                    inc="PE" if ic == NIC - 1 else None)
            mmv = p.c("PE")
            p.w("ve", "PE", mmv)
            p.o("ve", lambda E, b=bank, ob=ob: E.tensor_copy(
                scr(0)[:, ob * 512:(ob + 1) * 512], b[:, :]), inc="VE")
            ao_evac[ob] = p.c("VE")
        xt = AR1[:, tt * D:(tt + 1) * D]
        p.o("ve", lambda E, xt=xt, tt=tt: E.tensor_scalar(
            scr(1), xt, xstat[:, tt * 2: tt * 2 + 1], xstat[:, tt * 2 + 1: tt * 2 + 2],
            op0=ALU.subtract, op1=ALU.mult))
        p.o("ve", lambda E: E.tensor_add(scr(0), scr(0), scr(1)))
        p.o("ve", lambda E: E.bn_stats(st6[:, 0:6], scr(0)[:, 0:512]))
        p.o("ve", lambda E: E.bn_stats(st6[:, 6:12], scr(0)[:, 512:1024]), inc="VE")
        p.w("ve", "VE")
        p.o("ve", lambda E: E.bn_aggr(mv2[:, 0:2], st6[:, 0:12]), inc="VE")
        p.w("sc", "VE")
        p.o("sc", lambda E: E.activation(mv2[:, 2:3], mv2[:, 1:2], AF.Ln, bias=ccol[:, 0:1]), inc="AC")
        p.w("sc", "AC")
        p.o("sc", lambda E: E.activation(mv2[:, 2:3], mv2[:, 2:3], AF.Exp, scale=ccol[:, 1:2]), inc="AC")
        p.w("ve", "AC")
        p.o("ve", lambda E: E.tensor_scalar(scr(1), scr(0), mv2[:, 0:1], mv2[:, 2:3],
                                            op0=ALU.subtract, op1=ALU.mult))
        p.o("ve", lambda E: E.tensor_mul(scr(1), scr(1), g1t))
        xmb = AR3[:, tt * D:(tt + 1) * D]
        p.o("ve", lambda E, d=xmb: E.tensor_tensor(d, scr(1), b1t, op=ALU.add), inc="VE")
        xmid_ve[tt] = p.c("VE")
        qeng, qsem = "sy", "SY"
        p.w(qeng, "VE", xmid_ve[tt])
        for ic in range(NIC):
            p.dmat(qeng, qsem, AR4[:, ic * T + tt * 128: ic * T + (tt + 1) * 128],
                   xmb[:, ic * 128:(ic + 1) * 128])
        zfm_tp[tt] = (qsem, p.c(qsem))
    zfm_sy, zfm_sc = p.c("SY"), p.c("SC")
    zfm = lambda ic: AR4[:, ic * T:(ic + 1) * T]
    xmid_done = p.c("VE")
    tap_tm("xmid", lambda tt: AR3[:, tt * D:(tt + 1) * D], "VE", xmid_done)
    if stage <= 3:
        finish(p, nc, ctx)
        return nc

    p.w("te", "SY", zfm_sy)
    p.w("te", "SC", zfm_sc)
    gat_evac = {}
    for tt in range(NT):
        bank = PS[tt % 2]
        if tt >= 2:
            p.w("te", "VE", gat_evac[tt % 2])
        p.o("te", lambda E, b=bank: E.matmul(b[:, 0:NE], ones_b[:, :], gateb_r[:, :],
                                             start=True, stop=False))
        for ic in range(NIC):
            p.o("te", lambda E, b=bank, ic=ic, tt=tt: E.matmul(
                b[:, 0:NE], zfm(ic)[:, tt * 128:(tt + 1) * 128], gwsb[:, ic * NE:(ic + 1) * NE],
                start=False, stop=(ic == NIC - 1)),
                inc="PE" if ic == NIC - 1 else None)
        mmv = p.c("PE")
        p.w("ve", "PE", mmv)
        pr = scr(0)[:, 0:NE]
        m1 = scr(0)[:, 8:16]
        v1 = scr(0)[:, 16:17]
        v2 = scr(0)[:, 17:18]
        def ev(fn):
            p.o("ve", fn, inc="VE")
            p.w("ve", "VE")
        ev(lambda E, b=bank, pr=pr: E.tensor_copy(pr, b[:, 0:NE]))
        ev(lambda E, pr=pr, v1=v1: E.tensor_reduce(v1, pr, axis=AX.X, op=ALU.max))
        p.o("ve", lambda E, pr=pr, v1=v1: E.tensor_single_scalar(pr, pr, v1[:, 0:1],
                                                                 op=ALU.subtract), inc="VE")
        p.w("sc", "VE")
        p.o("sc", lambda E, pr=pr: E.activation(pr, pr, AF.Exp), inc="AC")
        p.w("ve", "AC")
        ev(lambda E, pr=pr, v1=v1: E.tensor_reduce(v1, pr, axis=AX.X, op=ALU.add))
        ev(lambda E, v1=v1: E.reciprocal(v1, v1))
        ev(lambda E, pr=pr, v1=v1: E.tensor_scalar_mul(pr, pr, v1[:, 0:1]))
        ev(lambda E, pr=pr, v1=v1: E.tensor_reduce(v1, pr, axis=AX.X, op=ALU.max))
        ev(lambda E, pr=pr, m1=m1, v1=v1: E.tensor_single_scalar(m1, pr, v1[:, 0:1],
                                                                 op=ALU.is_ge))
        ev(lambda E, pr=pr, m1=m1: E.scalar_tensor_tensor(m1, m1, LNEG, pr,
                                                          op0=ALU.mult, op1=ALU.add))
        ev(lambda E, m1=m1, v2=v2: E.tensor_reduce(v2, m1, axis=AX.X, op=ALU.max))
        ev(lambda E, pr=pr, m1=m1, v2=v2: E.tensor_single_scalar(m1, pr, v2[:, 0:1],
                                                                 op=ALU.is_ge))
        ev(lambda E, pr=pr, m1=m1: E.tensor_mul(pr, pr, m1))
        ev(lambda E, pr=pr, v1=v1: E.tensor_reduce(v1, pr, axis=AX.X, op=ALU.add))
        ev(lambda E, v1=v1: E.reciprocal(v1, v1))
        p.o("ve", lambda E, pr=pr, v1=v1, tt=tt: E.tensor_scalar_mul(
            comb[:, tt * NE:(tt + 1) * NE], pr, v1[:, 0:1]), inc="VE")
        gat_evac[tt % 2] = p.c("VE")
    comb_done = p.c("VE")
    tap_tm("comb", lambda tt: comb[:, tt * NE:(tt + 1) * NE], "VE", comb_done, width=NE)
    if stage == 35:
        finish(p, nc, ctx)
        return nc

    w1buf = lambda par: AR7[:, M7_W1 + (par % 4) * 1024: M7_W1 + (par % 4 + 1) * 1024]
    w2buf = lambda par: AR7[:, M7_W2 + (par % 4) * 1024: M7_W2 + (par % 4 + 1) * 1024]
    acc_par = lambda tb: f32_AR3[:, 8192 + (tb % 2) * 4096: 8192 + (tb % 2 + 1) * 4096]
    h1buf = AR2
    w1sems = ("W10", "W11", "W12", "W13")
    w2sems = ("W20", "W21", "W22", "W23")
    out_stage = lambda par: scr(par)
    st_cnt = 0
    acc_consumed = {}
    y_evac = 0

    def load_w1(e, fc, pe_gate):
        par = fc % 4
        p.w("sy", "PE", pe_gate)
        p.w("sy", "MC", moe_cast[e])
        p.dma("sy", w1sems[par], w1buf(par)[:, :],
              w1_c.ap()[e, :, :].rearrange("(fc p) c -> fc p c", p=128)[fc, :, :])
        return p.c(w1sems[par])

    def load_w2(e, fc, pe_gate):
        par = fc % 4
        p.w("sy", "PE", pe_gate)
        p.w("sy", "MC", moe_cast[e])
        p.dma("sy", w2sems[par], w2buf(par)[:, :],
              w2_c.ap()[e, fc * 128:(fc + 1) * 128, :])
        return p.c(w2sems[par])

    w1_ld = {i: load_w1(0, i, 0) for i in range(4)}
    w2_ld = {i: load_w2(0, i, 0) for i in range(4)}
    if stage == 41:
        for s_ in ("W10", "W11", "W12", "W13", "W20", "W21", "W22", "W23", "MC"):
            p.w("sy", s_, p.c(s_))
        finish(p, nc, ctx)
        return nc
    h1_evac = {}
    first_iter = True
    epi_pending = None
    for tb in range(4):
        acc_f32 = acc_par(tb)
        p.w("ve", "VE", comb_done)
        if tb > 1:
            p.w("ve", "VE", acc_consumed[tb % 2])
        for tc in range(4):
            tt = tb * 4 + tc
            p.o("ve", lambda E, tt=tt, tc=tc, acc_f32=acc_f32: E.tensor_copy(
                acc_f32[:, tc * 1024:(tc + 1) * 1024], AR3[:, tt * D:(tt + 1) * D]), inc="VE")
        acc_init = p.c("VE")
        for e in range(NE):
            if first_iter:
                p.w("te", "SY", zfm_sy)
                p.w("te", "SC", zfm_sc)
                p.w("te", "VE", comb_done)
            else:
                p.w("te", "VE", y_evac01)
            for fc in range(NFC):
                par = fc % 4
                bpar = fc % 2
                p.w("te", w1sems[par], w1_ld[fc])
                if fc >= 2:
                    s_, v_ = h1_evac[bpar]
                    p.w("te", s_, v_)
                bank = PS[bpar]
                for ic in range(NIC):
                    p.o("te", lambda E, b=bank, ic=ic, tb=tb, par=par: E.matmul(
                        b[:, :], w1buf(par)[:, ic * 128:(ic + 1) * 128],
                        zfm(ic)[:, tb * 512:(tb + 1) * 512],
                        start=(ic == 0), stop=(ic == NIC - 1)),
                        inc="PE" if ic == NIC - 1 else None)
                w1_pe = p.c("PE")
                if fc + 4 < NFC:
                    w1_ld[fc + 4] = load_w1(e, fc + 4, w1_pe)
                else:
                    ne_, nt_ = (e + 1, tb) if e + 1 < NE else (0, tb + 1)
                    if nt_ < 4:
                        w1_ld[fc + 4 - NFC] = load_w1(ne_, fc + 4 - NFC, w1_pe)
                eng3, sem3 = ("sc", "AC") if fc % 2 == 0 else ("ve", "VE")
                p.w(eng3, "PE", w1_pe)
                if stage == 39:
                    if eng3 == "sc":
                        p.o(eng3, lambda E, b=bank, fc=fc: E.copy(
                            h1buf[:, fc * 512:(fc + 1) * 512], b[:, :]), inc=sem3)
                    else:
                        p.o(eng3, lambda E, b=bank, fc=fc: E.tensor_copy(
                            h1buf[:, fc * 512:(fc + 1) * 512], b[:, :]), inc=sem3)
                elif eng3 == "sc":
                    p.o(eng3, lambda E, b=bank, e=e, fc=fc: E.activation(
                        h1buf[:, fc * 512:(fc + 1) * 512], b[:, :], AF.Relu,
                        bias=b1c[:, e * NFC + fc: e * NFC + fc + 1]), inc=sem3)
                else:
                    p.o(eng3, lambda E, b=bank, e=e, fc=fc: E.tensor_scalar(
                        h1buf[:, fc * 512:(fc + 1) * 512], b[:, :],
                        b1c[:, e * NFC + fc: e * NFC + fc + 1], 0.0,
                        op0=ALU.add, op1=ALU.max), inc=sem3)
                h1_evac[bpar] = (sem3, p.c(sem3))
            h1_ve, h1_ac = p.c("VE"), p.c("AC")
            if stage in (38, 39) and e == 0:
                first_iter = False
                break
            p.w("sy", "PE", p.c("PE"))
            p.dma("sy", "WQ", brow[:, 0:1024], bias_c.ap()[:, 3072 + D + e * D: 3072 + D + (e + 1) * D])
            b2_ld = p.c("WQ")
            p.w("te", "WQ", b2_ld)
            p.w("te", "VE", h1_ve)
            p.w("te", "AC", h1_ac)
            if not first_iter:
                p.w("te", "VE", y_evac)
            if tb > 0 or e > 0:
                p.w("te", "VE", acc_init)
            for tc in range(4):
                for ob in range(2):
                    bank = PS[tc * 2 + ob]
                    p.o("te", lambda E, b=bank, ob=ob: E.matmul(
                        b[:, :], ones_b[:, :], brow[:, ob * 512:(ob + 1) * 512],
                        start=True, stop=False))
            for fc in range(NFC):
                par = fc % 4
                p.w("te", w2sems[par], w2_ld[fc])
                for tc in range(4):
                    for ob in range(2):
                        bank = PS[tc * 2 + ob]
                        p.o("te", lambda E, b=bank, fc=fc, tc=tc, ob=ob: E.matmul(
                            b[:, :], h1buf[:, fc * 512 + tc * 128: fc * 512 + tc * 128 + 128],
                            w2buf(fc % 4)[:, ob * 512:(ob + 1) * 512],
                            start=False, stop=(fc == NFC - 1)),
                            inc="PE" if (tc == 3 and ob == 1) else None)
                w2_pe = p.c("PE")
                if fc + 4 < NFC:
                    w2_ld[fc + 4] = load_w2(e, fc + 4, w2_pe)
                else:
                    ne_, nt_ = (e + 1, tb) if e + 1 < NE else (0, tb + 1)
                    if nt_ < 4:
                        w2_ld[fc + 4 - NFC] = load_w2(ne_, fc + 4 - NFC, w2_pe)
            y_done = p.c("PE")
            p.w("ve", "PE", y_done)
            p.w("ve", "VE", acc_init)
            for tc in range(4):
                tt = tb * 4 + tc
                for ob in range(2):
                    bank = PS[tc * 2 + ob]
                    dst = acc_f32[:, tc * 1024 + ob * 512: tc * 1024 + (ob + 1) * 512]
                    p.o("ve", lambda E, b=bank, d=dst, tt=tt, e=e: E.scalar_tensor_tensor(
                        d, b[:, :], comb[:, tt * NE + e: tt * NE + e + 1], d,
                        op0=ALU.mult, op1=ALU.add),
                        inc="VE" if (tc == 0 and ob == 1) or (tc == 3 and ob == 1) else None)
                if tc == 0:
                    y_evac01 = p.c("VE")
            y_evac = p.c("VE")
            first_iter = False
            if stage == 37 and e == 0:
                break
        if "accf" in tap_ap:
            for tc in range(4):
                tt = tb * 4 + tc
                p.w("ve", "VE", y_evac)
                p.w("ve", "ST", p.c("ST"))
                p.o("ve", lambda E, tc=tc: E.tensor_copy(scr(0), acc_f32[:, tc * 1024:(tc + 1) * 1024]), inc="VE")
                p.w("sy", "VE")
                p.dma("sy", "ST", tap_ap["accf"][tt * 128:(tt + 1) * 128, :], scr(0))
        if "h1f" in tap_ap and tb == 0:
            for fc in range(NFC):
                p.w("ve", "VE", y_evac)
                p.w("ve", "ST", p.c("ST"))
                p.o("ve", lambda E, fc=fc: E.tensor_copy(scr(0)[:, 0:512], h1buf[:, fc * 512:(fc + 1) * 512]), inc="VE")
                p.w("sy", "VE")
                p.dma("sy", "ST", tap_ap["h1f"][fc * 128:(fc + 1) * 128, :], scr(0)[:, 0:512])
        if stage in (36, 37, 38, 39) and tb == 0:
            break
        for tc in range(4):
            tt = tb * 4 + tc
            at = acc_f32[:, tc * 1024:(tc + 1) * 1024]
            p.w("ve", "VE", y_evac)
            p.o("ve", lambda E, at=at: E.bn_stats(st6[:, 0:6], at[:, 0:512]))
            p.o("ve", lambda E, at=at: E.bn_stats(st6[:, 6:12], at[:, 512:1024]), inc="VE")
            p.w("ve", "VE")
            p.o("ve", lambda E: E.bn_aggr(mv2[:, 0:2], st6[:, 0:12]), inc="VE")
            p.w("sc", "VE")
            p.o("sc", lambda E: E.activation(mv2[:, 2:3], mv2[:, 1:2], AF.Ln, bias=ccol[:, 0:1]), inc="AC")
            p.w("sc", "AC")
            p.o("sc", lambda E: E.activation(mv2[:, 2:3], mv2[:, 2:3], AF.Exp, scale=ccol[:, 1:2]),
                inc="AC")
            p.w("ve", "AC")
            if st_cnt >= 2:
                p.w("ve", "ST", 16 * (st_cnt - 1))
            osd = out_stage(tc % 2)
            p.o("ve", lambda E, at=at, o=osd: E.tensor_scalar(o, at, mv2[:, 0:1], mv2[:, 2:3],
                                                              op0=ALU.subtract, op1=ALU.mult))
            p.o("ve", lambda E, o=osd: E.tensor_mul(o, o, g2t))
            p.o("ve", lambda E, o=osd: E.tensor_tensor(o, o, b2t, op=ALU.add))
            p.o("ve", lambda E, o=osd: E.bn_stats(st6[:, 0:6], o[:, 0:512]))
            p.o("ve", lambda E, o=osd: E.bn_stats(st6[:, 6:12], o[:, 512:1024]), inc="VE")
            p.w("ve", "VE")
            p.o("ve", lambda E: E.bn_aggr(mv2[:, 0:2], st6[:, 0:12]), inc="VE")
            p.w("sc", "VE")
            p.o("sc", lambda E: E.activation(mv2[:, 2:3], mv2[:, 1:2], AF.Ln, bias=ccol[:, 0:1]), inc="AC")
            p.w("sc", "AC")
            p.o("sc", lambda E: E.activation(mv2[:, 2:3], mv2[:, 2:3], AF.Exp, scale=ccol[:, 1:2]),
                inc="AC")
            p.w("ve", "AC")
            p.o("ve", lambda E, o=osd: E.tensor_scalar(o, o, mv2[:, 0:1], mv2[:, 2:3],
                                                       op0=ALU.subtract, op1=ALU.mult))
            xt = AR1[:, tt * D:(tt + 1) * D]
            p.o("ve", lambda E, xt=xt, o=osd: E.scalar_tensor_tensor(
                o, xt, 1.0, o, op0=ALU.mult, op1=ALU.add), inc="VE")
            odone = p.c("VE")
            p.w("sy", "VE", odone)
            p.dma("sy", "ST", y_out[tt * 128:(tt + 1) * 128, :], osd)
            st_cnt += 1
        acc_consumed[tb % 2] = p.c("VE")
        if stage == 36:
            break

    finish(p, nc, ctx)
    return nc


def finish(p, nc, ctx):
    p.w("sy", "ST", p.c("ST"))
    with nc.Block() as block:
        @block.sync
        def _(E):
            for fn in p.q["sy"]:
                fn(E)

        @block.scalar
        def _(E):
            for fn in p.q["sc"]:
                fn(E)

        @block.vector
        def _(E):
            for fn in p.q["ve"]:
                fn(E)

        @block.gpsimd
        def _(E):
            for fn in p.q["gp"]:
                fn(E)

        @block.tensor
        def _(E):
            for fn in p.q["te"]:
                fn(E)
    ctx.close()


def host_prep(inputs):
    f = np.float32
    x = np.ascontiguousarray(inputs["L0_attn_hidden_states"], dtype=f)
    w_qkv = np.asarray(inputs["L0_attn_w_qkv"], dtype=f)
    wq, wk, wv = w_qkv[0:1024], w_qkv[1024:2048], w_qkv[2048:4096]
    groups = []
    for h in range(NH):
        groups += [wq[h * 128:(h + 1) * 128].T, wk[h * 128:(h + 1) * 128].T,
                   wv[h * 256:(h + 1) * 256].T]
    wqkv_g = np.ascontiguousarray(np.concatenate(groups, axis=1))
    wgat_t = np.ascontiguousarray(np.asarray(inputs["L0_attn_w_gated"], dtype=f).T)
    wprj_t = np.ascontiguousarray(np.asarray(inputs["L0_attn_w_proj"], dtype=f).T)
    decay_r = np.asarray(inputs["L0_attn_decay"], dtype=f).reshape(1, NH)
    Dm, SCALE_BASE = 1024, 512.0
    half = Dm // 2
    sv = (np.arange(0, Dm, 2, dtype=np.float64) + 0.4 * Dm) / (1.4 * Dm)
    pos = np.arange(T, dtype=np.float64)
    scale = sv[None, :] ** (pos[:, None] / SCALE_BASE)
    inv_freq = 1.0 / 10000.0 ** (np.arange(half, dtype=np.float64) / half)
    sinus = pos[:, None] * inv_freq[None, :]
    sin, cos = np.sin(sinus), np.cos(sinus)
    Cq = (cos * scale).astype(f)
    Sq = (sin * scale).astype(f)
    Ck = (cos / scale).astype(f)
    Sk = (sin / scale).astype(f)
    tab = np.empty((NH, NT, 128, 256), dtype=f)
    for h in range(NH):
        cols = slice(h * 64, (h + 1) * 64)
        for tt in range(NT):
            rows = slice(tt * 128, (tt + 1) * 128)
            tab[h, tt, :, 0:64] = Cq[rows, cols]
            tab[h, tt, :, 64:128] = Sq[rows, cols]
            tab[h, tt, :, 128:192] = Ck[rows, cols]
            tab[h, tt, :, 192:256] = Sk[rows, cols]
    umask = np.triu(np.ones((128, 128), dtype=f)).copy()
    ipw = np.ascontiguousarray(np.asarray(inputs["L0_ffn_in_proj_w"], dtype=f).T)
    ipb = np.asarray(inputs["L0_ffn_in_proj_b"], dtype=f).reshape(1, 3072)
    outw = np.ascontiguousarray(np.asarray(inputs["L0_ffn_out_w"], dtype=f).T)
    outb = np.asarray(inputs["L0_ffn_out_b"], dtype=f).reshape(1, D)
    gatew = np.ascontiguousarray(np.asarray(inputs["L0_ffn_gate_w"], dtype=f).T)
    gateb = np.asarray(inputs["L0_ffn_gate_b"], dtype=f).reshape(1, NE)
    w1 = np.asarray(inputs["L0_ffn_w1"], dtype=f)
    w1pre = np.ascontiguousarray(
        w1.reshape(NE, NFC, 128, NIC, 128).transpose(0, 1, 4, 3, 2).reshape(NE, NFC * 128, NIC * 128))
    b1 = np.asarray(inputs["L0_ffn_b1"], dtype=f)
    b1c = np.ascontiguousarray(b1.reshape(NE, NFC, 128).transpose(2, 0, 1).reshape(128, NE * NFC))
    w2 = np.asarray(inputs["L0_ffn_w2"], dtype=f)
    w2pre = np.ascontiguousarray(w2.transpose(0, 2, 1))
    b2r = np.asarray(inputs["L0_ffn_b2"], dtype=f).reshape(1, NE * D)
    lng = np.empty((128, 4 * D), dtype=f)
    lng[:, 0:D] = np.asarray(inputs["L0_ffn_ln1_g"], dtype=f)[None, :]
    lng[:, D:2 * D] = np.asarray(inputs["L0_ffn_ln1_b"], dtype=f)[None, :]
    lng[:, 2 * D:3 * D] = np.asarray(inputs["L0_ffn_ln2_g"], dtype=f)[None, :]
    lng[:, 3 * D:4 * D] = np.asarray(inputs["L0_ffn_ln2_b"], dtype=f)[None, :]
    base = {
        "wqkv_in": wqkv_g, "wgat_in": wgat_t, "wprj_in": wprj_t, "decay_r": decay_r,
        "tab_in": tab, "umask_in": umask,
        "ipw_in": ipw, "outw_in": outw,
        "gatew_in": gatew, "gateb_in": gateb,
        "bias_in": np.ascontiguousarray(np.concatenate([ipb, outb, b2r], axis=1)),
        "w1pre_in": w1pre, "b1c_in": b1c, "w2pre_in": w2pre,
        "lng_in": np.ascontiguousarray(lng),
    }
    in_maps = []
    for c in range(NCORES):
        m = dict(base)
        m["x_in"] = np.ascontiguousarray(x[c])
        j = np.arange(T)
        gidx_v = ((j // 256) * T + c * 256 + (j % 256)).astype(np.int32)
        m["gidx_in"] = np.ascontiguousarray(gidx_v.reshape(NT, 128).T)
        in_maps.append(m)
    return in_maps


def kernel(**inputs):
    in_maps = host_prep(inputs)
    nc = build_nc(stage=99)
    res = run_bass_kernel_spmd(nc, in_maps, core_ids=list(range(NCORES)))
    out = np.empty((NCORES, T, D), dtype=np.float32)
    for c in range(NCORES):
        y = np.asarray(res.results[c]["y_out"])
        out[:, c * 256:(c + 1) * 256, :] = y.reshape(NCORES, 256, D)
    return out


# version 07 (final)
# speedup vs baseline: 1.0088x; 1.0044x over previous
import numpy as np
import contextlib
import concourse.bass as bass
import concourse.mybir as mybir
from concourse.bass_utils import run_bass_kernel_spmd
from concourse.bass_core_rust import IndirectOffsetOnAxis

F32 = mybir.dt.float32
BF16 = mybir.dt.bfloat16
I32 = mybir.dt.int32
AF = mybir.ActivationFunctionType
ALU = mybir.AluOpType
AX = mybir.AxisListType

T = 2048
D = 1024
NT = 16
NH = 8
HDQ, HDV = 128, 256
NIC = 8
EPS = 1e-5
LNEG = -30000.0
NCORES = 8
FF = 4096
NE = 8
NFC = 32

ENGS = ("sy", "sc", "ve", "gp", "te")


class Prog:
    def __init__(self, nc, ctx):
        self.nc = nc
        self.ctx = ctx
        self.q = {e: [] for e in ENGS}
        self.sems = {}
        self.cnt = {}

    def sem(self, *names):
        for n in names:
            self.sems[n] = self.ctx.enter_context(self.nc.semaphore(n))
            self.cnt[n] = 0

    def w(self, eng, sem, val=None):
        v = self.cnt[sem] if val is None else val
        if v <= 0:
            return
        h = self.sems[sem]
        self.q[eng].append(lambda E, h=h, v=v: E.wait_ge(h, v))

    def o(self, eng, fn, inc=None, n=1):
        if inc is not None:
            self.cnt[inc] += n
            h = self.sems[inc]
            self.q[eng].append(lambda E, fn=fn, h=h, n=n: fn(E).then_inc(h, n))
            return self.cnt[inc]
        self.q[eng].append(fn)
        return None

    def dma(self, eng, sem, out, in_, **kw):
        return self.o(eng, lambda E, out=out, in_=in_, kw=kw: E.dma_start(out=out, in_=in_, **kw),
                      inc=sem, n=16)

    def dmat(self, eng, sem, out, in_):
        return self.o(eng, lambda E, out=out, in_=in_: E.dma_start(out=out, in_=in_, transpose=True),
                      inc=sem, n=16)

    def c(self, sem):
        return self.cnt[sem]


def build_nc(stage=99, taps=(), n_heads=NH, stop_qkv=False, no_moe_cast=False):
    nc = bass.Bass("TRN2", target_bir_lowering=False, debug=False,
                   detect_race_conditions=False)

    dram = lambda n, sh, dt, kind: nc.dram_tensor(n, sh, dt, kind=kind).ap()
    x_in = dram("x_in", [T, D], F32, "ExternalInput")
    wqkv_in = dram("wqkv_in", [D, 4096], F32, "ExternalInput")
    wgat_in = dram("wgat_in", [D, 2048], F32, "ExternalInput")
    wprj_in = dram("wprj_in", [2048, D], F32, "ExternalInput")
    decay_r = dram("decay_r", [1, NH], F32, "ExternalInput")
    tab_in = dram("tab_in", [NH, NT, 128, 256], F32, "ExternalInput")
    umask_in = dram("umask_in", [128, 128], F32, "ExternalInput")
    gidx_in = dram("gidx_in", [128, NT], I32, "ExternalInput")
    ipw_in = dram("ipw_in", [D, 3072], F32, "ExternalInput")
    outw_in = dram("outw_in", [D, D], F32, "ExternalInput")
    gatew_in = dram("gatew_in", [D, NE], F32, "ExternalInput")
    gateb_in = dram("gateb_in", [1, NE], F32, "ExternalInput")
    bias_in = dram("bias_in", [1, 3072 + D + NE * D], F32, "ExternalInput")
    w1pre_in = dram("w1pre_in", [NE, NFC * 128, NIC * 128], F32, "ExternalInput")
    b1c_in = dram("b1c_in", [128, NE * NFC], F32, "ExternalInput")
    w2pre_in = dram("w2pre_in", [NE, FF, D], F32, "ExternalInput")
    lng_in = dram("lng_in", [128, 4 * D], F32, "ExternalInput")

    y_out = dram("y_out", [T, D], F32, "ExternalOutput")
    tap_ap = {}
    for tn in taps:
        shapes = {"hfm": [D, T], "x1": [T, D], "xmid": [T, D], "qk": [T, 2048],
                  "ctx": [T, D], "comb": [T, NE], "x1b": [T, D], "gnfm": [2048, T], "accf": [T, D], "h1f": [FF, 512],
                  "qkfm": [2 * HDQ + HDV, T], "xlnfm": [D, T]}
        tap_ap[tn] = dram("tap_" + tn, shapes[tn], F32, "ExternalOutput")

    ag_in = nc.dram_tensor("ag_in", [T, D], BF16)
    ag_out = nc.dram_tensor("ag_out", [NCORES * T, D], BF16)
    bias_c = nc.dram_tensor("bias_c", [1, 3072 + D + NE * D], BF16)
    wqkv_c = nc.dram_tensor("wqkv_c", [D, 4096], BF16)
    wgat_c = nc.dram_tensor("wgat_c", [D, 2048], BF16)
    wprj_c = nc.dram_tensor("wprj_c", [2048, D], BF16)
    ipw_c = nc.dram_tensor("ipw_c", [D, 3072], BF16)
    outw_c = nc.dram_tensor("outw_c", [D, D], BF16)
    w1_c = nc.dram_tensor("w1_c", [NE, NFC * 128, NIC * 128], BF16)
    w2_c = nc.dram_tensor("w2_c", [NE, FF, D], BF16)

    ctx = contextlib.ExitStack()
    SB = lambda n, cols, dt: ctx.enter_context(nc.sbuf_tensor(n, [128, cols], dt))
    AR1 = SB("AR1", 16384, BF16)
    AR2 = SB("AR2", 16384, BF16)
    AR3 = SB("AR3", 32768, BF16)
    AR4 = SB("AR4", 16384, BF16)
    AR6 = SB("AR6", 3072, F32)
    AR7 = SB("AR7", 12288, BF16)
    umask = SB("umask", 128, F32)
    tcol = SB("tcol", NT, F32)
    qdec = SB("qdec", NT, F32)
    kdec = SB("kdec", NT, F32)
    ldb = SB("ldb", NH, F32)
    nldb = SB("nldb", NH, F32)
    gidx = SB("gidx", NT, I32)
    xstat = SB("xstat", 2 * NT, F32)
    st6 = SB("st6", 12, F32)
    mv2 = SB("mv2", 8, F32)
    b1c = SB("b1c", NE * NFC, F32)
    comb = SB("comb", NT * NE, F32)
    gwsb = SB("gwsb", NIC * NE, BF16)
    attw = SB("attw", NH * 256, BF16)
    mqr = SB("mqr", 8, F32)
    ccol = SB("ccol", 4, F32)
    decs = ctx.enter_context(nc.sbuf_tensor("decs", [1, NH], F32))
    ones_f = ctx.enter_context(nc.sbuf_tensor("ones_f", [1, 128], F32))
    ones_b = ctx.enter_context(nc.sbuf_tensor("ones_b", [1, 128], BF16))
    gateb_r = ctx.enter_context(nc.sbuf_tensor("gateb_r", [1, NE], BF16))
    brow = ctx.enter_context(nc.sbuf_tensor("brow", [1, 2048], BF16))

    PS = [ctx.enter_context(nc.psum_tensor("ps%d" % i, [128, 512], F32)) for i in range(8)]

    p = Prog(nc, ctx)
    p.sem("SY", "SC", "GP", "PE", "VE", "AC", "PO", "CC", "ST",
          "TB0", "TB1", "WQ", "IP0", "IP1", "W10", "W11", "W12", "W13", "W20", "W21", "W22", "W23", "MC", "GPI")

    A4_V, A4_QFM, A4_KFM, A4_QKVW, A4_GATEW, A4_STB, A4_QKS = \
        0, 4096, 6144, 8192, 12288, 14336, 15360
    A6_TAB, A6_SCR, A6_OFM = 0, 512, 2560
    A7_OTM, A7_GT, A7_GN, A7_HST = 0, 1024, 2048, 4096
    B7_IPW, B7_LN = 0, 8192
    M7_W1, M7_W2 = 0, 4096

    scr = lambda i: AR6[:, A6_SCR + i * 1024: A6_SCR + (i + 1) * 1024]
    bf6 = AR6[:, :].bitcast(BF16)
    f32_AR3 = AR3[:, :].bitcast(F32)
    tabs = lambda par: AR6[:, A6_TAB + par * 256: A6_TAB + (par + 1) * 256]
    hfm = lambda ic: AR2[:, ic * T:(ic + 1) * T]
    g1t = AR7[:, B7_LN + 0 * D: B7_LN + 1 * D]
    b1t = AR7[:, B7_LN + 1 * D: B7_LN + 2 * D]
    g2t = AR7[:, B7_LN + 2 * D: B7_LN + 3 * D]
    b2t = AR7[:, B7_LN + 3 * D: B7_LN + 4 * D]

    p.dma("sy", "SY", umask[:, :], umask_in)
    p.dma("sy", "SY", gidx[:, :], gidx_in)
    p.dma("sy", "SY", decs[:, :], decay_r)
    p.dma("sy", "SY", b1c[:, :], b1c_in)
    misc_loaded = p.c("SY")

    for tt in range(NT):
        p.dma("gp", "GP", AR1[:, tt * D:(tt + 1) * D], x_in[tt * 128:(tt + 1) * 128, :])
    x_loaded = p.c("GP")
    p.dma("gp", "GP", gateb_r[:, :], gateb_in)
    p.dma("gp", "GP", bias_c.ap(), bias_in)
    p.dma("gp", "GP", gwsb[:, :].rearrange("p (ic e) -> p ic e", e=NE), gatew_in.rearrange("(ic p) e -> p ic e", p=128))
    p.dma("gp", "GP", AR7[:, B7_LN:B7_LN + 4 * D], lng_in)
    small_loaded = p.c("GP")
    p.dma("gp", "MC", wqkv_c.ap(), wqkv_in)
    p.dma("gp", "MC", wgat_c.ap(), wgat_in)
    qkv_cast = p.c("MC")

    p.o("ve", lambda E: E.memset(ones_f[:, :], 1.0))
    p.o("ve", lambda E: E.memset(ccol[:, 0:1], EPS))
    p.o("ve", lambda E: E.memset(ccol[:, 1:2], -0.5))
    p.o("ve", lambda E: E.memset(ccol[:, 2:3], float(np.log(128.0 ** -0.5))))
    p.o("ve", lambda E: E.memset(ones_b[:, :], 1.0), inc="VE")
    ones_ready = p.c("VE")
    p.o("gp", lambda E: E.iota(tcol[:, :], pattern=[[128, NT]], base=0, channel_multiplier=1,
                               allow_small_or_imprecise_dtypes=True), inc="PO")
    iota_done = p.c("PO")
    p.w("sc", "SY", misc_loaded)
    p.o("sc", lambda E: E.activation(decs[:, :], decs[:, :], AF.Ln), inc="AC")
    ln_dec = p.c("AC")
    p.w("te", "AC", ln_dec)
    p.w("te", "VE", ones_ready)
    p.o("te", lambda E: E.matmul(PS[6][:, 0:NH], ones_f[:, :], decs[:, :],
                                 start=True, stop=True), inc="PE")
    p.w("ve", "PE")
    p.o("ve", lambda E: E.tensor_copy(ldb[:, :], PS[6][:, 0:NH]), inc="VE")
    p.w("ve", "VE")
    p.o("ve", lambda E: E.tensor_scalar_mul(nldb[:, :], ldb[:, :], -1.0), inc="VE")
    ldb_ready = p.c("VE")
    p.w("sc", "VE", ldb_ready)
    p.w("sc", "PO", iota_done)
    dec_ready = p.c("AC")

    qkvw_early = AR4[:, 8192:8192 + 4096]
    gatew_early = AR4[:, 12288:12288 + 2048]
    p.w("sy", "MC", qkv_cast)
    p.dma("sy", "WQ", qkvw_early[:, :].rearrange("p (ic c) -> p ic c", c=512),
          wqkv_c.ap().rearrange("(ic p) o -> p ic o", p=128)[:, :, 0:512])
    p.dma("sy", "WQ", gatew_early[:, :].rearrange("p (ic c) -> p ic c", c=256),
          wgat_c.ap().rearrange("(ic p) o -> p ic o", p=128)[:, :, 0:256])
    early_wq = p.c("WQ")
    p.dma("sy", "TB0", AR6[:, 0:256], tab_in[0, 0, :, :])
    p.dma("sy", "TB1", AR6[:, 256:512], tab_in[0, 1, :, :])
    early_tb0, early_tb1 = p.c("TB0"), p.c("TB1")
    hst = lambda par: AR7[:, A7_HST + par * D: A7_HST + (par + 1) * D]
    hst_tp = {}
    hfm_tile_tp = {}
    p.w("ve", "GP", x_loaded)
    for tt in range(NT):
        xt = AR1[:, tt * D:(tt + 1) * D]
        p.o("ve", lambda E, xt=xt: E.bn_stats(st6[:, 0:6], xt[:, 0:512]))
        p.o("ve", lambda E, xt=xt: E.bn_stats(st6[:, 6:12], xt[:, 512:1024]), inc="VE")
        p.w("ve", "VE")
        p.o("ve", lambda E, tt=tt: E.bn_aggr(xstat[:, tt * 2: tt * 2 + 2], st6[:, 0:12]),
            inc="VE" if tt == NT - 1 else None)
    xstats_done = p.c("VE")
    p.w("sc", "VE", xstats_done)
    for tt in range(NT):
        p.o("sc", lambda E, tt=tt: E.activation(xstat[:, tt * 2 + 1: tt * 2 + 2],
                                                xstat[:, tt * 2 + 1: tt * 2 + 2], AF.Ln,
                                                bias=ccol[:, 0:1]), inc="AC")
    p.w("sc", "AC")
    for tt in range(NT):
        p.o("sc", lambda E, tt=tt: E.activation(xstat[:, tt * 2 + 1: tt * 2 + 2],
                                                xstat[:, tt * 2 + 1: tt * 2 + 2], AF.Exp,
                                                scale=ccol[:, 1:2]),
            inc="AC" if tt == NT - 1 else None)
    xrstd_done = p.c("AC")
    p.w("ve", "AC", xrstd_done)
    for tt in range(NT):
        xt = AR1[:, tt * D:(tt + 1) * D]
        if tt >= 2:
            s_, v_ = hst_tp[tt % 2]
            p.w("ve", s_, v_)
        p.o("ve", lambda E, xt=xt, tt=tt: E.tensor_scalar(
            hst(tt % 2)[:, :], xt, xstat[:, tt * 2: tt * 2 + 1],
            xstat[:, tt * 2 + 1: tt * 2 + 2], op0=ALU.subtract, op1=ALU.mult), inc="VE")
        hdone = p.c("VE")
        qeng, qsem = "sy", "SY"
        p.w(qeng, "VE", hdone)
        for ic in range(NIC):
            p.dmat(qeng, qsem, AR2[:, ic * T + tt * 128: ic * T + (tt + 1) * 128],
                   hst(tt % 2)[:, ic * 128:(ic + 1) * 128])
        hst_tp[tt % 2] = (qsem, p.c(qsem))
        hfm_tile_tp[tt] = p.c("SY")
    hfm_sy, hfm_sc = p.c("SY"), p.c("SC")
    p.w("gp", "SY", hfm_sy)
    p.dma("gp", "MC", ipw_c.ap(), ipw_in)
    p.dma("gp", "MC", outw_c.ap(), outw_in)
    p.dma("gp", "MC", wprj_c.ap(), wprj_in)
    misc_cast = p.c("MC")

    def tap_tm(name, src_fn, gate_sem, gate_val, width=D):
        if name not in tap_ap:
            return
        for tt in range(NT):
            p.w("ve", gate_sem, gate_val)
            p.w("ve", "ST", p.c("ST"))
            p.o("ve", lambda E, tt=tt: E.tensor_copy(scr(0)[:, 0:width], src_fn(tt)), inc="VE")
            p.w("sy", "VE")
            p.dma("sy", "ST", tap_ap[name][tt * 128:(tt + 1) * 128, 0:width], scr(0)[:, 0:width])

    if "hfm" in tap_ap:
        for ic in range(NIC):
            for half in range(2):
                p.w("ve", "SY", hfm_sy)
                p.w("ve", "SC", hfm_sc)
                p.w("ve", "ST", p.c("ST"))
                p.o("ve", lambda E, ic=ic, half=half: E.tensor_copy(
                    scr(0), hfm(ic)[:, half * 1024:(half + 1) * 1024]), inc="VE")
                p.w("sy", "VE")
                p.dma("sy", "ST", tap_ap["hfm"][ic * 128:(ic + 1) * 128,
                                                half * 1024:(half + 1) * 1024], scr(0))
    if stage <= 1:
        finish(p, nc, ctx)
        return nc

    vh = AR4[:, A4_V:A4_V + NT * HDV]
    qfm = AR4[:, A4_QFM:A4_QFM + T]
    kfm = AR4[:, A4_KFM:A4_KFM + T]
    qkvw = AR4[:, A4_QKVW:A4_QKVW + 4096]
    gatew = AR4[:, A4_GATEW:A4_GATEW + 2048]
    stb = lambda par: AR4[:, A4_STB + (par % 4) * 512: A4_STB + (par % 4 + 1) * 512]
    qks = lambda par: AR7[:, 6144 + par * 512: 6144 + (par + 1) * 512]
    otm_st = AR7[:, A7_OTM:A7_OTM + 1024]
    gt_st = AR7[:, A7_GT:A7_GT + 1024]
    gn_st = lambda par: AR7[:, A7_GN + par * 1024: A7_GN + (par + 1) * 1024]

    tabv = {(0, 0): early_tb0, (0, 1): early_tb1}

    def tab_load(h, tt):
        sem = "TB0" if tt % 2 == 0 else "TB1"
        p.dma("sy", sem, tabs(tt % 2)[:, :], tab_in[h, tt, :, :])
        tabv[(h, tt)] = p.c(sem)

    head_done_pe = 0
    xpos_done_ve = 0
    xpos_done_po = 0
    out_evac = {}
    gate_ac = {}
    ofm_tp = {}
    gn_done = {}
    gn_tp = {}
    prev_head_out_evac = None
    prev_head_gn_done = None

    for h in range(n_heads):
        if h > 0:
            p.w("sy", "PE", head_done_pe)
            p.dma("sy", "WQ", gatew[:, :].rearrange("p (ic c) -> p ic c", c=256),
                  wgat_c.ap().rearrange("(ic p) o -> p ic o", p=128)[:, :, h * 256:(h + 1) * 256])
        wq_ready = p.c("WQ")
        p.o("sc", lambda E, h=h: E.activation(qdec[:, :], tcol[:, :], AF.Exp,
                                              bias=ccol[:, 2:3],
                                              scale=ldb[:, h:h + 1]))
        p.o("sc", lambda E, h=h: E.activation(kdec[:, :], tcol[:, :], AF.Exp,
                                              scale=nldb[:, h:h + 1]), inc="AC")

        evac_ac = {}
        tpq = {}
        for tt in range(NT):
            bank = PS[tt % 4]
            p.w("te", "WQ", wq_ready)
            if h == 0:
                p.w("te", "SY", hfm_tile_tp[tt])
            if h == 0 and tt == 0:
                p.w("te", "AC", dec_ready)
            if tt >= 4:
                p.w("te", "AC", evac_ac[tt - 4])
            elif h > 0:
                p.w("te", "AC", prev_evac_last[tt])
            for ic in range(NIC):
                p.o("te", lambda E, b=bank, ic=ic, tt=tt: E.matmul(
                    b[:, :], hfm(ic)[:, tt * 128:(tt + 1) * 128],
                    qkvw[:, ic * 512:(ic + 1) * 512],
                    start=(ic == 0), stop=(ic == NIC - 1)),
                    inc="PE" if ic == NIC - 1 else None)
            mm_done = p.c("PE")
            p.w("sc", "PE", mm_done)
            if tt == 0:
                p.w("sc", "AC", p.c("AC"))
            if tt >= 2 or h > 0:
                p.w("sc", "VE", xpos_done_ve)
                p.w("sc", "PO", xpos_done_po)
            sp = scr(tt % 2)
            p.o("sc", lambda E, b=bank, sp=sp, h=h, tt=tt: E.mul(
                sp[:, 0:128], b[:, 0:128], qdec[:, tt:tt + 1]))
            p.o("sc", lambda E, b=bank, sp=sp, h=h, tt=tt: E.mul(
                sp[:, 128:256], b[:, 128:256], kdec[:, tt:tt + 1]))
            p.o("sc", lambda E, b=bank, tt=tt: E.copy(vh[:, tt * HDV:(tt + 1) * HDV],
                                                      b[:, 256:512]), inc="AC")
            evac_ac[tt] = p.c("AC")
            eng = "ve" if (tt % 3 != 2) else "gp"
            esem = "VE" if eng == "ve" else "PO"
            p.w(eng, "AC", evac_ac[tt])
            p.w(eng, "TB0" if tt % 2 == 0 else "TB1", tabv[(h, tt)])
            if tt >= 2:
                s_, v_ = tpq[tt % 2]
                p.w(eng, s_, v_)
            elif h > 0:
                p.w(eng, "SY", prev_tpq_sy)
                p.w(eng, "SC", prev_tpq_sc)
            tb = tabs(tt % 2)
            stg = qks(tt % 2)
            for o_q in (0, 128):
                base_t = 0 if o_q == 0 else 128
                se = sp[:, o_q:o_q + 128][:, 0::2]
                so = sp[:, o_q:o_q + 128][:, 1::2]
                Ct = tb[:, base_t: base_t + 64]
                St_ = tb[:, base_t + 64: base_t + 128]
                t0 = sp[:, 512:576]
                t1 = sp[:, 576:640]
                de = stg[:, o_q:o_q + 128][:, 0::2]
                do = stg[:, o_q:o_q + 128][:, 1::2]
                p.o(eng, lambda E, a=t0, b_=so, c=St_: E.tensor_mul(a, b_, c))
                p.o(eng, lambda E, a=t1, b_=se, c=Ct: E.tensor_mul(a, b_, c), inc=esem)
                p.w(eng, esem)
                p.o(eng, lambda E, a=de, b_=t1, c=t0: E.tensor_sub(a, b_, c))
                p.o(eng, lambda E, a=t0, b_=se, c=St_: E.tensor_mul(a, b_, c))
                p.o(eng, lambda E, a=t1, b_=so, c=Ct: E.tensor_mul(a, b_, c), inc=esem)
                p.w(eng, esem)
                p.o(eng, lambda E, a=do, b_=t1, c=t0: E.tensor_add(a, b_, c),
                    inc=esem)
            if eng == "ve":
                xpos_done_ve = p.c("VE")
            else:
                xpos_done_po = p.c("PO")
            p.w("sy", esem, p.c(esem))
            if tt + 2 < NT:
                tab_load(h, tt + 2)
            elif h + 1 < NH:
                tab_load(h + 1, tt + 2 - NT)
            qeng, qsem = "sy", "SY"
            p.w(qeng, esem, p.c(esem))
            p.dmat(qeng, qsem, qfm[:, tt * 128:(tt + 1) * 128], stg[:, 0:128])
            p.dmat(qeng, qsem, kfm[:, tt * 128:(tt + 1) * 128], stg[:, 128:256])
            tpq[tt % 2] = (qsem, p.c(qsem))
        prev_evac_last = {i: evac_ac[12 + i] for i in range(4)}
        prev_tpq_sy, prev_tpq_sc = p.c("SY"), p.c("SC")
        qkv_tp_sy, qkv_tp_sc = p.c("SY"), p.c("SC")
        if h + 1 < n_heads:
            p.w("sy", "PE", p.c("PE"))
            p.dma("sy", "WQ", qkvw[:, :].rearrange("p (ic c) -> p ic c", c=512),
                  wqkv_c.ap().rearrange("(ic p) o -> p ic o", p=128)[:, :, (h + 1) * 512:(h + 2) * 512])
        if "qkfm" in tap_ap and h == n_heads - 1:
            for row, srcv in ((0, qfm), (1, kfm)):
                for half in range(2):
                    p.w("ve", "SY", qkv_tp_sy)
                    p.w("ve", "SC", qkv_tp_sc)
                    p.w("ve", "ST", p.c("ST"))
                    p.o("ve", lambda E, s=srcv, half=half: E.tensor_copy(
                        scr(0), s[:, half * 1024:(half + 1) * 1024]), inc="VE")
                    p.w("sy", "VE")
                    p.dma("sy", "ST", tap_ap["qkfm"][row * 128:(row + 1) * 128,
                                                     half * 1024:(half + 1) * 1024], scr(0))
            for tt4 in range(4):
                p.w("ve", "AC", evac_ac[15])
                p.w("ve", "ST", p.c("ST"))
                p.o("ve", lambda E, tt4=tt4: E.tensor_copy(
                    scr(0), vh[:, tt4 * 1024:(tt4 + 1) * 1024]), inc="VE")
                p.w("sy", "VE")
                p.dma("sy", "ST", tap_ap["qkfm"][256:384, tt4 * 512:(tt4 + 1) * 512],
                      scr(0)[:, 0:512])
                p.w("ve", "ST", p.c("ST"))
                p.o("ve", lambda E: E.engine_nop(), inc="VE")
                p.w("sy", "VE")
                p.dma("sy", "ST", tap_ap["qkfm"][384:512, tt4 * 512:(tt4 + 1) * 512],
                      scr(0)[:, 512:1024])

        if stop_qkv:
            finish(p, nc, ctx)
            return nc
        p.w("te", "SY", qkv_tp_sy)
        p.w("te", "SC", qkv_tp_sc)
        p.w("te", "AC", evac_ac[15])
        st_evac = {}
        sv_pe = {}

        for qb in range(4):
            nkb = 4 * qb + 4

            def emit_st(kb, qb=qb, nkb=nkb):
                bank = [PS[2], PS[3], PS[6], PS[7]][kb % 4]
                if kb >= 4:
                    s_, v_ = st_evac[(qb, kb - 4)]
                    p.w("te", s_, v_)
                elif qb > 0:
                    pk = 4 * qb
                    s_, v_ = st_evac[(qb - 1, pk - 4 + kb)]
                    p.w("te", s_, v_)
                p.o("te", lambda E, b=bank, kb=kb, qb=qb: E.matmul(
                    b[:, :], kfm[:, kb * 128:(kb + 1) * 128], qfm[:, qb * 512:(qb + 1) * 512],
                    start=True, stop=True), inc="PE")
                st_pe = p.c("PE")
                boundary = kb >= 4 * qb
                eng2 = "ve" if (boundary or kb % 2 == 0) else "sc"
                sem2 = "VE" if eng2 == "ve" else "AC"
                p.w(eng2, "PE", st_pe)
                if (qb, kb - 4) in sv_pe:
                    p.w(eng2, "PE", sv_pe[(qb, kb - 4)])
                elif qb > 0 and kb < 4:
                    pv = 4 * qb - 4 + kb
                    if (qb - 1, pv) in sv_pe:
                        p.w(eng2, "PE", sv_pe[(qb - 1, pv)])
                dstb = stb(kb)
                if not boundary:
                    if eng2 == "ve":
                        p.o(eng2, lambda E, d=dstb, b=bank: E.tensor_copy(d[:, :], b[:, :]),
                            inc=sem2)
                    else:
                        p.o(eng2, lambda E, d=dstb, b=bank: E.copy(d[:, :], b[:, :]), inc=sem2)
                else:
                    cc_ = kb - 4 * qb
                    if cc_ > 0:
                        p.o(eng2, lambda E, d=dstb, cc_=cc_: E.memset(d[:, 0:cc_ * 128], 0.0))
                    p.o(eng2, lambda E, d=dstb, b=bank, cc_=cc_: E.tensor_mul(
                        d[:, cc_ * 128:(cc_ + 1) * 128], b[:, cc_ * 128:(cc_ + 1) * 128],
                        umask[:, :]), inc=sem2 if cc_ == 3 else None)
                    if cc_ < 3:
                        p.o(eng2, lambda E, d=dstb, b=bank, cc_=cc_: E.tensor_copy(
                            d[:, (cc_ + 1) * 128:512], b[:, (cc_ + 1) * 128:512]), inc=sem2)
                st_evac[(qb, kb)] = (sem2, p.c(sem2))

            def emit_sv(kb, qb=qb, nkb=nkb):
                s_, v_ = st_evac[(qb, kb)]
                p.w("te", s_, v_)
                if kb == 0 and qb > 0:
                    ev, ea = out_evac[qb - 1]
                    p.w("te", "VE", ev)
                    p.w("te", "AC", ea)
                elif kb == 0 and prev_head_out_evac is not None:
                    ev, ea = prev_head_out_evac
                    p.w("te", "VE", ev)
                    p.w("te", "AC", ea)
                for e in range(2):
                    p.o("te", lambda E, kb=kb, e=e, nkb=nkb: E.matmul(
                        PS[4 + e][:, :], vh[:, kb * HDV + e * 128: kb * HDV + e * 128 + 128],
                        stb(kb)[:, :],
                        start=(kb == 0), stop=(kb == nkb - 1)),
                        inc="PE" if e == 1 else None)
                sv_pe[(qb, kb)] = p.c("PE")

            for k0 in range(min(3, nkb)):
                emit_st(k0)
            for kb in range(nkb):
                emit_sv(kb)
                if kb + 3 < nkb:
                    emit_st(kb + 3)
            ofm_stage = bf6[:, 2 * A6_OFM: 2 * A6_OFM + 1024]
            last_pe = sv_pe[(qb, nkb - 1)]
            if qb >= 1:
                s_, v_ = ofm_tp[(qb - 1) % 2]
                p.w("ve", s_, v_)
                p.w("sc", s_, v_)
            elif 3 % 2 in ofm_tp and h > 0:
                s_, v_ = ofm_tp[3 % 2]
                p.w("ve", s_, v_)
                p.w("sc", s_, v_)
            p.w("ve", "PE", last_pe)
            p.o("ve", lambda E, o=ofm_stage: E.tensor_copy(o[:, 0:512], PS[4][:, :]), inc="VE")
            p.w("sc", "PE", last_pe)
            p.o("sc", lambda E, o=ofm_stage: E.copy(o[:, 512:1024], PS[5][:, :]), inc="AC")
            out_evac[qb] = (p.c("VE"), p.c("AC"))
            qeng, qsem = "sy", "SY"
            p.w(qeng, "VE", out_evac[qb][0])
            p.w(qeng, "AC", out_evac[qb][1])
            if qb >= 1:
                p.w(qeng, "VE", gn_done[qb - 1])
            elif prev_head_gn_done is not None:
                p.w(qeng, "VE", prev_head_gn_done)
            for tl in range(4):
                for e in range(2):
                    p.dmat(qeng, qsem, otm_st[:, tl * 256 + e * 128: tl * 256 + e * 128 + 128],
                           ofm_stage[:, e * 512 + tl * 128: e * 512 + tl * 128 + 128])
            ofm_tp[qb % 2] = (qsem, p.c(qsem))
            otm_ready = (qsem, p.c(qsem))

            for tl in range(4):
                tt = 4 * qb + tl
                bank = PS[tl % 2]
                if qb == 0 and tl < 2:
                    p.w("te", "AC", evac_ac[12 + tl])
                elif tl < 2:
                    p.w("te", "AC", gate_ac[(qb - 1, 2 + tl)])
                else:
                    p.w("te", "AC", gate_ac[(qb, tl - 2)])
                for ic in range(NIC):
                    p.o("te", lambda E, b=bank, ic=ic, tt=tt: E.matmul(
                        b[:, 0:256], hfm(ic)[:, tt * 128:(tt + 1) * 128],
                        gatew[:, ic * 256:(ic + 1) * 256],
                        start=(ic == 0), stop=(ic == NIC - 1)),
                        inc="PE" if ic == NIC - 1 else None)
                gmm = p.c("PE")
                p.w("sc", "PE", gmm)
                if qb >= 1:
                    p.w("sc", "VE", gn_done[qb - 1])
                elif prev_head_gn_done is not None:
                    p.w("sc", "VE", prev_head_gn_done)
                p.o("sc", lambda E, b=bank, tl=tl: E.activation(
                    gt_st[:, tl * 256:(tl + 1) * 256], b[:, 0:256], AF.Silu), inc="AC")
                gate_ac[(qb, tl)] = p.c("AC")

            p.w("ve", otm_ready[0], otm_ready[1])
            for tl in range(4):
                ot = otm_st[:, tl * 256:(tl + 1) * 256]
                p.o("ve", lambda E, ot=ot: E.bn_stats(st6[:, 0:6], ot), inc="VE")
                p.w("ve", "VE")
                p.o("ve", lambda E, tl=tl: E.bn_aggr(mqr[:, tl * 2: tl * 2 + 2], st6[:, 0:6]),
                    inc="VE")
                p.w("sc", "VE")
                p.o("sc", lambda E, tl=tl: E.activation(mqr[:, tl * 2 + 1: tl * 2 + 2],
                                                        mqr[:, tl * 2 + 1: tl * 2 + 2],
                                                        AF.Ln, bias=ccol[:, 0:1]), inc="AC")
                p.w("sc", "AC")
                p.o("sc", lambda E, tl=tl: E.activation(mqr[:, tl * 2 + 1: tl * 2 + 2],
                                                        mqr[:, tl * 2 + 1: tl * 2 + 2],
                                                        AF.Exp, scale=ccol[:, 1:2]), inc="AC")
                p.w("ve", "AC")
                p.w("ve", "AC", gate_ac[(qb, tl)])
                gnp = gn_st(qb % 2)
                if tl == 0 and (qb % 2) in gn_tp and (qb >= 2 or h > 0):
                    s_, v_ = gn_tp[qb % 2]
                    p.w("ve", s_, v_)
                p.o("ve", lambda E, ot=ot, tl=tl, g=gnp: E.tensor_scalar(
                    g[:, tl * 256:(tl + 1) * 256], ot, mqr[:, tl * 2: tl * 2 + 1],
                    mqr[:, tl * 2 + 1: tl * 2 + 2], op0=ALU.subtract, op1=ALU.mult))
                p.o("ve", lambda E, tl=tl, g=gnp: E.tensor_mul(
                    g[:, tl * 256:(tl + 1) * 256], g[:, tl * 256:(tl + 1) * 256],
                    gt_st[:, tl * 256:(tl + 1) * 256]), inc="VE")
            gn_done[qb] = p.c("VE")
            qeng, qsem = "sy", "SY"
            p.w(qeng, "VE", gn_done[qb])
            for tl in range(4):
                tt = 4 * qb + tl
                for e in range(2):
                    fc = 2 * h + e
                    p.dmat(qeng, qsem,
                           AR3[:, fc * T + tt * 128: fc * T + (tt + 1) * 128],
                           gn_st(qb % 2)[:, tl * 256 + e * 128: tl * 256 + e * 128 + 128])
            gn_tp[qb % 2] = (qsem, p.c(qsem))
        head_done_pe = p.c("PE")
        prev_head_out_evac = out_evac[3]
        prev_head_gn_done = gn_done[3]

    gnfm_sy, gnfm_sc = p.c("SY"), p.c("SC")
    if "gnfm" in tap_ap:
        for fc in range(16):
            for half in range(2):
                p.w("ve", "SY", gnfm_sy)
                p.w("ve", "SC", gnfm_sc)
                p.w("ve", "ST", p.c("ST"))
                p.o("ve", lambda E, fc=fc, half=half: E.tensor_copy(
                    scr(0), AR3[:, fc * T + half * 1024: fc * T + (half + 1) * 1024]), inc="VE")
                p.w("sy", "VE")
                p.dma("sy", "ST", tap_ap["gnfm"][fc * 128:(fc + 1) * 128,
                                                 half * 1024:(half + 1) * 1024], scr(0))

    p.w("sy", "PE", head_done_pe)
    p.w("sy", "MC", misc_cast)
    p.w("sy", "VE", p.c("VE"))
    p.w("sy", "PO", p.c("PO"))
    p.w("sy", "AC", p.c("AC"))
    p.w("sy", "SC", p.c("SC"))
    p.w("sy", "SY", p.c("SY"))
    for q4 in range(4):
        p.dma("sy", "WQ", AR4[:, q4 * 4096:(q4 + 1) * 4096].rearrange("p (fc c) -> p fc c", c=1024),
              wprj_c.ap().rearrange("(fc p) o -> p fc o", p=128)[:, q4 * 4:(q4 + 1) * 4, :])
    prj_ready = p.c("WQ")
    prjw = lambda fc: AR4[:, fc * 1024:(fc + 1) * 1024]

    p.w("te", "WQ", prj_ready)
    p.w("te", "SY", gnfm_sy)
    p.w("te", "SC", gnfm_sc)
    prev_evac = {}
    x1_ve = {}
    for tt in range(NT):
        for ob in range(2):
            k2 = tt * 2 + ob
            bank = PS[2 + k2 % 2]
            if k2 >= 2:
                p.w("te", "VE", prev_evac[k2 % 2])
            for fc in range(16):
                p.o("te", lambda E, b=bank, fc=fc, tt=tt, ob=ob: E.matmul(
                    b[:, :], AR3[:, fc * T + tt * 128: fc * T + (tt + 1) * 128],
                    prjw(fc)[:, ob * 512:(ob + 1) * 512],
                    start=(fc == 0), stop=(fc == 15)),
                    inc="PE" if fc == 15 else None)
            mmv = p.c("PE")
            p.w("ve", "PE", mmv)
            p.o("ve", lambda E, b=bank, ob=ob: E.tensor_copy(
                scr(0)[:, ob * 512:(ob + 1) * 512], b[:, :]), inc="VE")
            prev_evac[k2 % 2] = p.c("VE")
        p.o("ve", lambda E: E.bn_stats(st6[:, 0:6], scr(0)[:, 0:512]))
        p.o("ve", lambda E: E.bn_stats(st6[:, 6:12], scr(0)[:, 512:1024]), inc="VE")
        p.w("ve", "VE")
        p.o("ve", lambda E: E.bn_aggr(mv2[:, 0:2], st6[:, 0:12]), inc="VE")
        p.w("sc", "VE")
        p.o("sc", lambda E: E.activation(mv2[:, 2:3], mv2[:, 1:2], AF.Ln, bias=ccol[:, 0:1]), inc="AC")
        p.w("sc", "AC")
        p.o("sc", lambda E: E.activation(mv2[:, 2:3], mv2[:, 2:3], AF.Exp, scale=ccol[:, 1:2]), inc="AC")
        p.w("ve", "AC")
        p.o("ve", lambda E: E.tensor_scalar(scr(1), scr(0), mv2[:, 0:1], mv2[:, 2:3],
                                            op0=ALU.subtract, op1=ALU.mult))
        xt = AR1[:, tt * D:(tt + 1) * D]
        p.o("ve", lambda E, xt=xt: E.tensor_add(xt, xt, scr(1)), inc="VE")
        x1_ve[tt] = p.c("VE")
    x1_done = p.c("VE")

    tap_tm("x1", lambda tt: AR1[:, tt * D:(tt + 1) * D], "VE", x1_done)
    if stage <= 2:
        finish(p, nc, ctx)
        return nc

    p.w("sy", "VE", x1_done)
    p.dma("sy", "SY", ag_in.ap().rearrange("(tile p) d -> p tile d", p=128),
          AR1[:, :].rearrange("p (tile d) -> p tile d", d=D))
    agin_done = p.c("SY")
    p.w("gp", "SY", agin_done)
    p.o("gp", lambda E: E.collective_compute(
        "AllGather", ALU.bypass, replica_groups=[list(range(NCORES))],
        ins=[ag_in.ap().opt()], outs=[ag_out.ap().opt()]), inc="CC")
    p.w("gp", "CC", p.c("CC"))
    p.w("gp", "SY", misc_loaded)
    for tt in range(NT):
        p.o("gp", lambda E, tt=tt: E.indirect_dma_start(
            AR1[:, tt * D:(tt + 1) * D], None,
            ag_out.ap(), IndirectOffsetOnAxis(gidx[:, tt:tt + 1], 0)), inc="GPI", n=16)
    p.dma("gp", "GP", ag_in.ap()[0:1, :], AR1[:, 0:8].bitcast(F32)[0:1, 0:4]) if False else None
    p.dma("gp", "GP", ag_in.ap()[0:1, 0:1024], AR1[0:1, 0:2048].bitcast(F32))
    x1b_done = p.c("GP")
    tap_tm("x1b", lambda tt: AR1[:, tt * D:(tt + 1) * D], "GP", x1b_done)

    p.w("ve", "GP", x1b_done)
    bst = lambda par: AR3[:, 30720 + par * 1024: 30720 + (par + 1) * 1024]
    xln_tp = {}
    for tt in range(NT):
        xt = AR1[:, tt * D:(tt + 1) * D]
        p.o("ve", lambda E, xt=xt: E.bn_stats(st6[:, 0:6], xt[:, 0:512]))
        p.o("ve", lambda E, xt=xt: E.bn_stats(st6[:, 6:12], xt[:, 512:1024]), inc="VE")
        p.w("ve", "VE")
        p.o("ve", lambda E, tt=tt: E.bn_aggr(xstat[:, tt * 2: tt * 2 + 2], st6[:, 0:12]),
            inc="VE" if tt == NT - 1 else None)
    bstats_done = p.c("VE")
    p.w("sc", "VE", bstats_done)
    for tt in range(NT):
        p.o("sc", lambda E, tt=tt: E.activation(xstat[:, tt * 2 + 1: tt * 2 + 2],
                                                xstat[:, tt * 2 + 1: tt * 2 + 2], AF.Ln,
                                                bias=ccol[:, 0:1]), inc="AC")
    p.w("sc", "AC")
    for tt in range(NT):
        p.o("sc", lambda E, tt=tt: E.activation(xstat[:, tt * 2 + 1: tt * 2 + 2],
                                                xstat[:, tt * 2 + 1: tt * 2 + 2], AF.Exp,
                                                scale=ccol[:, 1:2]),
            inc="AC" if tt == NT - 1 else None)
    brstd_done = p.c("AC")
    p.w("ve", "AC", brstd_done)
    for tt in range(NT):
        xt = AR1[:, tt * D:(tt + 1) * D]
        if tt >= 2:
            s_, v_ = xln_tp[tt % 2]
            p.w("ve", s_, v_)
        p.o("ve", lambda E, xt=xt, tt=tt: E.tensor_scalar(
            bst(tt % 2)[:, :], xt, xstat[:, tt * 2: tt * 2 + 1],
            xstat[:, tt * 2 + 1: tt * 2 + 2], op0=ALU.subtract, op1=ALU.mult), inc="VE")
        hdone = p.c("VE")
        qeng, qsem = "sy", "SY"
        p.w(qeng, "VE", hdone)
        for ic in range(NIC):
            p.dmat(qeng, qsem, AR2[:, ic * T + tt * 128: ic * T + (tt + 1) * 128],
                   bst(tt % 2)[:, ic * 128:(ic + 1) * 128])
        xln_tp[tt % 2] = (qsem, p.c(qsem))
    xln_sy, xln_sc = p.c("SY"), p.c("SC")
    xlnfm = lambda ic: AR2[:, ic * T:(ic + 1) * T]
    if "xlnfm" in tap_ap:
        for ic in range(NIC):
            for half in range(2):
                p.w("ve", "SY", xln_sy)
                p.w("ve", "ST", p.c("ST"))
                p.o("ve", lambda E, ic=ic, half=half: E.tensor_copy(
                    scr(0), xlnfm(ic)[:, half * 1024:(half + 1) * 1024]), inc="VE")
                p.w("sy", "VE")
                p.dma("sy", "ST", tap_ap["xlnfm"][ic * 128:(ic + 1) * 128,
                                                  half * 1024:(half + 1) * 1024], scr(0))

    ipw_par = lambda par: AR7[:, B7_IPW + par * 4096: B7_IPW + (par + 1) * 4096]
    ipsem = ("IP0", "IP1")
    ip_loaded = {}

    def load_ip(src_c, ob, pe_gate, bias_off=None):
        par = ob % 2
        p.w("sy", "PE", pe_gate)
        p.w("sy", "MC", misc_cast)
        p.w("sy", "GP", small_loaded)
        p.dma("sy", ipsem[par], ipw_par(par)[:, :].rearrange("p (ic c) -> p ic c", c=512),
              src_c.ap().rearrange("(ic p) o -> p ic o", p=128)[:, :, ob * 512:(ob + 1) * 512])
        if bias_off is not None:
            p.dma("sy", ipsem[par], brow[:, par * 1024: par * 1024 + 512],
                  bias_c.ap()[:, bias_off: bias_off + 512])
        return p.c(ipsem[par])

    ip_loaded[0] = load_ip(ipw_c, 0, 0, bias_off=0)
    ip_loaded[1] = load_ip(ipw_c, 1, 0, bias_off=512)
    ip_consumed = {}
    ip_evac = {}
    for ob in range(6):
        par = ob % 2
        p.w("te", ipsem[par], ip_loaded[ob])
        if ob == 0:
            p.w("te", "SY", xln_sy)
            p.w("te", "SC", xln_sc)
            p.w("te", "GP", small_loaded)
        for tt in range(NT):
            k2 = tt % 2
            bank = PS[4 + k2]
            if (ob * NT + tt) >= 2:
                p.w("te", "VE", ip_evac[k2])
            p.o("te", lambda E, b=bank, par=par: E.matmul(
                b[:, :], ones_b[:, :], brow[:, par * 1024: par * 1024 + 512], start=True, stop=False))
            for ic in range(NIC):
                p.o("te", lambda E, b=bank, ic=ic, tt=tt, par=par: E.matmul(
                    b[:, :], xlnfm(ic)[:, tt * 128:(tt + 1) * 128],
                    ipw_par(par)[:, ic * 512:(ic + 1) * 512],
                    start=False, stop=(ic == NIC - 1)),
                    inc="PE" if ic == NIC - 1 else None)
            mmv = p.c("PE")
            p.w("ve", "PE", mmv)
            if ob < 2:
                dst = AR3[:, tt * 1024 + ob * 512: tt * 1024 + (ob + 1) * 512]
                p.o("ve", lambda E, b=bank, d=dst: E.tensor_scalar_mul(d, b[:, :], 0.125),
                    inc="VE")
            elif ob < 4:
                dst = AR3[:, 16384 + tt * 1024 + (ob - 2) * 512: 16384 + tt * 1024 + (ob - 1) * 512]
                p.o("ve", lambda E, b=bank, d=dst: E.tensor_copy(d, b[:, :]), inc="VE")
            else:
                dst = AR4[:, tt * 1024 + (ob - 4) * 512: tt * 1024 + (ob - 3) * 512]
                p.o("ve", lambda E, b=bank, d=dst: E.tensor_copy(d, b[:, :]), inc="VE")
            ip_evac[k2] = p.c("VE")
        ip_consumed[ob] = p.c("PE")
        if ob + 2 < 6:
            ip_loaded[ob + 2] = load_ip(ipw_c, ob + 2, ip_consumed[ob], bias_off=(ob + 2) * 512)
    qk_done = p.c("VE")

    if "qk" in tap_ap:
        for tt in range(NT):
            p.w("ve", "VE", qk_done)
            p.w("ve", "ST", p.c("ST"))
            p.o("ve", lambda E, tt=tt: E.tensor_copy(scr(0), AR3[:, tt * 1024:(tt + 1) * 1024]),
                inc="VE")
            p.w("sy", "VE")
            p.dma("sy", "ST", tap_ap["qk"][tt * 128:(tt + 1) * 128, 0:1024], scr(0))
            p.w("ve", "ST", p.c("ST"))
            p.o("ve", lambda E, tt=tt: E.tensor_copy(
                scr(0), AR3[:, 16384 + tt * 1024: 16384 + (tt + 1) * 1024]), inc="VE")
            p.w("sy", "VE")
            p.dma("sy", "ST", tap_ap["qk"][tt * 128:(tt + 1) * 128, 1024:2048], scr(0))

    mbuf = bf6[:, 2 * (A6_SCR + 1024): 2 * (A6_SCR + 1024) + 1024]
    mbuf3 = mbuf.rearrange("p (h c) -> p h c", c=64)
    p.w("ve", "VE", qk_done)
    sc_done = {}
    for s in range(NH):
        for hf in range(2):
            qtile = AR3[:, (s * 2 + hf) * 1024: (s * 2 + hf + 1) * 1024]
            for t in range(NH):
                ktile = AR3[:, 16384 + (t * 2 + hf) * 1024: 16384 + (t * 2 + hf + 1) * 1024]
                p.o("ve", lambda E, q=qtile, k=ktile: E.tensor_mul(mbuf[:, :], q, k))
                dst = scr(0)[:, hf * 128: hf * 128 + 128].rearrange(
                    "p (h t8) -> p h t8", t8=8)[:, :, t:t + 1]
                p.o("ve", lambda E, d=dst: E.tensor_reduce(d, mbuf3, axis=AX.X, op=ALU.add))
        for hf in range(2):
            sc3 = scr(0)[:, hf * 128: hf * 128 + 128].rearrange("p (h t8) -> p h t8", t8=8)
            mx = scr(1)[:, 992 + hf * 16: 992 + hf * 16 + 16]
            p.o("ve", lambda E, a=mx, b=sc3: E.tensor_reduce(
                a.unsqueeze(2), b, axis=AX.X, op=ALU.max), inc="VE")
            p.w("ve", "VE")
            p.o("ve", lambda E, a=sc3, m=mx: E.tensor_sub(
                a, a, m.unsqueeze(2).broadcast_to([128, 16, 8])), inc="VE")
            p.w("sc", "VE")
            flat = scr(0)[:, hf * 128: hf * 128 + 128]
            p.o("sc", lambda E, f=flat: E.activation(f, f, AF.Exp), inc="AC")
            p.w("ve", "AC")
            p.o("ve", lambda E, a=mx, b=sc3: E.tensor_reduce(
                a.unsqueeze(2), b, axis=AX.X, op=ALU.add), inc="VE")
            p.w("ve", "VE")
            p.o("ve", lambda E, a=mx: E.reciprocal(a, a), inc="VE")
            p.w("ve", "VE")
            dstw = attw[:, s * 256 + hf * 128: s * 256 + hf * 128 + 128].rearrange(
                "p (h t8) -> p h t8", t8=8)
            p.o("ve", lambda E, d=dstw, b=sc3, m=mx: E.tensor_tensor(
                d, b, m.unsqueeze(2).broadcast_to([128, 16, 8]), op=ALU.mult),
                inc="VE")
        sc_done[s] = p.c("VE")
    scores_all = p.c("VE")
    ctx_done = {}
    p.w("ve", "VE", scores_all)
    for s in range(NH):
        for hf in range(2):
            dstc = AR3[:, (s * 2 + hf) * 1024: (s * 2 + hf + 1) * 1024]
            d3 = dstc.rearrange("p (h c) -> p h c", c=64)
            aw = attw[:, s * 256 + hf * 128: s * 256 + hf * 128 + 128].rearrange(
                "p (h t8) -> p h t8", t8=8)
            for t in range(NH):
                vtile = AR4[:, (t * 2 + hf) * 1024: (t * 2 + hf + 1) * 1024]
                v3 = vtile.rearrange("p (h c) -> p h c", c=64)
                awb = aw[:, :, t:t + 1].broadcast_to([128, 16, 64])
                if t == 0:
                    p.o("ve", lambda E, d=d3, v=v3, a=awb: E.tensor_tensor(d, v, a, op=ALU.mult))
                else:
                    p.o("ve", lambda E, v=v3, a=awb: E.tensor_tensor(mbuf3, v, a, op=ALU.mult))
                    p.o("ve", lambda E, d=d3: E.tensor_add(d, d, mbuf3))
            p.o("ve", lambda E: E.engine_nop(), inc="VE")
        ctx_done[s] = p.c("VE")
    for s in range(NH):
        for hf in range(2):
            tt = s * 2 + hf
            qeng, qsem = "sy", "SY"
            p.w(qeng, "VE", ctx_done[s])
            for ic in range(NIC):
                p.dmat(qeng, qsem, AR2[:, ic * T + tt * 128: ic * T + (tt + 1) * 128],
                       AR3[:, tt * 1024 + ic * 128: tt * 1024 + ic * 128 + 128])
    ctxfm_sy, ctxfm_sc = p.c("SY"), p.c("SC")
    moe_cast = []
    p.w("gp", "SY", ctxfm_sy)
    for e in range(NE):
        if not no_moe_cast:
            p.dma("gp", "MC", w1_c.ap()[e, :, :], w1pre_in[e, :, :])
            p.dma("gp", "MC", w2_c.ap()[e, :, :], w2pre_in[e, :, :])
        moe_cast.append(p.c("MC"))
    tap_tm("ctx", lambda tt: AR3[:, tt * 1024:(tt + 1) * 1024], "VE", ctx_done[NH - 1])

    ow_loaded = {ob: load_ip(outw_c, ob, ip_consumed[5], bias_off=3072 + ob * 512) for ob in range(2)}
    p.w("te", "SY", ctxfm_sy)
    p.w("te", "SC", ctxfm_sc)
    ao_evac = {}
    zfm_tp = {}
    xmid_ve = {}
    for tt in range(NT):
        for ob in range(2):
            bank = PS[4 + ob]
            p.w("te", ipsem[ob], ow_loaded[ob])
            if tt >= 1:
                p.w("te", "VE", ao_evac[ob])
            p.o("te", lambda E, b=bank, ob=ob: E.matmul(
                b[:, :], ones_b[:, :], brow[:, ob * 1024: ob * 1024 + 512],
                start=True, stop=False))
            for ic in range(NIC):
                p.o("te", lambda E, b=bank, ic=ic, tt=tt, ob=ob: E.matmul(
                    b[:, :], AR2[:, ic * T + tt * 128: ic * T + (tt + 1) * 128],
                    ipw_par(ob)[:, ic * 512:(ic + 1) * 512],
                    start=False, stop=(ic == NIC - 1)),
                    inc="PE" if ic == NIC - 1 else None)
            mmv = p.c("PE")
            p.w("ve", "PE", mmv)
            p.o("ve", lambda E, b=bank, ob=ob: E.tensor_copy(
                scr(0)[:, ob * 512:(ob + 1) * 512], b[:, :]), inc="VE")
            ao_evac[ob] = p.c("VE")
        xt = AR1[:, tt * D:(tt + 1) * D]
        p.o("ve", lambda E, xt=xt, tt=tt: E.tensor_scalar(
            scr(1), xt, xstat[:, tt * 2: tt * 2 + 1], xstat[:, tt * 2 + 1: tt * 2 + 2],
            op0=ALU.subtract, op1=ALU.mult))
        p.o("ve", lambda E: E.tensor_add(scr(0), scr(0), scr(1)))
        p.o("ve", lambda E: E.bn_stats(st6[:, 0:6], scr(0)[:, 0:512]))
        p.o("ve", lambda E: E.bn_stats(st6[:, 6:12], scr(0)[:, 512:1024]), inc="VE")
        p.w("ve", "VE")
        p.o("ve", lambda E: E.bn_aggr(mv2[:, 0:2], st6[:, 0:12]), inc="VE")
        p.w("sc", "VE")
        p.o("sc", lambda E: E.activation(mv2[:, 2:3], mv2[:, 1:2], AF.Ln, bias=ccol[:, 0:1]), inc="AC")
        p.w("sc", "AC")
        p.o("sc", lambda E: E.activation(mv2[:, 2:3], mv2[:, 2:3], AF.Exp, scale=ccol[:, 1:2]), inc="AC")
        p.w("ve", "AC")
        p.o("ve", lambda E: E.tensor_scalar(scr(1), scr(0), mv2[:, 0:1], mv2[:, 2:3],
                                            op0=ALU.subtract, op1=ALU.mult))
        p.o("ve", lambda E: E.tensor_mul(scr(1), scr(1), g1t))
        xmb = AR3[:, tt * D:(tt + 1) * D]
        p.o("ve", lambda E, d=xmb: E.tensor_tensor(d, scr(1), b1t, op=ALU.add), inc="VE")
        xmid_ve[tt] = p.c("VE")
        qeng, qsem = "sy", "SY"
        p.w(qeng, "VE", xmid_ve[tt])
        for ic in range(NIC):
            p.dmat(qeng, qsem, AR4[:, ic * T + tt * 128: ic * T + (tt + 1) * 128],
                   xmb[:, ic * 128:(ic + 1) * 128])
        zfm_tp[tt] = (qsem, p.c(qsem))
    zfm_sy, zfm_sc = p.c("SY"), p.c("SC")
    zfm = lambda ic: AR4[:, ic * T:(ic + 1) * T]
    xmid_done = p.c("VE")
    tap_tm("xmid", lambda tt: AR3[:, tt * D:(tt + 1) * D], "VE", xmid_done)
    if stage <= 3:
        finish(p, nc, ctx)
        return nc

    p.w("te", "SY", zfm_sy)
    p.w("te", "SC", zfm_sc)
    gat_evac = {}
    for tt in range(NT):
        bank = PS[tt % 2]
        if tt >= 2:
            p.w("te", "VE", gat_evac[tt % 2])
        p.o("te", lambda E, b=bank: E.matmul(b[:, 0:NE], ones_b[:, :], gateb_r[:, :],
                                             start=True, stop=False))
        for ic in range(NIC):
            p.o("te", lambda E, b=bank, ic=ic, tt=tt: E.matmul(
                b[:, 0:NE], zfm(ic)[:, tt * 128:(tt + 1) * 128], gwsb[:, ic * NE:(ic + 1) * NE],
                start=False, stop=(ic == NIC - 1)),
                inc="PE" if ic == NIC - 1 else None)
        mmv = p.c("PE")
        p.w("ve", "PE", mmv)
        pr = scr(0)[:, 0:NE]
        m1 = scr(0)[:, 8:16]
        v1 = scr(0)[:, 16:17]
        v2 = scr(0)[:, 17:18]
        def ev(fn):
            p.o("ve", fn, inc="VE")
            p.w("ve", "VE")
        ev(lambda E, b=bank, pr=pr: E.tensor_copy(pr, b[:, 0:NE]))
        ev(lambda E, pr=pr, v1=v1: E.tensor_reduce(v1, pr, axis=AX.X, op=ALU.max))
        p.o("ve", lambda E, pr=pr, v1=v1: E.tensor_single_scalar(pr, pr, v1[:, 0:1],
                                                                 op=ALU.subtract), inc="VE")
        p.w("sc", "VE")
        p.o("sc", lambda E, pr=pr: E.activation(pr, pr, AF.Exp), inc="AC")
        p.w("ve", "AC")
        ev(lambda E, pr=pr, v1=v1: E.tensor_reduce(v1, pr, axis=AX.X, op=ALU.add))
        ev(lambda E, v1=v1: E.reciprocal(v1, v1))
        ev(lambda E, pr=pr, v1=v1: E.tensor_scalar_mul(pr, pr, v1[:, 0:1]))
        ev(lambda E, pr=pr, v1=v1: E.tensor_reduce(v1, pr, axis=AX.X, op=ALU.max))
        ev(lambda E, pr=pr, m1=m1, v1=v1: E.tensor_single_scalar(m1, pr, v1[:, 0:1],
                                                                 op=ALU.is_ge))
        ev(lambda E, pr=pr, m1=m1: E.scalar_tensor_tensor(m1, m1, LNEG, pr,
                                                          op0=ALU.mult, op1=ALU.add))
        ev(lambda E, m1=m1, v2=v2: E.tensor_reduce(v2, m1, axis=AX.X, op=ALU.max))
        ev(lambda E, pr=pr, m1=m1, v2=v2: E.tensor_single_scalar(m1, pr, v2[:, 0:1],
                                                                 op=ALU.is_ge))
        ev(lambda E, pr=pr, m1=m1: E.tensor_mul(pr, pr, m1))
        ev(lambda E, pr=pr, v1=v1: E.tensor_reduce(v1, pr, axis=AX.X, op=ALU.add))
        ev(lambda E, v1=v1: E.reciprocal(v1, v1))
        p.o("ve", lambda E, pr=pr, v1=v1, tt=tt: E.tensor_scalar_mul(
            comb[:, tt * NE:(tt + 1) * NE], pr, v1[:, 0:1]), inc="VE")
        gat_evac[tt % 2] = p.c("VE")
    comb_done = p.c("VE")
    tap_tm("comb", lambda tt: comb[:, tt * NE:(tt + 1) * NE], "VE", comb_done, width=NE)
    if stage == 35:
        finish(p, nc, ctx)
        return nc

    w1buf = lambda par: AR7[:, M7_W1 + (par % 4) * 1024: M7_W1 + (par % 4 + 1) * 1024]
    w2buf = lambda par: AR7[:, M7_W2 + (par % 4) * 1024: M7_W2 + (par % 4 + 1) * 1024]
    acc_par = lambda tb: f32_AR3[:, 8192 + (tb % 2) * 4096: 8192 + (tb % 2 + 1) * 4096]
    h1buf = AR2
    w1sems = ("W10", "W11", "W12", "W13")
    w2sems = ("W20", "W21", "W22", "W23")
    out_stage = lambda par: scr(par)
    st_cnt = 0
    acc_consumed = {}
    y_evac = 0

    def load_w1(e, fc, pe_gate):
        par = fc % 4
        p.w("sy", "PE", pe_gate)
        p.w("sy", "MC", moe_cast[e])
        p.dma("sy", w1sems[par], w1buf(par)[:, :],
              w1_c.ap()[e, :, :].rearrange("(fc p) c -> fc p c", p=128)[fc, :, :])
        return p.c(w1sems[par])

    def load_w2(e, fc, pe_gate):
        par = fc % 4
        p.w("sy", "PE", pe_gate)
        p.w("sy", "MC", moe_cast[e])
        p.dma("sy", w2sems[par], w2buf(par)[:, :],
              w2_c.ap()[e, fc * 128:(fc + 1) * 128, :])
        return p.c(w2sems[par])

    w1_ld = {i: load_w1(0, i, 0) for i in range(4)}
    w2_ld = {i: load_w2(0, i, 0) for i in range(4)}
    if stage == 41:
        for s_ in ("W10", "W11", "W12", "W13", "W20", "W21", "W22", "W23", "MC"):
            p.w("sy", s_, p.c(s_))
        finish(p, nc, ctx)
        return nc
    h1_evac = {}
    first_iter = True
    epi_pending = None
    for tb in range(4):
        acc_f32 = acc_par(tb)
        p.w("ve", "VE", comb_done)
        if tb > 1:
            p.w("ve", "VE", acc_consumed[tb % 2])
        for tc in range(4):
            tt = tb * 4 + tc
            p.o("ve", lambda E, tt=tt, tc=tc, acc_f32=acc_f32: E.tensor_copy(
                acc_f32[:, tc * 1024:(tc + 1) * 1024], AR3[:, tt * D:(tt + 1) * D]), inc="VE")
        acc_init = p.c("VE")
        for e in range(NE):
            if first_iter:
                p.w("te", "SY", zfm_sy)
                p.w("te", "SC", zfm_sc)
                p.w("te", "VE", comb_done)
            else:
                p.w("te", "VE", y_evac01)
            for fc in range(NFC):
                par = fc % 4
                bpar = fc % 2
                p.w("te", w1sems[par], w1_ld[fc])
                if fc >= 2:
                    s_, v_ = h1_evac[bpar]
                    p.w("te", s_, v_)
                bank = PS[bpar]
                for ic in range(NIC):
                    p.o("te", lambda E, b=bank, ic=ic, tb=tb, par=par: E.matmul(
                        b[:, :], w1buf(par)[:, ic * 128:(ic + 1) * 128],
                        zfm(ic)[:, tb * 512:(tb + 1) * 512],
                        start=(ic == 0), stop=(ic == NIC - 1)),
                        inc="PE" if ic == NIC - 1 else None)
                w1_pe = p.c("PE")
                if fc + 4 < NFC:
                    w1_ld[fc + 4] = load_w1(e, fc + 4, w1_pe)
                else:
                    ne_, nt_ = (e + 1, tb) if e + 1 < NE else (0, tb + 1)
                    if nt_ < 4:
                        w1_ld[fc + 4 - NFC] = load_w1(ne_, fc + 4 - NFC, w1_pe)
                eng3, sem3 = ("sc", "AC") if fc % 2 == 0 else ("ve", "VE")
                p.w(eng3, "PE", w1_pe)
                if stage == 39:
                    if eng3 == "sc":
                        p.o(eng3, lambda E, b=bank, fc=fc: E.copy(
                            h1buf[:, fc * 512:(fc + 1) * 512], b[:, :]), inc=sem3)
                    else:
                        p.o(eng3, lambda E, b=bank, fc=fc: E.tensor_copy(
                            h1buf[:, fc * 512:(fc + 1) * 512], b[:, :]), inc=sem3)
                elif eng3 == "sc":
                    p.o(eng3, lambda E, b=bank, e=e, fc=fc: E.activation(
                        h1buf[:, fc * 512:(fc + 1) * 512], b[:, :], AF.Relu,
                        bias=b1c[:, e * NFC + fc: e * NFC + fc + 1]), inc=sem3)
                else:
                    p.o(eng3, lambda E, b=bank, e=e, fc=fc: E.tensor_scalar(
                        h1buf[:, fc * 512:(fc + 1) * 512], b[:, :],
                        b1c[:, e * NFC + fc: e * NFC + fc + 1], 0.0,
                        op0=ALU.add, op1=ALU.max), inc=sem3)
                h1_evac[bpar] = (sem3, p.c(sem3))
            h1_ve, h1_ac = p.c("VE"), p.c("AC")
            if stage in (38, 39) and e == 0:
                first_iter = False
                break
            p.w("sy", "PE", p.c("PE"))
            p.dma("sy", "WQ", brow[:, 0:1024], bias_c.ap()[:, 3072 + D + e * D: 3072 + D + (e + 1) * D])
            b2_ld = p.c("WQ")
            p.w("te", "WQ", b2_ld)
            p.w("te", "VE", h1_ve)
            p.w("te", "AC", h1_ac)
            if not first_iter:
                p.w("te", "VE", y_evac)
            if tb > 0 or e > 0:
                p.w("te", "VE", acc_init)
            for tc in range(4):
                for ob in range(2):
                    bank = PS[tc * 2 + ob]
                    p.o("te", lambda E, b=bank, ob=ob: E.matmul(
                        b[:, :], ones_b[:, :], brow[:, ob * 512:(ob + 1) * 512],
                        start=True, stop=False))
            for fc in range(NFC):
                par = fc % 4
                p.w("te", w2sems[par], w2_ld[fc])
                for tc in range(4):
                    for ob in range(2):
                        bank = PS[tc * 2 + ob]
                        p.o("te", lambda E, b=bank, fc=fc, tc=tc, ob=ob: E.matmul(
                            b[:, :], h1buf[:, fc * 512 + tc * 128: fc * 512 + tc * 128 + 128],
                            w2buf(fc % 4)[:, ob * 512:(ob + 1) * 512],
                            start=False, stop=(fc == NFC - 1)),
                            inc="PE" if (tc == 3 and ob == 1) else None)
                w2_pe = p.c("PE")
                if fc + 4 < NFC:
                    w2_ld[fc + 4] = load_w2(e, fc + 4, w2_pe)
                else:
                    ne_, nt_ = (e + 1, tb) if e + 1 < NE else (0, tb + 1)
                    if nt_ < 4:
                        w2_ld[fc + 4 - NFC] = load_w2(ne_, fc + 4 - NFC, w2_pe)
            y_done = p.c("PE")
            p.w("ve", "PE", y_done)
            p.w("ve", "VE", acc_init)
            for tc in range(4):
                tt = tb * 4 + tc
                for ob in range(2):
                    bank = PS[tc * 2 + ob]
                    dst = acc_f32[:, tc * 1024 + ob * 512: tc * 1024 + (ob + 1) * 512]
                    p.o("ve", lambda E, b=bank, d=dst, tt=tt, e=e: E.scalar_tensor_tensor(
                        d, b[:, :], comb[:, tt * NE + e: tt * NE + e + 1], d,
                        op0=ALU.mult, op1=ALU.add),
                        inc="VE" if (tc == 0 and ob == 1) or (tc == 3 and ob == 1) else None)
                if tc == 0:
                    y_evac01 = p.c("VE")
            y_evac = p.c("VE")
            first_iter = False
            if stage == 37 and e == 0:
                break
        if "accf" in tap_ap:
            for tc in range(4):
                tt = tb * 4 + tc
                p.w("ve", "VE", y_evac)
                p.w("ve", "ST", p.c("ST"))
                p.o("ve", lambda E, tc=tc: E.tensor_copy(scr(0), acc_f32[:, tc * 1024:(tc + 1) * 1024]), inc="VE")
                p.w("sy", "VE")
                p.dma("sy", "ST", tap_ap["accf"][tt * 128:(tt + 1) * 128, :], scr(0))
        if "h1f" in tap_ap and tb == 0:
            for fc in range(NFC):
                p.w("ve", "VE", y_evac)
                p.w("ve", "ST", p.c("ST"))
                p.o("ve", lambda E, fc=fc: E.tensor_copy(scr(0)[:, 0:512], h1buf[:, fc * 512:(fc + 1) * 512]), inc="VE")
                p.w("sy", "VE")
                p.dma("sy", "ST", tap_ap["h1f"][fc * 128:(fc + 1) * 128, :], scr(0)[:, 0:512])
        if stage in (36, 37, 38, 39) and tb == 0:
            break
        for tc in range(4):
            tt = tb * 4 + tc
            at = acc_f32[:, tc * 1024:(tc + 1) * 1024]
            p.w("ve", "VE", y_evac)
            p.o("ve", lambda E, at=at: E.bn_stats(st6[:, 0:6], at[:, 0:512]))
            p.o("ve", lambda E, at=at: E.bn_stats(st6[:, 6:12], at[:, 512:1024]), inc="VE")
            p.w("ve", "VE")
            p.o("ve", lambda E: E.bn_aggr(mv2[:, 0:2], st6[:, 0:12]), inc="VE")
            p.w("sc", "VE")
            p.o("sc", lambda E: E.activation(mv2[:, 2:3], mv2[:, 1:2], AF.Ln, bias=ccol[:, 0:1]), inc="AC")
            p.w("sc", "AC")
            p.o("sc", lambda E: E.activation(mv2[:, 2:3], mv2[:, 2:3], AF.Exp, scale=ccol[:, 1:2]),
                inc="AC")
            p.w("ve", "AC")
            if st_cnt >= 2:
                p.w("ve", "ST", 16 * (st_cnt - 1))
            osd = out_stage(tc % 2)
            p.o("ve", lambda E, at=at, o=osd: E.tensor_scalar(o, at, mv2[:, 0:1], mv2[:, 2:3],
                                                              op0=ALU.subtract, op1=ALU.mult))
            p.o("ve", lambda E, o=osd: E.tensor_mul(o, o, g2t))
            p.o("ve", lambda E, o=osd: E.tensor_tensor(o, o, b2t, op=ALU.add))
            p.o("ve", lambda E, o=osd: E.bn_stats(st6[:, 0:6], o[:, 0:512]))
            p.o("ve", lambda E, o=osd: E.bn_stats(st6[:, 6:12], o[:, 512:1024]), inc="VE")
            p.w("ve", "VE")
            p.o("ve", lambda E: E.bn_aggr(mv2[:, 0:2], st6[:, 0:12]), inc="VE")
            p.w("sc", "VE")
            p.o("sc", lambda E: E.activation(mv2[:, 2:3], mv2[:, 1:2], AF.Ln, bias=ccol[:, 0:1]), inc="AC")
            p.w("sc", "AC")
            p.o("sc", lambda E: E.activation(mv2[:, 2:3], mv2[:, 2:3], AF.Exp, scale=ccol[:, 1:2]),
                inc="AC")
            p.w("ve", "AC")
            p.o("ve", lambda E, o=osd: E.tensor_scalar(o, o, mv2[:, 0:1], mv2[:, 2:3],
                                                       op0=ALU.subtract, op1=ALU.mult))
            xt = AR1[:, tt * D:(tt + 1) * D]
            p.o("ve", lambda E, xt=xt, o=osd: E.scalar_tensor_tensor(
                o, xt, 1.0, o, op0=ALU.mult, op1=ALU.add), inc="VE")
            odone = p.c("VE")
            p.w("sy", "VE", odone)
            p.dma("sy", "ST", y_out[tt * 128:(tt + 1) * 128, :], osd)
            st_cnt += 1
        acc_consumed[tb % 2] = p.c("VE")
        if stage == 36:
            break

    finish(p, nc, ctx)
    return nc


def finish(p, nc, ctx):
    p.w("sy", "ST", p.c("ST"))
    with nc.Block() as block:
        @block.sync
        def _(E):
            for fn in p.q["sy"]:
                fn(E)

        @block.scalar
        def _(E):
            for fn in p.q["sc"]:
                fn(E)

        @block.vector
        def _(E):
            for fn in p.q["ve"]:
                fn(E)

        @block.gpsimd
        def _(E):
            for fn in p.q["gp"]:
                fn(E)

        @block.tensor
        def _(E):
            for fn in p.q["te"]:
                fn(E)
    ctx.close()


def host_prep(inputs):
    f = np.float32
    x = np.ascontiguousarray(inputs["L0_attn_hidden_states"], dtype=f)
    w_qkv = np.asarray(inputs["L0_attn_w_qkv"], dtype=f)
    wq, wk, wv = w_qkv[0:1024], w_qkv[1024:2048], w_qkv[2048:4096]
    groups = []
    for h in range(NH):
        groups += [wq[h * 128:(h + 1) * 128].T, wk[h * 128:(h + 1) * 128].T,
                   wv[h * 256:(h + 1) * 256].T]
    wqkv_g = np.ascontiguousarray(np.concatenate(groups, axis=1))
    wgat_t = np.ascontiguousarray(np.asarray(inputs["L0_attn_w_gated"], dtype=f).T)
    wprj_t = np.ascontiguousarray(np.asarray(inputs["L0_attn_w_proj"], dtype=f).T)
    decay_r = np.asarray(inputs["L0_attn_decay"], dtype=f).reshape(1, NH)
    Dm, SCALE_BASE = 1024, 512.0
    half = Dm // 2
    sv = (np.arange(0, Dm, 2, dtype=np.float64) + 0.4 * Dm) / (1.4 * Dm)
    pos = np.arange(T, dtype=np.float64)
    scale = sv[None, :] ** (pos[:, None] / SCALE_BASE)
    inv_freq = 1.0 / 10000.0 ** (np.arange(half, dtype=np.float64) / half)
    sinus = pos[:, None] * inv_freq[None, :]
    sin, cos = np.sin(sinus), np.cos(sinus)
    Cq = (cos * scale).astype(f)
    Sq = (sin * scale).astype(f)
    Ck = (cos / scale).astype(f)
    Sk = (sin / scale).astype(f)
    tab = np.empty((NH, NT, 128, 256), dtype=f)
    for h in range(NH):
        cols = slice(h * 64, (h + 1) * 64)
        for tt in range(NT):
            rows = slice(tt * 128, (tt + 1) * 128)
            tab[h, tt, :, 0:64] = Cq[rows, cols]
            tab[h, tt, :, 64:128] = Sq[rows, cols]
            tab[h, tt, :, 128:192] = Ck[rows, cols]
            tab[h, tt, :, 192:256] = Sk[rows, cols]
    umask = np.triu(np.ones((128, 128), dtype=f)).copy()
    ipw = np.ascontiguousarray(np.asarray(inputs["L0_ffn_in_proj_w"], dtype=f).T)
    ipb = np.asarray(inputs["L0_ffn_in_proj_b"], dtype=f).reshape(1, 3072)
    outw = np.ascontiguousarray(np.asarray(inputs["L0_ffn_out_w"], dtype=f).T)
    outb = np.asarray(inputs["L0_ffn_out_b"], dtype=f).reshape(1, D)
    gatew = np.ascontiguousarray(np.asarray(inputs["L0_ffn_gate_w"], dtype=f).T)
    gateb = np.asarray(inputs["L0_ffn_gate_b"], dtype=f).reshape(1, NE)
    w1 = np.asarray(inputs["L0_ffn_w1"], dtype=f)
    w1pre = np.ascontiguousarray(
        w1.reshape(NE, NFC, 128, NIC, 128).transpose(0, 1, 4, 3, 2).reshape(NE, NFC * 128, NIC * 128))
    b1 = np.asarray(inputs["L0_ffn_b1"], dtype=f)
    b1c = np.ascontiguousarray(b1.reshape(NE, NFC, 128).transpose(2, 0, 1).reshape(128, NE * NFC))
    w2 = np.asarray(inputs["L0_ffn_w2"], dtype=f)
    w2pre = np.ascontiguousarray(w2.transpose(0, 2, 1))
    b2r = np.asarray(inputs["L0_ffn_b2"], dtype=f).reshape(1, NE * D)
    lng = np.empty((128, 4 * D), dtype=f)
    lng[:, 0:D] = np.asarray(inputs["L0_ffn_ln1_g"], dtype=f)[None, :]
    lng[:, D:2 * D] = np.asarray(inputs["L0_ffn_ln1_b"], dtype=f)[None, :]
    lng[:, 2 * D:3 * D] = np.asarray(inputs["L0_ffn_ln2_g"], dtype=f)[None, :]
    lng[:, 3 * D:4 * D] = np.asarray(inputs["L0_ffn_ln2_b"], dtype=f)[None, :]
    base = {
        "wqkv_in": wqkv_g, "wgat_in": wgat_t, "wprj_in": wprj_t, "decay_r": decay_r,
        "tab_in": tab, "umask_in": umask,
        "ipw_in": ipw, "outw_in": outw,
        "gatew_in": gatew, "gateb_in": gateb,
        "bias_in": np.ascontiguousarray(np.concatenate([ipb, outb, b2r], axis=1)),
        "w1pre_in": w1pre, "b1c_in": b1c, "w2pre_in": w2pre,
        "lng_in": np.ascontiguousarray(lng),
    }
    in_maps = []
    for c in range(NCORES):
        m = dict(base)
        m["x_in"] = np.ascontiguousarray(x[c])
        j = np.arange(T)
        gidx_v = ((j // 256) * T + c * 256 + (j % 256)).astype(np.int32)
        m["gidx_in"] = np.ascontiguousarray(gidx_v.reshape(NT, 128).T)
        in_maps.append(m)
    return in_maps


def kernel(**inputs):
    in_maps = host_prep(inputs)
    nc = build_nc(stage=99)
    res = run_bass_kernel_spmd(nc, in_maps, core_ids=list(range(NCORES)))
    out = np.empty((NCORES, T, D), dtype=np.float32)
    for c in range(NCORES):
        y = np.asarray(res.results[c]["y_out"])
        out[:, c * 256:(c + 1) * 256, :] = y.reshape(NCORES, 256, D)
    return out
